# version 03
# baseline (speedup 1.0000x reference)
import numpy as np
import ml_dtypes
import concourse.bass as bass
import concourse.mybir as mybir
from concourse.bass_utils import run_bass_kernel_spmd

F32 = mybir.dt.float32
BF16 = mybir.dt.bfloat16
AF = mybir.ActivationFunctionType
OP = mybir.AluOpType
BF = ml_dtypes.bfloat16

D = 1024
T = 2048
NKT = D // 128

SUBLAYER_GROUPS = [[0], [1], [2], [3], [4], [5]]
DEBUG_TAPS = False
L1_PHASES = 99
L1_NCHUNK = 128
L1_NHP = 8
L1_COPS = 9


ENGS = ("ten", "act", "dve", "pool", "sp")
DMA_Q = ("pool", "sp")


class Op:
    __slots__ = ("eng", "fn", "reads", "writes", "idx", "needs_inc", "seq",
                 "waits", "tag", "_deps")

    def __init__(self, eng, fn, reads, writes, tag=""):
        self.eng = eng
        self.fn = fn
        self.reads = tuple(reads)
        self.writes = tuple(writes)
        self.needs_inc = False
        self.seq = 0
        self.waits = []
        self.tag = tag
        self._deps = []


class Prog:
    def __init__(self):
        self.items = []
        self.per_eng = {e: [] for e in ENGS}

    def op(self, eng, fn, reads=(), writes=(), tag=""):
        o = Op(eng, fn, reads, writes, tag or "op")
        o.tag = f"{o.tag}|{len(self.per_eng[eng])}"
        self.per_eng[eng].append(o)
        self.items.append(o)
        return o

    def dma(self, q, out, in_, reads=(), writes=(), tag=""):
        assert q in DMA_Q
        return self.op(q, lambda e, out=out, in_=in_: e.dma_start(out=out, in_=in_),
                       reads=reads, writes=writes, tag=tag)

    def barrier(self):
        self.items.append("BARRIER")

    def schedule(self):
        last_w = {}
        readers = {}
        eng_last = {e: None for e in ENGS}
        pending_barrier = {}
        for it in self.items:
            if it == "BARRIER":
                prods = [op for op in eng_last.values() if op is not None]
                pending_barrier = {e: [pp for pp in prods if pp.eng != e] for e in ENGS}
                last_w = {}
                readers = {}
                continue
            o = it
            deps = list(pending_barrier.pop(o.eng, ()))
            for b in o.reads:
                w = last_w.get(b)
                if w is not None and w.eng != o.eng:
                    deps.append(w)
            for b in o.writes:
                w = last_w.get(b)
                if w is not None and w.eng != o.eng:
                    deps.append(w)
                for reng, rop in readers.get(b, {}).items():
                    if reng != o.eng:
                        deps.append(rop)
            for p in deps:
                p.needs_inc = True
            o._deps = deps
            for b in o.reads:
                readers.setdefault(b, {})[o.eng] = o
            for b in o.writes:
                last_w[b] = o
                readers[b] = {}
            eng_last[o.eng] = o
        cnt = {e: 0 for e in ENGS}
        for it in self.items:
            if it == "BARRIER":
                continue
            if it.eng in DMA_Q:
                it.needs_inc = True
            if it.needs_inc:
                cnt[it.eng] += 1
            it.seq = cnt[it.eng]
        waited = {}
        for it in self.items:
            if it == "BARRIER":
                continue
            need = {}
            for p in it._deps:
                need[p.eng] = max(need.get(p.eng, 0), p.seq)
            for peng in sorted(need):
                seq = need[peng]
                key = (it.eng, peng)
                if waited.get(key, 0) < seq:
                    it.waits.append((peng, seq))
                    waited[key] = seq

    def emit(self, nc):
        self.schedule()
        sems = {}
        ctxs = []
        for e in ENGS:
            cm = nc.semaphore("sem_" + e)
            ctxs.append(cm)
            sems[e] = cm.__enter__()
        for q in DMA_Q:
            cm = nc.semaphore("raw_" + q)
            ctxs.append(cm)
            sems["raw_" + q] = cm.__enter__()
        blk_cm = nc.Block()
        block = blk_cm.__enter__()

        def run_engine(eng_obj, eng_name):
            is_q = eng_name in DMA_Q
            for o in self.per_eng[eng_name]:
                for peng, seq in o.waits:
                    eng_obj.wait_ge(sems[peng], seq)
                instr = o.fn(eng_obj)
                instr.annotate(o.tag)
                if o.needs_inc:
                    if is_q:
                        instr.then_inc(sems["raw_" + eng_name], 16)
                        eng_obj.wait_ge(sems["raw_" + eng_name], o.seq * 16)
                        eng_obj.sem_inc(sems[eng_name], 1)
                    else:
                        instr.then_inc(sems[o.eng], 1)

        block.tensor(lambda e: run_engine(e, "ten"))
        block.scalar(lambda e: run_engine(e, "act"))
        block.vector(lambda e: run_engine(e, "dve"))
        block.gpsimd(lambda e: run_engine(e, "pool"))

        def run_sync(e):
            run_engine(e, "sp")
            for en in ("pool", "ten", "act", "dve"):
                tot = sum(1 for o in self.per_eng[en] if o.needs_inc)
                if tot:
                    e.wait_ge(sems[en], tot)

        block.sync(run_sync)
        blk_cm.__exit__(None, None, None)
        for cm in reversed(ctxs):
            cm.__exit__(None, None, None)


def _pack_cols(wT, I, O):
    a = np.ascontiguousarray(np.asarray(wT, dtype=np.float32))
    assert a.shape == (I, O)
    return np.ascontiguousarray(
        a.reshape(I // 128, 128, O // 128, 128).transpose(2, 1, 0, 3))


def _cols(a):
    a = np.asarray(a, dtype=np.float32)
    return np.ascontiguousarray(a.reshape(-1, 128).T)


def prep_inputs(kw):
    g = {}
    g["w_qkv_p"] = _pack_cols(np.asarray(kw["L0_attn_w_qkv"]).T, D, 3 * D)
    g["b_qkv"] = np.asarray(kw["L0_attn_b_qkv"], np.float32).reshape(1, -1)
    g["w_out_p"] = _pack_cols(np.asarray(kw["L0_attn_w_out"]).T, D, D)
    g["b_out"] = np.asarray(kw["L0_attn_b_out"], np.float32).reshape(1, -1)
    om = np.asarray(kw["L0_attn_omega"], np.float32)
    g["omegaT"] = np.ascontiguousarray(np.transpose(om, (1, 2, 0, 3)).reshape(16, 64, 64))
    g["qnodes2"] = np.asarray(kw["L0_attn_quad_nodes"], np.float32).reshape(2, 1)
    g["qweights2"] = np.asarray(kw["L0_attn_quad_weights"], np.float32).reshape(2, 1)
    g["w1_p"] = _pack_cols(np.asarray(kw["L0_ffn_w1_w"]).T, D, 4096)
    g["w2_p"] = _pack_cols(np.asarray(kw["L0_ffn_w2_w"]).T, D, 4096)
    g["h1_p"] = _pack_cols(np.asarray(kw["L0_ffn_h1_w"]).T, 4096, 4096)
    g["h2_p"] = _pack_cols(np.asarray(kw["L0_ffn_h2_w"]).T, 4096, 4096)
    g["w3_p"] = _pack_cols(np.asarray(kw["L0_ffn_w3_w"]).T, 4096, D)
    for s in ("w1_s", "w2_s", "h1_s", "h2_s", "w3_s"):
        g[s] = _cols(kw["L0_ffn_" + s])
    g["key_w_p"] = _pack_cols(np.asarray(kw["L1_attn_key_w"]).T, D, D)
    g["key_b"] = np.asarray(kw["L1_attn_key_b"], np.float32).reshape(1, -1)
    g["vq_w_p"] = _pack_cols(np.asarray(kw["L1_attn_vq_w"]).T, D, 2 * D)
    g["out_w_p"] = _pack_cols(np.asarray(kw["L1_attn_out_w"]).T, D, D)
    g["out_b"] = np.asarray(kw["L1_attn_out_b"], np.float32).reshape(1, -1)
    G1R = np.asarray(kw["L1_ffn_G1_R"], np.float32)[0]
    G2R = np.asarray(kw["L1_ffn_G2_R"], np.float32)[:, :, 0, :]
    G1I = np.asarray(kw["L1_ffn_G1_I"], np.float32)[0]
    G2I = np.asarray(kw["L1_ffn_G2_I"], np.float32)[:, :, 0, :]

    def expand(G1, G2):
        A = np.broadcast_to(G1.transpose(1, 2, 0)[:, :, None, :, None], (4, 32, 16, 32, 16))
        A = np.ascontiguousarray(A.reshape(4, 512, 512))
        B = np.broadcast_to(G2.transpose(0, 2, 1)[:, None, :, None, :], (4, 32, 16, 32, 16))
        B = np.ascontiguousarray(B.reshape(4, 512, 512))
        return A, B
    g["G1R_x"], g["G2R_x"] = expand(G1R, G2R)
    g["G1I_x"], g["G2I_x"] = expand(G1I, G2I)
    g["bias_ri"] = np.concatenate([np.asarray(kw["L1_ffn_bias_real"]),
                                   np.asarray(kw["L1_ffn_bias_imag"])]).astype(np.float32).reshape(1, -1)

    g["trimask"] = np.triu(np.ones((128, 128), np.float32)).astype(BF)
    g["identity"] = np.eye(128, dtype=np.float32).astype(BF)
    g["ones_col_f32"] = np.ones((128, 1), np.float32)
    g["ones_col_bf"] = np.ones((128, 1), np.float32).astype(BF)
    g["ones_row_bf"] = np.ones((1, 512), np.float32).astype(BF)
    g["ones_row_f32"] = np.ones((1, 1024), np.float32)
    g["eps_col"] = np.full((1, 1), 1e-5, np.float32)
    tri16 = np.triu(np.ones((16, 16), np.float32)).astype(np.uint8)
    g["seltri16"] = np.concatenate([tri16, tri16], axis=1)
    g["zeros1632"] = np.zeros((16, 32), np.float32)
    g["seltri32"] = np.concatenate([tri16, tri16], axis=0)
    g["zeros3216"] = np.zeros((32, 16), np.float32)
    bo2 = np.zeros((128, 2), np.float32)
    bo2[:64, 0] = 1.0
    bo2[64:, 1] = 1.0
    g["blockones2"] = bo2.astype(BF)
    g["eps30"] = np.full((128, 1), 1e-30, np.float32)
    ind2 = np.zeros((2, 128), np.float32)
    ind2[0, :64] = 1.0
    ind2[1, 64:] = 1.0
    g["ind2"] = ind2
    sel2 = np.zeros((2, 128), np.float32)
    rr = (np.arange(128) // 32) % 2
    sel2[0, rr == 0] = 1.0
    sel2[1, rr == 1] = 1.0
    g["sel2"] = sel2
    return g


DEV_INPUTS = [
    ("xT", [D, T], F32),
    ("w_qkv_p", [24, 128, 8, 128], F32), ("b_qkv", [1, 3 * D], F32),
    ("w_out_p", [8, 128, 8, 128], F32), ("b_out", [1, D], F32),
    ("omegaT", [16, 64, 64], F32), ("qnodes2", [2, 1], F32), ("qweights2", [2, 1], F32),
    ("w1_p", [32, 128, 8, 128], F32), ("w2_p", [32, 128, 8, 128], F32),
    ("h1_p", [32, 128, 32, 128], F32), ("h2_p", [32, 128, 32, 128], F32),
    ("w3_p", [8, 128, 32, 128], F32),
    ("w1_s", [128, 32], F32), ("w2_s", [128, 32], F32),
    ("h1_s", [128, 32], F32), ("h2_s", [128, 32], F32), ("w3_s", [128, 8], F32),
    ("key_w_p", [8, 128, 8, 128], F32), ("key_b", [1, D], F32),
    ("vq_w_p", [16, 128, 8, 128], F32),
    ("out_w_p", [8, 128, 8, 128], F32), ("out_b", [1, D], F32),
    ("G1R_x", [4, 512, 512], F32), ("G2R_x", [4, 512, 512], F32),
    ("G1I_x", [4, 512, 512], F32), ("G2I_x", [4, 512, 512], F32),
    ("bias_ri", [1, D], F32),
    ("trimask", [128, 128], BF16), ("identity", [128, 128], BF16),
    ("ones_col_f32", [128, 1], F32), ("ones_col_bf", [128, 1], BF16),
    ("ones_row_bf", [1, 512], BF16), ("ones_row_f32", [1, 1024], F32),
    ("eps_col", [1, 1], F32),
    ("ind2", [2, 128], F32), ("sel2", [2, 128], F32),
    ("blockones2", [128, 2], BF16), ("eps30", [128, 1], F32),
    ("seltri16", [16, 32], mybir.dt.uint8), ("zeros1632", [16, 32], F32),
    ("seltri32", [32, 16], mybir.dt.uint8), ("zeros3216", [32, 16], F32),
]
CONST_NAMES = ("trimask", "identity", "ones_col_f32", "ones_col_bf", "ones_row_bf",
               "ones_row_f32", "eps_col", "ind2", "sel2", "blockones2", "eps30",
               "seltri16", "zeros1632", "seltri32", "zeros3216")


class Ctx:
    pass


def build_program(sublayers):
    nc = bass.Bass("TRN2", target_bir_lowering=False, detect_race_conditions=False)
    c = Ctx()
    c.nc = nc
    c.d = {}
    for name, shape, dt in DEV_INPUTS:
        c.d[name] = nc.dram_tensor(name, shape, dt, kind="ExternalInput").ap()
    c.d["out"] = nc.dram_tensor("out", [D, T], F32, kind="ExternalOutput").ap()
    c.d["xs0"] = nc.dram_tensor("xs0", [D, T], F32, kind="Internal").ap()
    c.d["xs1"] = nc.dram_tensor("xs1", [D, T], F32, kind="Internal").ap()

    p = Prog()
    c.p = p
    sb_cms = []
    c.sb = {}
    for nm in CONST_NAMES:
        shape = dict((n, s) for n, s, _ in DEV_INPUTS)[nm]
        dt = dict((n, dd) for n, _, dd in DEV_INPUTS)[nm]
        cm = nc.sbuf_tensor("c_" + nm, shape, dt)
        sb_cms.append(cm)
        c.sb[nm] = cm.__enter__()
        p.dma("sp", c.sb[nm][:, :], c.d[nm], writes=("c_" + nm,), tag="const")

    cur = "xT"
    scratch = ["xs0", "xs1"]
    for i, sl in enumerate(sublayers):
        nxt = "out" if i == len(sublayers) - 1 else scratch[i % 2]
        EMITTERS[sl](c, c.d[cur], c.d[nxt])
        p.barrier()
        cur = nxt

    p.emit(nc)
    return nc


def emit_l0_ffn(c, x_in, x_out):
    nc, p = c.nc, c.p
    TC = 1024
    NCH = T // TC
    sb_cms = []

    def alloc(name, shape, dt):
        cm = nc.sbuf_tensor(name, shape, dt)
        sb_cms.append(cm)
        return cm.__enter__()

    xrot = [alloc(f"xrot{i}", [128, 2 * TC], F32) for i in range(2)]
    ln = [alloc(f"lnt{k}", [128, TC], BF16) for k in range(NKT)]
    hbuf = alloc("hbuf", [128, 32 * TC], BF16)
    gbuf = alloc("gbuf", [128, 32 * TC], BF16)
    wcol = alloc("wcol", [128, 2 * 32 * 128], BF16)
    wsml = alloc("wsml", [128, 2 * 2048], BF16)
    scol = alloc("scol", [128, 128], F32)
    s3col = alloc("s3col", [128, 8], F32)
    wsrow = alloc("wsrow", [1, 2 * 2 * 128], BF16)
    rowA = alloc("rowA", [1, TC], F32)
    rowB = alloc("rowB", [1, TC], F32)
    rowC = alloc("rowC", [1, TC], F32)
    rowD = alloc("rowD", [1, TC], BF16)
    ps = []
    for i in range(4):
        cm = nc.psum_tensor(f"ps{i}", [128, TC], F32)
        sb_cms.append(cm)
        ps.append(cm.__enter__())

    xbf = [hbuf[:, k * TC:(k + 1) * TC] for k in range(NKT)]
    x2v = [hbuf[:, (8 + k) * TC:(9 + k) * TC] for k in range(NKT)]
    silu_v = [gbuf[:, (i * 2) * TC:(i * 2 + 2) * TC].bitcast(F32) for i in range(2)]

    p.dma("sp", scol[:, 0:32], c.d["w1_s"], writes=("scol",))
    p.dma("sp", scol[:, 32:64], c.d["w2_s"], writes=("scol",))
    p.dma("sp", scol[:, 64:96], c.d["h1_s"], writes=("scol",))
    p.dma("sp", scol[:, 96:128], c.d["h2_s"], writes=("scol",))
    p.dma("sp", s3col[:, :], c.d["w3_s"], writes=("s3col",))
    p.op("dve", lambda e: e.tensor_scalar_mul(s3col[:, :], s3col[:, :], 0.5),
         reads=("s3col",), writes=("s3col",))

    xin_r = x_in.rearrange("(k p) t -> p k t", p=128)
    xout_r = x_out.rearrange("(k p) t -> p k t", p=128)

    for ch in range(NCH):
        tsl = slice(ch * TC, (ch + 1) * TC)

        for kk in range(4):
            xr = xrot[kk % 2]
            xk = f"xrot{kk % 2}"
            p.dma("sp", xr.rearrange("p (k t) -> p k t", k=2),
                  xin_r[:, 2 * kk:2 * kk + 2, tsl], writes=(xk,), tag="xload")
            for j in range(2):
                k = 2 * kk + j
                p.op("dve", lambda e, xr=xr, j=j, k=k: e.tensor_copy(
                    xbf[k], xr[:, j * TC:(j + 1) * TC]),
                    reads=(xk,), writes=(f"h{k}",))
                p.op("dve", lambda e, k=k: e.tensor_mul(x2v[k], xbf[k], xbf[k]),
                    reads=(f"h{k}",), writes=(f"h{8 + k}",))
        for sc in range(TC // 512):
            sl = slice(sc * 512, (sc + 1) * 512)
            for k in range(NKT):
                p.op("ten", lambda e, k=k, sl=sl: e.matmul(
                    ps[0][0:1, sl], c.sb["ones_col_bf"][:, :], xbf[k][:, sl],
                    start=(k == 0), stop=(k == NKT - 1)),
                    reads=(f"h{k}", "c_ones_col_bf"), writes=("ps0",))
                p.op("ten", lambda e, k=k, sl=sl: e.matmul(
                    ps[1][0:1, sl], c.sb["ones_col_bf"][:, :], x2v[k][:, sl],
                    start=(k == 0), stop=(k == NKT - 1)),
                    reads=(f"h{8 + k}", "c_ones_col_bf"), writes=("ps1",))
            p.op("dve", lambda e, sl=sl: e.tensor_copy(rowA[0:1, sl], ps[0][0:1, sl]),
                 reads=("ps0",), writes=("rowA",))
            p.op("dve", lambda e, sl=sl: e.tensor_copy(rowB[0:1, sl], ps[1][0:1, sl]),
                 reads=("ps1",), writes=("rowB",))
        p.op("dve", lambda e: e.tensor_scalar_mul(rowA[0:1, :], rowA[0:1, :], 1.0 / D),
             reads=("rowA",), writes=("rowA",))
        p.op("dve", lambda e: e.scalar_tensor_tensor(rowC[0:1, :], rowA[0:1, :], -1.0,
                                                     rowA[0:1, :], OP.mult, OP.mult),
             reads=("rowA",), writes=("rowC",))
        p.op("dve", lambda e: e.scalar_tensor_tensor(rowB[0:1, :], rowB[0:1, :], 1.0 / D,
                                                     rowC[0:1, :], OP.mult, OP.add),
             reads=("rowB", "rowC"), writes=("rowB",))
        p.op("act", lambda e: e.activation(rowC[0:1, :], rowB[0:1, :], AF.Ln,
                                           bias=c.sb["eps_col"][0:1, 0:1], scale=1.0),
             reads=("rowB", "c_eps_col"), writes=("rowC",))
        p.op("act", lambda e: e.activation(rowC[0:1, :], rowC[0:1, :], AF.Exp, scale=-0.5),
             reads=("rowC",), writes=("rowC",))
        p.op("dve", lambda e: e.scalar_tensor_tensor(rowA[0:1, :], rowA[0:1, :], -1.0,
                                                     rowC[0:1, :], OP.mult, OP.mult),
             reads=("rowA", "rowC"), writes=("rowA",))
        p.op("dve", lambda e: e.tensor_copy(rowD[0:1, :], rowA[0:1, :]),
             reads=("rowA",), writes=("rowD",))

        for sc in range(TC // 512):
            sl = slice(sc * 512, (sc + 1) * 512)
            p.op("ten", lambda e, sl=sl: e.matmul(
                ps[0][:, sl], c.sb["ones_row_f32"][0:1, 0:128], rowC[0:1, sl],
                start=True, stop=True),
                reads=("rowC", "c_ones_row_f32"), writes=("ps0",))
        for kk in range(4):
            xr = xrot[kk % 2]
            xk = f"xrot{kk % 2}"
            p.dma("sp", xr.rearrange("p (k t) -> p k t", k=2),
                  xin_r[:, 2 * kk:2 * kk + 2, tsl], writes=(xk,), tag="xload2")
            for j in range(2):
                k = 2 * kk + j
                p.op("dve", lambda e, xr=xr, j=j, k=k: e.tensor_mul(
                    ln[k][:, :], xr[:, j * TC:(j + 1) * TC], ps[0][:, :]),
                    reads=(xk, "ps0"), writes=(f"ln{k}",))

        for o in range(32):
            ob = o % 2
            w1dst = wsml[:, ob * 2048:ob * 2048 + 1024]
            w2dst = wsml[:, ob * 2048 + 1024:ob * 2048 + 2048]
            k1, k2 = f"wsml{ob}a", f"wsml{ob}b"
            p.dma("pool", w1dst.rearrange("p (k o) -> p k o", k=8),
                  c.d["w1_p"][o], writes=(k1,), tag="w1col")
            p.dma("pool", w2dst.rearrange("p (k o) -> p k o", k=8),
                  c.d["w2_p"][o], writes=(k2,), tag="w2col")
            pst1, kt1 = ps[2 + ob], f"ps{2 + ob}"
            pst2, kt2 = ps[0 + ob], f"ps{0 + ob}"
            for wd, wk, roff in ((w1dst, k1, ob * 256), (w2dst, k2, ob * 256 + 128)):
                for k in range(NKT):
                    p.op("ten", lambda e, k=k, wd=wd: e.matmul(
                        ps[1][0:1, 0:128], c.sb["ones_col_bf"][:, :], wd[:, k * 128:(k + 1) * 128],
                        start=(k == 0), stop=(k == NKT - 1)),
                        reads=(wk, "c_ones_col_bf"), writes=("ps1",))
                p.op("dve", lambda e, roff=roff: e.tensor_copy(
                    wsrow[0:1, roff:roff + 128], ps[1][0:1, 0:128]),
                    reads=("ps1",), writes=(f"wsrow{roff}",))
            for half in range(2):
                hsl = slice(half * 512, (half + 1) * 512)
                for k in range(NKT):
                    p.op("ten", lambda e, k=k, hsl=hsl, w1dst=w1dst, pst1=pst1: e.matmul(
                        pst1[:, hsl], w1dst[:, k * 128:(k + 1) * 128], ln[k][:, hsl],
                        start=(k == 0), stop=False),
                        reads=(k1, f"ln{k}"), writes=(kt1,))
                p.op("ten", lambda e, hsl=hsl, o=o, pst1=pst1, ob=ob: e.matmul(
                    pst1[:, hsl], wsrow[0:1, ob * 256:ob * 256 + 128], rowD[0:1, hsl],
                    start=False, stop=True),
                    reads=(f"wsrow{ob * 256}", "rowD"), writes=(kt1,))
                for k in range(NKT):
                    p.op("ten", lambda e, k=k, hsl=hsl, w2dst=w2dst, pst2=pst2: e.matmul(
                        pst2[:, hsl], w2dst[:, k * 128:(k + 1) * 128], ln[k][:, hsl],
                        start=(k == 0), stop=False),
                        reads=(k2, f"ln{k}"), writes=(kt2,))
                p.op("ten", lambda e, hsl=hsl, o=o, pst2=pst2, ob=ob: e.matmul(
                    pst2[:, hsl], wsrow[0:1, ob * 256 + 128:ob * 256 + 256], rowD[0:1, hsl],
                    start=False, stop=True),
                    reads=(f"wsrow{ob * 256 + 128}", "rowD"), writes=(kt2,))
            p.op("act", lambda e, o=o, pst1=pst1: e.activation(
                silu_v[o % 2], pst1[:, :], AF.Silu, scale=scol[:, o:o + 1]),
                reads=(kt1, "scol"), writes=(f"g{(o % 2) * 2}", f"g{(o % 2) * 2 + 1}"))
            p.op("dve", lambda e, o=o, pst2=pst2: e.scalar_tensor_tensor(
                hbuf[:, o * TC:(o + 1) * TC], pst2[:, :], scol[:, 32 + o:32 + o + 1],
                silu_v[o % 2], OP.mult, OP.mult),
                reads=(kt2, f"g{(o % 2) * 2}", f"g{(o % 2) * 2 + 1}", "scol"), writes=(f"h{o}",))

        for o in range(32):
            ob = o % 2
            wdst = wcol[:, ob * 4096:(ob + 1) * 4096]
            wk = f"wcol{ob}"
            p.dma("pool", wdst.rearrange("p (k o) -> p k o", k=32),
                  c.d["h1_p"][o], writes=(wk,), tag="h1col")
            pst, kt = ps[2 + ob], f"ps{2 + ob}"
            for half in range(2):
                hsl = slice(half * 512, (half + 1) * 512)
                for k in range(32):
                    p.op("ten", lambda e, k=k, hsl=hsl, wdst=wdst, pst=pst: e.matmul(
                        pst[:, hsl], wdst[:, k * 128:(k + 1) * 128],
                        hbuf[:, k * TC + hsl.start:k * TC + hsl.stop],
                        start=(k == 0), stop=(k == 31)),
                        reads=(wk, f"h{k}"), writes=(kt,))
            p.op("act", lambda e, o=o, pst=pst: e.activation(
                gbuf[:, o * TC:(o + 1) * TC], pst[:, :], AF.Gelu,
                scale=scol[:, 64 + o:64 + o + 1]),
                reads=(kt, "scol"), writes=(f"g{o}",))

        for o in range(32):
            ob = o % 2
            wdst = wcol[:, ob * 4096:(ob + 1) * 4096]
            wk = f"wcol{ob}"
            p.dma("pool", wdst.rearrange("p (k o) -> p k o", k=32),
                  c.d["h2_p"][o], writes=(wk,), tag="h2col")
            pst, kt = ps[2 + ob], f"ps{2 + ob}"
            for half in range(2):
                hsl = slice(half * 512, (half + 1) * 512)
                for k in range(32):
                    p.op("ten", lambda e, k=k, hsl=hsl, wdst=wdst, pst=pst: e.matmul(
                        pst[:, hsl], wdst[:, k * 128:(k + 1) * 128],
                        gbuf[:, k * TC + hsl.start:k * TC + hsl.stop],
                        start=(k == 0), stop=(k == 31)),
                        reads=(wk, f"g{k}"), writes=(kt,))
            p.op("act", lambda e, o=o, pst=pst: e.activation(
                hbuf[:, o * TC:(o + 1) * TC], pst[:, :], AF.Gelu,
                scale=scol[:, 96 + o:96 + o + 1]),
                reads=(kt, "scol"), writes=(f"h{o}",))

        for op2 in range(4):
            xr = xrot[op2 % 2]
            xk = f"xrot{op2 % 2}"
            p.dma("sp", xr.rearrange("p (k t) -> p k t", k=2),
                  xin_r[:, 2 * op2:2 * op2 + 2, tsl], writes=(xk,), tag="xres")
            for j in range(2):
                od = 2 * op2 + j
                ob = od % 2
                wdst = wcol[:, ob * 4096:(ob + 1) * 4096]
                wk = f"wcol{ob}"
                p.dma("pool", wdst.rearrange("p (k o) -> p k o", k=32),
                      c.d["w3_p"][od], writes=(wk,), tag="w3col")
                pst, kt = ps[2 + ob], f"ps{2 + ob}"
                for half in range(2):
                    hsl = slice(half * 512, (half + 1) * 512)
                    for k in range(32):
                        p.op("ten", lambda e, k=k, hsl=hsl, wdst=wdst, pst=pst: e.matmul(
                            pst[:, hsl], wdst[:, k * 128:(k + 1) * 128],
                            hbuf[:, k * TC + hsl.start:k * TC + hsl.stop],
                            start=(k == 0), stop=(k == 31)),
                            reads=(wk, f"h{k}"), writes=(kt,))
                p.op("dve", lambda e, od=od, j=j, xr=xr, pst=pst: e.scalar_tensor_tensor(
                    xr[:, j * TC:(j + 1) * TC], pst[:, :], s3col[:, od:od + 1],
                    xr[:, j * TC:(j + 1) * TC], OP.mult, OP.add),
                    reads=(kt, "s3col", xk), writes=(xk,))
            p.dma("sp", xout_r[:, 2 * op2:2 * op2 + 2, tsl],
                  xr.rearrange("p (k t) -> p k t", k=2), reads=(xk,), tag="xstore")

    for cm in reversed(sb_cms):
        cm.__exit__(None, None, None)



def emit_l0_attn(c, x_in, x_out):
    nc, p = c.nc, c.p
    sb_cms = []

    def alloc(name, shape, dt):
        cm = nc.sbuf_tensor(name, shape, dt)
        sb_cms.append(cm)
        return cm.__enter__()

    xrot = [alloc(f"xrot{i}", [128, 2 * T], F32) for i in range(2)]
    xbf2 = alloc("xbf2", [128, 2 * T], BF16)
    ln = [alloc(f"lnt{k}", [128, T], BF16) for k in range(NKT)]
    rows = {"A": alloc("rowA", [1, T], F32), "B": alloc("rowB", [1, T], F32),
            "C": alloc("rowC", [1, T], F32), "D": alloc("rowD", [1, T], BF16)}
    rbc = alloc("rbc", [128, T], F32)
    apre = [alloc(f"apre{i}", [128, T], BF16) for i in range(8)]
    qzb = alloc("qzb", [128, T], BF16)
    kzb = alloc("kzb", [128, T], BF16)
    qf = alloc("qf", [128, T], BF16)
    kf = alloc("kf", [128, T], BF16)
    vfm = alloc("vfm", [128, T], BF16)
    z2b = vfm
    om_sb = alloc("om_sb", [128, 64], BF16)
    wqcol = alloc("wqcol", [128, 2 * 1024], BF16)
    s2row = alloc("s2row", [2, 512], F32)
    rnbc = alloc("rnbc", [128, 1024], F32)
    fw1 = None
    fw2 = None
    tbf = alloc("tbf", [128, 1024], BF16)
    brow = alloc("brow", [1, 3 * D], F32)
    borow = alloc("borow", [1, D], F32)
    qcol2 = alloc("qcol2", [2, 4], F32)
    qcols = alloc("qcols", [128, 4], F32)
    kftok = alloc("kftok", [128, 128], BF16)
    wsq = alloc("wsq", [1, 256], BF16)
    vtok = alloc("vtok", [128, 16 * 128], BF16)
    smb = [alloc(f"smb{h}", [128, 128], BF16) for h in range(2)]
    KVsb = alloc("KVsb", [128, 64], BF16)
    kssb = alloc("kssb", [128, 1], BF16)
    KVf = alloc("KVf", [128, 64], F32)
    ksf = alloc("ksf", [128, 1], F32)
    ctxsb = alloc("ctxsb", [128, 512], F32)
    denr = alloc("denr", [1, 512], F32)
    wocol = alloc("wocol", [128, 1024], BF16)
    ps_cms = []

    def palloc(name, shape, dt=F32):
        cm = nc.psum_tensor(name, shape, dt)
        ps_cms.append(cm)
        sb_cms.append(cm)
        return cm.__enter__()

    fw1 = rbc[:, 0:1024]
    fw2 = rbc[:, 1024:2048]
    xin_r = x_in.rearrange("(k p) t -> p k t", p=128)
    xout_r = x_out.rearrange("(k p) t -> p k t", p=128)

    p.dma("sp", brow[0:1, :], c.d["b_qkv"], writes=("brow",))
    p.dma("sp", borow[0:1, :], c.d["b_out"], writes=("borow",))
    p.dma("sp", qcol2[:, 0:1], c.d["qnodes2"], writes=("qcol2",))
    p.dma("sp", qcol2[:, 1:2], c.d["qweights2"], writes=("qcol2",))

    ln_stream(c, p, xin_r, slice(0, T), T, xrot, ln, rows, rbc, xbf2)

    psq = palloc("psq", [128, 1024])
    psk = palloc("psk", [128, 1024])
    psp = palloc("psp", [128, 1024])
    psv = palloc("psv", [128, 512])
    psw = palloc("psw", [128, 512])

    p.op("dve", lambda e: e.tensor_scalar(qcol2[0:2, 2:3], qcol2[0:2, 0:1], 0.0, 2.0,
                                          OP.max, OP.mult), reads=("qcol2",), writes=("qcol2",))
    p.op("act", lambda e: e.activation(qcol2[0:2, 2:3], qcol2[0:2, 2:3], AF.Ln,
                                       bias=c.sb["eps30"][0:2, 0:1]),
         reads=("qcol2", "c_eps30"), writes=("qcol2",))
    p.op("act", lambda e: e.activation(qcol2[0:2, 2:3], qcol2[0:2, 2:3], AF.Exp, scale=0.5),
         reads=("qcol2",), writes=("qcol2",))
    p.op("dve", lambda e: e.tensor_scalar_mul(qcol2[0:2, 3:4], qcol2[0:2, 0:1], -1.0),
         reads=("qcol2",), writes=("qcol2",))
    p.op("dve", lambda e: e.tensor_scalar_max(qcol2[0:2, 1:2], qcol2[0:2, 1:2], 0.0),
         reads=("qcol2",), writes=("qcol2",))
    p.op("act", lambda e: e.activation(qcol2[0:2, 1:2], qcol2[0:2, 1:2], AF.Ln,
                                       bias=c.sb["eps30"][0:2, 0:1]),
         reads=("qcol2", "c_eps30"), writes=("qcol2",))
    p.op("dve", lambda e: e.tensor_scalar(qcol2[0:2, 1:2], qcol2[0:2, 1:2], 0.25,
                                          -1.7328679513998633, OP.mult, OP.add),
         reads=("qcol2",), writes=("qcol2",))
    for i, src in ((0, 2), (1, 1), (2, 3)):
        p.op("ten", lambda e, i=i, src=src: e.matmul(
            psw[:, i:i + 1], c.sb["sel2"][:, :], qcol2[0:2, src:src + 1],
            start=True, stop=True),
            reads=("qcol2", "c_sel2"), writes=("psw",))
        p.op("dve", lambda e, i=i: e.tensor_copy(qcols[:, i:i + 1], psw[:, i:i + 1]),
             reads=("psw",), writes=("qcols",))

    for hp in range(8):
        hb = hp % 2
        p.dma("pool", om_sb[0:64, :], c.d["omegaT"][2 * hp], writes=("om_sb",))
        p.dma("pool", om_sb[64:128, :], c.d["omegaT"][2 * hp + 1], writes=("om_sb",))
        for w_i, (wname, pst, kt, zb, zk, boff) in enumerate(
                ((hp, psq, "psq", qzb, "qzb", 0),
                 (8 + hp, psk, "psk", kzb, "kzb", 1024),
                 (16 + hp, psv, "psv", vfm, "vfm", 2048))):
            wdst = wqcol[:, w_i % 2 * 1024:(w_i % 2 + 1) * 1024]
            wk = f"wqcol{w_i % 2}"
            p.dma("pool", wdst.rearrange("p (k o) -> p k o", k=8),
                  c.d["w_qkv_p"][wname], writes=(wk,), tag="wqkv")
            wb = w_i % 2
            for k in range(NKT):
                p.op("ten", lambda e, k=k, wdst=wdst: e.matmul(
                    psw[0:1, 0:128], c.sb["ones_col_bf"][:, :], wdst[:, k * 128:(k + 1) * 128],
                    start=(k == 0), stop=(k == NKT - 1)),
                    reads=(wk, "c_ones_col_bf"), writes=("psw",))
            p.op("dve", lambda e, wb=wb: e.tensor_copy(wsq[0:1, wb * 128:(wb + 1) * 128],
                                                       psw[0:1, 0:128]),
                 reads=("psw",), writes=(f"wsq{wb}",))
            if w_i < 2:
                for sc in range(4):
                    sl = slice(sc * 512, (sc + 1) * 512)
                    psl = slice((sc % 2) * 512, (sc % 2 + 1) * 512)
                    for k in range(NKT):
                        p.op("ten", lambda e, k=k, sl=sl, psl=psl, wdst=wdst, pst=pst: e.matmul(
                            pst[:, psl], wdst[:, k * 128:(k + 1) * 128], ln[k][:, sl],
                            start=(k == 0), stop=False),
                            reads=(wk, f"ln{k}"), writes=(kt,))
                    p.op("ten", lambda e, sl=sl, psl=psl, pst=pst, wb=wb: e.matmul(
                        pst[:, psl], wsq[0:1, wb * 128:(wb + 1) * 128], rows["D"][0:1, sl],
                        start=False, stop=False),
                        reads=(f"wsq{wb}", "rowD"), writes=(kt,))
                    p.op("ten", lambda e, sl=sl, psl=psl, pst=pst, boff=boff, hp=hp: e.matmul(
                        pst[:, psl], brow[0:1, boff + hp * 128:boff + (hp + 1) * 128],
                        c.sb["ones_row_f32"][0:1, 0:512], start=False, stop=True),
                        reads=("brow", "c_ones_row_f32"), writes=(kt,))
                    if sc % 2 == 1:
                        ch2 = sc // 2
                        p.op("act", lambda e, ch2=ch2, pst=pst, zb=zb: e.activation(
                            zb[:, ch2 * 1024:(ch2 + 1) * 1024], pst[:, :], AF.Identity),
                            reads=(kt,), writes=(zk,))
            else:
                for sc in range(4):
                    sl = slice(sc * 512, (sc + 1) * 512)
                    for k in range(NKT):
                        p.op("ten", lambda e, k=k, sl=sl, wdst=wdst, pst=pst: e.matmul(
                            pst[:, 0:512], wdst[:, k * 128:(k + 1) * 128], ln[k][:, sl],
                            start=(k == 0), stop=False),
                            reads=(wk, f"ln{k}"), writes=(kt,))
                    p.op("ten", lambda e, sl=sl, pst=pst, wb=wb: e.matmul(
                        pst[:, 0:512], wsq[0:1, wb * 128:(wb + 1) * 128], rows["D"][0:1, sl],
                        start=False, stop=False),
                        reads=(f"wsq{wb}", "rowD"), writes=(kt,))
                    p.op("ten", lambda e, pst=pst, hp=hp: e.matmul(
                        pst[:, 0:512], brow[0:1, 2048 + hp * 128:2048 + (hp + 1) * 128],
                        c.sb["ones_row_f32"][0:1, 0:512], start=False, stop=True),
                        reads=("brow", "c_ones_row_f32"), writes=(kt,))
                    p.op("act", lambda e, sc=sc, pst=pst: e.activation(
                        vfm[:, sc * 512:(sc + 1) * 512], pst[:, :], AF.Identity),
                        reads=(kt,), writes=("vfm",))
        for tt in range(16):
            p.op("ten", lambda e, tt=tt: e.transpose(
                psp[:, 0:64].bitcast(BF16), vfm[:, tt * 128:(tt + 1) * 128],
                c.sb["identity"][:, :]),
                reads=("vfm", "c_identity"), writes=("psp",))
            p.op("dve", lambda e, tt=tt: e.tensor_copy(
                vtok[:, tt * 128:(tt + 1) * 128], psp[:, 0:64].bitcast(BF16)),
                reads=("psp",), writes=(f"vtok{tt}",))

        if DEBUG_TAPS and hp == 0:
            for ki in range(2):
                dd = nc.dram_tensor(f"dbg_ln{ki}", [128, T], BF16, kind="ExternalOutput").ap()
                p.dma("sp", dd, ln[ki][:, :], reads=(f"ln{ki}",), tag="dbg")
            ddr = nc.dram_tensor("dbg_rows", [1, T], F32, kind="ExternalOutput").ap()
            p.dma("sp", ddr, rows["C"][0:1, :], reads=("rowC",), tag="dbg")
            ddr2 = nc.dram_tensor("dbg_nmr", [1, T], F32, kind="ExternalOutput").ap()
            p.dma("sp", ddr2, rows["A"][0:1, :], reads=("rowA",), tag="dbg")
            for nm, buf in (("qzb", qzb), ("kzb", kzb), ("vfm", vfm)):
                dd = nc.dram_tensor("dbg_" + nm, [128, T], BF16, kind="ExternalOutput").ap()
                p.dma("sp", dd, buf[:, :], reads=(nm if nm != "vfm" else "vfm",), tag="dbg")
        if DEBUG_TAPS and hp == 0:
            ddo = nc.dram_tensor("dbg_om", [128, 64], BF16, kind="ExternalOutput").ap()
            p.dma("sp", ddo, om_sb[:, :], reads=("om_sb",), tag="dbg")
        for qk, (zb, zk, fdst, fk) in enumerate(((qzb, "qzb", qf, "qf"),
                                                 (kzb, "kzb", kf, "kf"))):
            p.op("dve", lambda e, zb=zb: e.tensor_mul(z2b[:, :], zb[:, :], zb[:, :]),
                 reads=(zk,), writes=("vfm",))
            for ch2 in range(2):
                csl = slice(ch2 * 1024, (ch2 + 1) * 1024)
                for sc in range(2):
                    ssl = slice(ch2 * 1024 + sc * 512, ch2 * 1024 + (sc + 1) * 512)
                    p.op("ten", lambda e, ssl=ssl: e.matmul(
                        psw[0:2, 0:512], c.sb["blockones2"][:, :], z2b[:, ssl],
                        start=True, stop=True),
                        reads=("vfm", "c_blockones2"), writes=("psw",))
                    p.op("dve", lambda e: e.tensor_copy(
                        s2row[0:2, 0:512], psw[0:2, 0:512]),
                        reads=("psw",), writes=("s2row",))
                    p.op("ten", lambda e: e.matmul(
                        psw[:, 0:512], c.sb["ind2"][:, :], s2row[0:2, 0:512],
                        start=True, stop=True),
                        reads=("s2row", "c_ind2"), writes=("psw",))
                    p.op("dve", lambda e, sc=sc: e.tensor_scalar_max(
                        rnbc[:, sc * 512:(sc + 1) * 512], psw[:, 0:512], 1e-24),
                        reads=("psw",), writes=("rnbc",))
                    p.op("act", lambda e, sc=sc: e.activation(
                        rnbc[:, sc * 512:(sc + 1) * 512], rnbc[:, sc * 512:(sc + 1) * 512],
                        AF.Ln), reads=("rnbc",), writes=("rnbc",))
                    p.op("act", lambda e, sc=sc: e.activation(
                        rnbc[:, sc * 512:(sc + 1) * 512], rnbc[:, sc * 512:(sc + 1) * 512],
                        AF.Exp, scale=-0.5), reads=("rnbc",), writes=("rnbc",))
                    for h in range(2):
                        p.op("ten", lambda e, h=h, ssl=ssl, sc=sc, zb=zb: e.matmul(
                            psp[h * 64:(h + 1) * 64, sc * 512:(sc + 1) * 512],
                            om_sb[h * 64:(h + 1) * 64, :], zb[h * 64:(h + 1) * 64, ssl],
                            start=True, stop=True),
                            reads=("om_sb", zk), writes=("psp",))
                if DEBUG_TAPS and hp == 0 and qk == 0 and ch2 == 0:
                    dd0 = nc.dram_tensor("dbg_praw", [128, 1024], F32, kind="ExternalOutput").ap()
                    p.op("dve", lambda e: e.tensor_copy(tbf[:, :].bitcast(F32)[:, 0:512], psp[:, 0:512]),
                         reads=("psp",), writes=("tbf",))
                    p.dma("sp", dd0[:, 0:512], tbf[:, :].bitcast(F32)[:, 0:512], reads=("tbf",), tag="dbg")
                    p.op("dve", lambda e: e.tensor_copy(tbf[:, :].bitcast(F32)[:, 0:512], psp[:, 512:1024]),
                         reads=("psp",), writes=("tbf",))
                    p.dma("sp", dd0[:, 512:1024], tbf[:, :].bitcast(F32)[:, 0:512], reads=("tbf",), tag="dbg")
                p.op("dve", lambda e: e.tensor_mul(fw1[:, :], psp[:, :], rnbc[:, :]),
                     reads=("psp", "rnbc"), writes=("rbc",))
                if DEBUG_TAPS and hp == 0 and qk == 0 and ch2 == 0:
                    dd1 = nc.dram_tensor("dbg_p", [128, 1024], F32, kind="ExternalOutput").ap()
                    p.dma("sp", dd1, fw1[:, :], reads=("rbc",), tag="dbg")
                    dd2 = nc.dram_tensor("dbg_rn", [128, 1024], F32, kind="ExternalOutput").ap()
                    p.dma("sp", dd2, rnbc[:, :], reads=("rnbc",), tag="dbg")
                p.op("dve", lambda e: e.tensor_scalar(fw2[:, :], fw1[:, :], qcols[:, 0:1],
                                                      qcols[:, 2:3], OP.mult, OP.add),
                     reads=("rbc", "qcols"), writes=("rbc",))
                p.op("dve", lambda e: e.tensor_scalar(fw2[:, :], fw2[:, :], 20.0, -20.0,
                                                      OP.min, OP.max),
                     reads=("rbc",), writes=("rbc",))
                p.op("act", lambda e: e.activation(fw2[:, :], fw2[:, :], AF.Exp,
                                                   bias=qcols[:, 1:2], scale=0.5),
                     reads=("rbc", "qcols"), writes=("rbc",))
                p.op("dve", lambda e: e.tensor_mul(tbf[:, :], fw1[:, :], fw2[:, :]),
                     reads=("rbc",), writes=("tbf",))
                p.op("act", lambda e, fdst=fdst, csl=csl: e.activation(
                    fdst[:, csl], tbf[:, :], AF.Square),
                    reads=("tbf",), writes=(fk,))

        if DEBUG_TAPS and hp == 0:
            for nm, buf in (("qf", qf), ("kf", kf)):
                dd = nc.dram_tensor("dbg_" + nm, [128, T], BF16, kind="ExternalOutput").ap()
                p.dma("sp", dd, buf[:, :], reads=(nm,), tag="dbg")
            dd = nc.dram_tensor("dbg_vtok", [128, 16 * 128], BF16, kind="ExternalOutput").ap()
            p.dma("sp", dd, vtok[:, :], reads=tuple(f"vtok{i}" for i in range(16)), tag="dbg")
        p.op("dve", lambda e: e.memset(KVf[:, :], 0.0), writes=("KVf",))
        p.op("dve", lambda e: e.memset(ksf[:, :], 0.0), writes=("ksf",))
        for g in range(4):
            for ccl in range(4):
                cc = g * 4 + ccl
                sl = slice(cc * 128, (cc + 1) * 128)
                p.op("ten", lambda e, sl=sl: e.transpose(
                    psk[:, 0:64].bitcast(BF16), kf[:, sl], c.sb["identity"][:, :]),
                    reads=("kf", "c_identity"), writes=("psk",))
                p.op("dve", lambda e: e.tensor_copy(kftok[:, :], psk[:, 0:64].bitcast(BF16)),
                     reads=("psk",), writes=("kftok",))
                for h in range(2):
                    hsl = slice(h * 64, (h + 1) * 64)
                    p.op("ten", lambda e, h=h, sl=sl, hsl=hsl: e.matmul(
                        psp[:, h * 512:h * 512 + 128], kf[hsl, sl], qf[hsl, sl],
                        start=True, stop=True),
                        reads=("kf", "qf"), writes=("psp",))
                    p.op("dve", lambda e, h=h: e.tensor_mul(
                        smb[h][:, :], psp[:, h * 512:h * 512 + 128], c.sb["trimask"][:, :]),
                        reads=("psp", "c_trimask"), writes=(f"smb{h}",))
                    dpart = slice(0, 1) if h == 0 else slice(32, 33)
                    p.op("ten", lambda e, h=h, ccl=ccl, dpart=dpart: e.matmul(
                        psw[dpart, ccl * 128:(ccl + 1) * 128], c.sb["ones_col_bf"][:, :],
                        smb[h][:, :], start=True, stop=(cc % 16 == 0)),
                        reads=(f"smb{h}", "c_ones_col_bf"), writes=("psw",))
                    if cc % 16 != 0:
                        p.op("ten", lambda e, h=h, sl=sl, ccl=ccl, hsl=hsl, dpart=dpart: e.matmul(
                            psw[dpart, ccl * 128:(ccl + 1) * 128], kssb[hsl, 0:1],
                            qf[hsl, sl], start=False, stop=True),
                            reads=("kssb", "qf"), writes=("psw",))
                    if cc % 16 != 0:
                        p.op("ten", lambda e, h=h, sl=sl, hsl=hsl: e.matmul(
                            psq[hsl, 0:128], KVsb[hsl, :], qf[hsl, sl],
                            start=True, stop=False),
                            reads=("KVsb", "qf"), writes=("psq",))
                    p.op("ten", lambda e, h=h, cc=cc, hsl=hsl: e.matmul(
                        psq[hsl, 0:128], vtok[:, cc % 16 * 128 + h * 64:cc % 16 * 128 + (h + 1) * 64],
                        smb[h][:, :], start=(cc % 16 == 0), stop=True),
                        reads=(f"vtok{cc % 16}", f"smb{h}"), writes=("psq",))
                    p.op("ten", lambda e, h=h, cc=cc, hsl=hsl: e.matmul(
                        psv[hsl, 0:64], kftok[:, h * 64:(h + 1) * 64],
                        vtok[:, cc % 16 * 128 + h * 64:cc % 16 * 128 + (h + 1) * 64],
                        start=True, stop=False),
                        reads=("kftok", f"vtok{cc % 16}"), writes=("psv",))
                    p.op("ten", lambda e, h=h, cc=cc, hsl=hsl: e.matmul(
                        psv[hsl, 64:65], kftok[:, h * 64:(h + 1) * 64],
                        c.sb["ones_col_bf"][:, :], start=False,
                        stop=True),
                        reads=("kftok", "c_ones_col_bf"), writes=("psv",))
                p.op("dve", lambda e: e.tensor_add(KVf[:, :], KVf[:, :], psv[:, 0:64]),
                     reads=("psv", "KVf"), writes=("KVf",))
                p.op("dve", lambda e: e.tensor_add(ksf[:, :], ksf[:, :], psv[:, 64:65]),
                     reads=("psv", "ksf"), writes=("ksf",))
                p.op("dve", lambda e: e.tensor_copy(KVsb[:, :], KVf[:, :]),
                     reads=("KVf",), writes=("KVsb",))
                p.op("dve", lambda e: e.tensor_copy(kssb[:, :], ksf[:, :]),
                     reads=("ksf",), writes=("kssb",))
                if DEBUG_TAPS and hp == 0 and cc == 0:
                    ddk = nc.dram_tensor("dbg_KVf", [128, 64], F32, kind="ExternalOutput").ap()
                    p.dma("sp", ddk, KVf[:, :], reads=("KVf",), tag="dbg")
                    ddk2 = nc.dram_tensor("dbg_ksf", [128, 1], F32, kind="ExternalOutput").ap()
                    p.dma("sp", ddk2, ksf[:, :], reads=("ksf",), tag="dbg")
                    ddk3 = nc.dram_tensor("dbg_kftok", [128, 128], BF16, kind="ExternalOutput").ap()
                    p.dma("sp", ddk3, kftok[:, :], reads=("kftok",), tag="dbg")
                p.op("dve", lambda e, ccl=ccl: e.tensor_copy(
                    ctxsb[:, ccl * 128:(ccl + 1) * 128], psq[:, 0:128]),
                    reads=("psq",), writes=("ctxsb",))
            for h in range(2):
                dpart = slice(0, 1) if h == 0 else slice(32, 33)
                p.op("dve", lambda e, h=h, dpart=dpart: e.tensor_scalar_max(
                    denr[0:1, 0:512], psw[dpart, 0:512], 1e-6),
                    reads=("psw",), writes=("denr",))
                p.op("dve", lambda e: e.reciprocal(
                    denr[0:1, 0:512], denr[0:1, 0:512]),
                    reads=("denr",), writes=("denr",))
                p.op("ten", lambda e, h=h: e.matmul(
                    psk[h * 64:(h + 1) * 64, 0:512], c.sb["ones_row_f32"][0:1, 0:64],
                    denr[0:1, 0:512], start=True, stop=True),
                    reads=("denr", "c_ones_row_f32"), writes=("psk",))
            p.op("dve", lambda e, g=g, hp=hp: e.tensor_mul(
                apre[hp][:, g * 512:(g + 1) * 512], ctxsb[:, :], psk[:, 0:512]),
                reads=("ctxsb", "psk"), writes=(f"apre{hp}",))

    if DEBUG_TAPS:
        for i in range(8):
            dd = nc.dram_tensor(f"dbg_apre{i}", [128, T], BF16, kind="ExternalOutput").ap()
            p.dma("sp", dd, apre[i][:, :], reads=(f"apre{i}",), tag="dbg")
    for odp in range(4):
        xr = xrot[odp % 2]
        xk = f"xrot{odp % 2}"
        p.dma("sp", xr.rearrange("p (k t) -> p k t", k=2),
              xin_r[:, 2 * odp:2 * odp + 2, 0:T], writes=(xk,), tag="xres")
        for j in range(2):
            od = 2 * odp + j
            wdst = wocol[:, 0:1024]
            wk = "wocol"
            p.dma("pool", wdst.rearrange("p (k o) -> p k o", k=8),
                  c.d["w_out_p"][od], writes=(wk,), tag="wout")
            for tch in range(4):
                sl = slice(tch * 512, (tch + 1) * 512)
                pst = psq if tch % 2 == 0 else psk
                kt = "psq" if tch % 2 == 0 else "psk"
                for k in range(NKT):
                    p.op("ten", lambda e, k=k, sl=sl, wdst=wdst, pst=pst: e.matmul(
                        pst[:, 0:512], wdst[:, k * 128:(k + 1) * 128], apre[k][:, sl],
                        start=(k == 0), stop=False),
                        reads=(wk, f"apre{k}"), writes=(kt,))
                p.op("ten", lambda e, od=od, pst=pst: e.matmul(
                    pst[:, 0:512], borow[0:1, od * 128:(od + 1) * 128],
                    c.sb["ones_row_f32"][0:1, 0:512], start=False, stop=True),
                    reads=("borow", "c_ones_row_f32"), writes=(kt,))
                p.op("dve", lambda e, j=j, sl=sl, xr=xr, pst=pst: e.scalar_tensor_tensor(
                    xr[:, j * T + sl.start:j * T + sl.stop], pst[:, 0:512], 1.0,
                    xr[:, j * T + sl.start:j * T + sl.stop], OP.mult, OP.add),
                    reads=(kt, xk), writes=(xk,))
        p.dma("sp", xout_r[:, 2 * odp:2 * odp + 2, 0:T],
              xr.rearrange("p (k t) -> p k t", k=2), reads=(xk,), tag="xstore")


def ln_stream(c, p, x_in_r, tsl, TC, xrot, ln, rows, rbc, xbf2):
    nc = c.nc
    psA_cm = nc.psum_tensor("lnpsA", [128, 2048], F32)
    psA = psA_cm.__enter__()
    psB_cm = nc.psum_tensor("lnpsB", [128, 2048], F32)
    psB = psB_cm.__enter__()
    pA, pB = "psA", "psB"
    for kk in range(4):
        xr = xrot[kk % 2]
        xk = f"xrot{kk % 2}"
        p.dma("sp", xr.rearrange("p (k t) -> p k t", k=2),
              x_in_r[:, 2 * kk:2 * kk + 2, tsl], writes=(xk,), tag="lnx")
        for j in range(2):
            k = 2 * kk + j
            xb = xbf2[:, j * TC:(j + 1) * TC]
            xbk = f"xbf{j}"
            p.op("dve", lambda e, xr=xr, j=j, xb=xb: e.tensor_copy(
                xb, xr[:, j * TC:(j + 1) * TC]), reads=(xk,), writes=(xbk,))
            nch = TC // 512
            for sc in range(nch):
                sl = slice(sc * 512, (sc + 1) * 512)
                bsl = sl
                p.op("ten", lambda e, xb=xb, sl=sl, bsl=bsl, k=k: e.matmul(
                    psA[0:1, bsl], c.sb["ones_col_bf"][:, :], xb[:, sl],
                    start=(k == 0), stop=(k == NKT - 1)),
                    reads=(xbk, "c_ones_col_bf"), writes=(pA,))
            x2 = xbf2[:, (1 - j) * TC:(2 - j) * TC]
            x2k = f"xbf{1 - j}"
            p.op("dve", lambda e, xb=xb, x2=x2: e.tensor_mul(x2, xb, xb),
                 reads=(xbk,), writes=(x2k,))
            for sc in range(nch):
                sl = slice(sc * 512, (sc + 1) * 512)
                bsl = sl
                p.op("ten", lambda e, x2=x2, sl=sl, bsl=bsl, k=k: e.matmul(
                    psB[0:1, bsl], c.sb["ones_col_bf"][:, :], x2[:, sl],
                    start=(k == 0), stop=(k == NKT - 1)),
                    reads=(x2k, "c_ones_col_bf"), writes=(pB,))
    rowA, rowB, rowC, rowD = rows["A"], rows["B"], rows["C"], rows["D"]
    for sc in range(TC // 512):
        sl = slice(sc * 512, (sc + 1) * 512)
        bsl = sl
        p.op("dve", lambda e, sl=sl, bsl=bsl: e.tensor_copy(rowA[0:1, sl], psA[0:1, bsl]),
             reads=(pA,), writes=("rowA",))
        p.op("dve", lambda e, sl=sl, bsl=bsl: e.tensor_copy(rowB[0:1, sl], psB[0:1, bsl]),
             reads=(pB,), writes=("rowB",))
    p.op("dve", lambda e: e.tensor_scalar_mul(rowA[0:1, :TC], rowA[0:1, :TC], 1.0 / D),
         reads=("rowA",), writes=("rowA",))
    p.op("dve", lambda e: e.scalar_tensor_tensor(rowC[0:1, :TC], rowA[0:1, :TC], -1.0,
                                                 rowA[0:1, :TC], OP.mult, OP.mult),
         reads=("rowA",), writes=("rowC",))
    p.op("dve", lambda e: e.scalar_tensor_tensor(rowB[0:1, :TC], rowB[0:1, :TC], 1.0 / D,
                                                 rowC[0:1, :TC], OP.mult, OP.add),
         reads=("rowB", "rowC"), writes=("rowB",))
    p.op("act", lambda e: e.activation(rowC[0:1, :TC], rowB[0:1, :TC], AF.Ln,
                                       bias=c.sb["eps_col"][0:1, 0:1], scale=1.0),
         reads=("rowB", "c_eps_col"), writes=("rowC",))
    p.op("act", lambda e: e.activation(rowC[0:1, :TC], rowC[0:1, :TC], AF.Exp, scale=-0.5),
         reads=("rowC",), writes=("rowC",))
    p.op("dve", lambda e: e.scalar_tensor_tensor(rowA[0:1, :TC], rowA[0:1, :TC], -1.0,
                                                 rowC[0:1, :TC], OP.mult, OP.mult),
         reads=("rowA", "rowC"), writes=("rowA",))
    p.op("dve", lambda e: e.tensor_copy(rowD[0:1, :TC], rowA[0:1, :TC]),
         reads=("rowA",), writes=("rowD",))
    for sc in range(TC // 512):
        sl = slice(sc * 512, (sc + 1) * 512)
        bsl = sl
        p.op("ten", lambda e, sl=sl, bsl=bsl: e.matmul(
            psA[:, sc * 512:(sc + 1) * 512][:, 0:512] if False else psA[:, bsl],
            c.sb["ones_row_f32"][0:1, 0:128], rowC[0:1, sl],
            start=True, stop=True),
            reads=("rowC", "c_ones_row_f32"), writes=(pA,))
        p.op("dve", lambda e, sl=sl, bsl=bsl: e.tensor_copy(rbc[:, sl], psA[:, bsl]),
             reads=(pA,), writes=("rbc",))
    for kk in range(4):
        xr = xrot[kk % 2]
        xk = f"xrot{kk % 2}"
        p.dma("sp", xr.rearrange("p (k t) -> p k t", k=2),
              x_in_r[:, 2 * kk:2 * kk + 2, tsl], writes=(xk,), tag="lnx2")
        for j in range(2):
            k = 2 * kk + j
            p.op("dve", lambda e, xr=xr, j=j, k=k: e.tensor_mul(
                ln[k][:, :TC], xr[:, j * TC:(j + 1) * TC], rbc[:, :TC]),
                reads=(xk, "rbc"), writes=(f"ln{k}",))
    psB_cm.__exit__(None, None, None)
    psA_cm.__exit__(None, None, None)


def emit_l1_ffn(c, x_in, x_out):
    nc, p = c.nc, c.p
    TC = T
    sb_cms = []

    def alloc(name, shape, dt):
        cm = nc.sbuf_tensor(name, shape, dt)
        sb_cms.append(cm)
        return cm.__enter__()

    xrot = [alloc(f"xrot{i}", [128, 2 * TC], F32) for i in range(2)]
    xbf2 = alloc("xbf2", [128, 2 * TC], BF16)
    ln = [alloc(f"lnt{k}", [128, TC], BF16) for k in range(NKT)]
    rows = {"A": alloc("rowA", [1, TC], F32), "B": alloc("rowB", [1, TC], F32),
            "C": alloc("rowC", [1, TC], F32), "D": alloc("rowD", [1, TC], BF16)}
    rbc = alloc("rbc", [128, TC], F32)
    gA = alloc("gA", [128, 512], F32)
    gB = alloc("gB", [128, 512], F32)
    gT = alloc("gT", [128, 512], F32)
    Wf = [alloc(f"Wf{rt}", [128, 4 * 512], F32) for rt in range(2)]
    WB = [alloc(f"WB{i}", [128, 4 * 512], BF16) for i in range(3)]
    wsx = alloc("wsx", [1, 1024], BF16)
    bias_sb = alloc("bias_sb", [1, 1024], F32)
    p.dma("sp", bias_sb[0:1, :], c.d["bias_ri"], writes=("bias_sb",))
    xin_r = x_in.rearrange("(k p) t -> p k t", p=128)
    xout_r = x_out.rearrange("(k p) t -> p k t", p=128)

    ln_stream(c, p, xin_r, slice(0, T), TC, xrot, ln, rows, rbc, xbf2)
    ps_cms = []
    pso = []
    for i in range(2):
        cm = nc.psum_tensor(f"pso{i}", [128, 512], F32)
        ps_cms.append(cm)
        pso.append(cm.__enter__())
    psB_cm2 = nc.psum_tensor("psB2", [128, 512], F32)
    ps_cms.append(psB_cm2)
    psB = psB_cm2.__enter__()
    sb_cms.extend(ps_cms)

    for rt in range(2):
        g1 = c.d["G1R_x"] if rt == 0 else c.d["G1I_x"]
        g2 = c.d["G2R_x"] if rt == 0 else c.d["G2I_x"]
        for I in range(4):
            acc = Wf[rt][:, I * 512:(I + 1) * 512]
            ak = f"Wf{rt}_{I}"
            for ci in range(4):
                p.dma("sp", gA[:, :], g1[ci, I * 128:(I + 1) * 128, :], writes=("gA",))
                p.dma("sp", gB[:, :], g2[ci, I * 128:(I + 1) * 128, :], writes=("gB",))
                if ci == 0:
                    p.op("dve", lambda e, acc=acc: e.tensor_mul(acc, gA[:, :], gB[:, :]),
                         reads=("gA", "gB"), writes=(ak,))
                else:
                    p.op("dve", lambda e: e.tensor_mul(gT[:, :], gA[:, :], gB[:, :]),
                         reads=("gA", "gB"), writes=("gT",))
                    p.op("dve", lambda e, acc=acc: e.tensor_add(acc, acc, gT[:, :]),
                         reads=("gT", ak), writes=(ak,))
            p.op("dve", lambda e, acc=acc, rt=rt, I=I: e.tensor_copy(
                WB[rt][:, I * 512:(I + 1) * 512], acc),
                reads=(ak,), writes=(f"WB{rt}_{I}",))
            if rt == 1:
                p.op("dve", lambda e, acc=acc, I=I: e.tensor_scalar_mul(
                    WB[2][:, I * 512:(I + 1) * 512], acc, -1.0),
                    reads=(ak,), writes=(f"WB2_{I}",))

    for half in range(2):
        Wa = WB[0] if half == 0 else WB[1]
        Wak = "WB0" if half == 0 else "WB1"
        Wb = WB[2] if half == 0 else WB[0]
        Wbk = "WB2" if half == 0 else "WB0"
        for O in range(4):
            for I in range(4):
                p.op("ten", lambda e, Wa=Wa, I=I, O=O: e.matmul(
                    psB[0:1, 0:128], c.sb["ones_col_bf"][:, :],
                    Wa[:, I * 512 + O * 128:I * 512 + (O + 1) * 128],
                    start=(I == 0), stop=False),
                    reads=(f"{Wak}_{I}", "c_ones_col_bf"), writes=("psB",))
            for I in range(4):
                p.op("ten", lambda e, Wb=Wb, I=I, O=O: e.matmul(
                    psB[0:1, 0:128], c.sb["ones_col_bf"][:, :],
                    Wb[:, I * 512 + O * 128:I * 512 + (O + 1) * 128],
                    start=False, stop=(I == 3)),
                    reads=(f"{Wbk}_{I}", "c_ones_col_bf"), writes=("psB",))
            p.op("dve", lambda e, half=half, O=O: e.tensor_copy(
                wsx[0:1, half * 512 + O * 128:half * 512 + (O + 1) * 128], psB[0:1, 0:128]),
                reads=("psB",), writes=("wsx",))

    for odp in range(4):
        xr = xrot[odp % 2]
        xk = f"xrot{odp % 2}"
        p.dma("sp", xr.rearrange("p (k t) -> p k t", k=2),
              xin_r[:, 2 * odp:2 * odp + 2, 0:T], writes=(xk,), tag="xres")
        for j in range(2):
            od = 2 * odp + j
            half = od // 4
            O = od % 4
            Wa = WB[0] if half == 0 else WB[1]
            Wak = "WB0" if half == 0 else "WB1"
            Wb = WB[2] if half == 0 else WB[0]
            Wbk = "WB2" if half == 0 else "WB0"
            for tch in range(4):
                sl = slice(tch * 512, (tch + 1) * 512)
                pst = pso[tch % 2]
                kt = f"pso{tch % 2}"
                for I in range(4):
                    p.op("ten", lambda e, Wa=Wa, I=I, O=O, sl=sl, pst=pst: e.matmul(
                        pst[:, 0:512], Wa[:, I * 512 + O * 128:I * 512 + (O + 1) * 128],
                        ln[I][:, sl], start=(I == 0), stop=False),
                        reads=(f"{Wak}_{I}", f"ln{I}"), writes=(kt,))
                for I in range(4):
                    p.op("ten", lambda e, Wb=Wb, I=I, O=O, sl=sl, pst=pst: e.matmul(
                        pst[:, 0:512], Wb[:, I * 512 + O * 128:I * 512 + (O + 1) * 128],
                        ln[4 + I][:, sl], start=False, stop=False),
                        reads=(f"{Wbk}_{I}", f"ln{4 + I}"), writes=(kt,))
                p.op("ten", lambda e, od=od, sl=sl, pst=pst: e.matmul(
                    pst[:, 0:512], wsx[0:1, od * 128:(od + 1) * 128], rows["D"][0:1, sl],
                    start=False, stop=False),
                    reads=("wsx", "rowD"), writes=(kt,))
                p.op("ten", lambda e, od=od, pst=pst: e.matmul(
                    pst[:, 0:512], bias_sb[0:1, od * 128:(od + 1) * 128],
                    c.sb["ones_row_f32"][0:1, 0:512], start=False, stop=True),
                    reads=("bias_sb", "c_ones_row_f32"), writes=(kt,))
                p.op("dve", lambda e, j=j, sl=sl, xr=xr, pst=pst: e.scalar_tensor_tensor(
                    xr[:, j * TC + sl.start:j * TC + sl.stop], pst[:, 0:512], 0.5,
                    xr[:, j * TC + sl.start:j * TC + sl.stop], OP.mult, OP.add),
                    reads=(kt, xk), writes=(xk,))
        p.dma("sp", xout_r[:, 2 * odp:2 * odp + 2, 0:T],
              xr.rearrange("p (k t) -> p k t", k=2), reads=(xk,), tag="xstore")


def emit_l1_attn(c, x_in, x_out):
    nc, p = c.nc, c.p
    sb_cms = []

    def alloc(name, shape, dt):
        cm = nc.sbuf_tensor(name, shape, dt)
        sb_cms.append(cm)
        return cm.__enter__()

    abf = alloc("abf", [128, T], BF16)
    ktl = alloc("ktl", [128, T], BF16)
    xrot = [alloc(f"xrot{i}", [128, 2 * T], F32) for i in range(2)]
    xbf2 = alloc("xbf2", [128, 2 * T], BF16)
    ln = [alloc(f"lnt{k}", [128, T], BF16) for k in range(NKT)]
    rows = {"A": alloc("rowA", [1, T], F32), "B": alloc("rowB", [1, T], F32),
            "C": alloc("rowC", [1, T], F32), "D": alloc("rowD", [1, T], BF16)}
    rbc = alloc("rbc", [128, T], F32)
    hpre = [alloc(f"hpre{i}", [128, T], BF16) for i in range(8)]
    ksb = alloc("ksb", [128, T], F32)
    usb = alloc("usb", [128, T], F32)
    qsb = alloc("qsb", [128, T], F32)
    Lsb = alloc("Lsb", [128, T + 16], F32)
    bbf = alloc("bbf", [128, T], BF16)
    vfm = alloc("vfm", [128, T], BF16)
    dec = alloc("dec", [128, 128], F32)
    bdK = alloc("bdK", [128, 32], BF16)
    bdT = alloc("bdT", [128, 64], BF16)
    vBDp = alloc("vBDp", [128, 128], BF16)
    bBDp = alloc("bBDp", [128, 128], BF16)
    smp = alloc("smp", [128, 16], BF16)
    KVbd = alloc("KVbd", [128, 128], F32)
    KVbds = alloc("KVbds", [128, 128], BF16)
    wqcol = alloc("wqcol", [128, 2 * 1024], BF16)
    wsq = alloc("wsq", [1, 256], BF16)
    wocol = alloc("wocol", [128, 1024], BF16)

    def palloc(name, shape, dt=F32):
        cm = nc.psum_tensor(name, shape, dt)
        sb_cms.append(cm)
        return cm.__enter__()

    xin_r = x_in.rearrange("(k p) t -> p k t", p=128)
    xout_r = x_out.rearrange("(k p) t -> p k t", p=128)

    fw1 = rbc
    af32 = xbf2[:, :].bitcast(F32)
    brow = rows["B"][0:1, 0:D]
    borow = rows["B"][0:1, D:2 * D]

    for i in range(8):
        p.op("dve", lambda e, i=i: e.memset(hpre[i][:, :], 0.0), writes=(f"hpre{i}",))
    p.op("dve", lambda e: e.memset(vfm[:, :], 0.0), writes=("vfm",))
    p.op("dve", lambda e: e.memset(abf[:, :], 0.0), writes=("abf",))
    p.op("dve", lambda e: e.memset(ktl[:, :], 0.0), writes=("ktl",))
    p.op("dve", lambda e: e.memset(bbf[:, :], 0.0), writes=("bbf",))

    ln_stream(c, p, xin_r, slice(0, T), T, xrot, ln, rows, rbc, xbf2)
    p.dma("sp", brow, c.d["key_b"], reads=("rowB",), writes=("brow",))
    p.dma("sp", borow, c.d["out_b"], reads=("rowB",), writes=("borow",))

    pA = palloc("pA", [128, 1024])
    pB = palloc("pB", [128, 1024])
    pC = palloc("pC", [128, 512])
    pD = palloc("pD", [128, 512])
    pE = palloc("pE", [128, 512])

    for hp in range(8):
        _phA = [] if L1_COPS in (2.95,) else list(enumerate(
                (("key_w_p", hp, pA, "pA", 0),
                 ("vq_w_p", 8 + hp, pB, "pB", None),
                 ("vq_w_p", hp, pC, "pC", None))))
        for w_i, (wtens, wname, pst, kt, boff) in _phA:
            wdst = wqcol[:, (w_i % 2) * 1024:(w_i % 2 + 1) * 1024]
            wk = f"wqcol{w_i % 2}"
            p.dma("pool", wdst.rearrange("p (k o) -> p k o", k=8),
                  c.d[wtens][wname], writes=(wk,), tag="wl1")
            wb = w_i % 2
            for k in range(NKT):
                p.op("ten", lambda e, k=k, wdst=wdst: e.matmul(
                    pD[0:1, 0:128], c.sb["ones_col_bf"][:, :], wdst[:, k * 128:(k + 1) * 128],
                    start=(k == 0), stop=(k == NKT - 1)),
                    reads=(wk, "c_ones_col_bf"), writes=("pD",))
            p.op("dve", lambda e, wb=wb: e.tensor_copy(wsq[0:1, wb * 128:(wb + 1) * 128],
                                                       pD[0:1, 0:128]),
                 reads=("pD",), writes=(f"wsq{wb}",))
            for sc in range(4):
                sl = slice(sc * 512, (sc + 1) * 512)
                psl = slice((sc % 2) * 512, (sc % 2 + 1) * 512) if w_i < 2 else slice(0, 512)
                for k in range(NKT):
                    p.op("ten", lambda e, k=k, sl=sl, psl=psl, wdst=wdst, pst=pst: e.matmul(
                        pst[:, psl], wdst[:, k * 128:(k + 1) * 128], ln[k][:, sl],
                        start=(k == 0), stop=False),
                        reads=(wk, f"ln{k}"), writes=(kt,))
                p.op("ten", lambda e, sl=sl, psl=psl, pst=pst, wb=wb, boff=boff: e.matmul(
                    pst[:, psl], wsq[0:1, wb * 128:(wb + 1) * 128], rows["D"][0:1, sl],
                    start=False, stop=(boff is None)),
                    reads=(f"wsq{wb}", "rowD"), writes=(kt,))
                if boff is not None:
                    p.op("ten", lambda e, sl=sl, psl=psl, pst=pst, boff=boff, hp=hp: e.matmul(
                        pst[:, psl], brow[0:1, hp * 128:(hp + 1) * 128],
                        c.sb["ones_row_f32"][0:1, 0:512], start=False, stop=True),
                        reads=("brow", "c_ones_row_f32"), writes=(kt,))
                if w_i == 0 and sc % 2 == 1:
                    ch2 = sc // 2
                    p.op("act", lambda e, ch2=ch2, pst=pst: e.activation(
                        ksb[:, ch2 * 1024:(ch2 + 1) * 1024], pst[:, :], AF.Sigmoid),
                        reads=(kt,), writes=("ksb",))
                elif w_i == 1 and sc % 2 == 1:
                    ch2 = sc // 2
                    p.op("act", lambda e, ch2=ch2, pst=pst: e.activation(
                        qsb[:, ch2 * 1024:(ch2 + 1) * 1024], pst[:, :], AF.Identity),
                        reads=(kt,), writes=("qsb",))
                elif w_i == 2:
                    p.op("act", lambda e, sc=sc, pst=pst: e.activation(
                        vfm[:, sc * 512:(sc + 1) * 512], pst[:, :], AF.Identity),
                        reads=(kt,), writes=("vfm",))

        if L1_PHASES < 2:
            continue
        if L1_COPS in (2.9, 2.95):
            p.op("dve", lambda e: e.tensor_copy(ktl[:, :], ln[0][:, :]),
                 reads=("ln0",), writes=("ktl",))
            p.op("dve", lambda e: e.tensor_copy(abf[:, :], ln[1][:, :]),
                 reads=("ln1",), writes=("abf",))
            p.op("dve", lambda e: e.tensor_copy(bbf[:, :], ln[2][:, :]),
                 reads=("ln2",), writes=("bbf",))
            p.op("dve", lambda e: e.memset(dec[:, :], 1.0), writes=("dec",))
            p.op("dve", lambda e: e.memset(KVf[:, :], 0.0), writes=("KVf",))
            p.op("dve", lambda e: e.memset(KVsb[:, :], 0.0), writes=("KVsb",))
            p.op("dve", lambda e: e.memset(vtok16[:, :], 0.0), writes=("vtok16",))
            p.op("dve", lambda e: e.memset(btok16[:, :], 0.0), writes=("btok16",))
            p.op("dve", lambda e: e.memset(smbf[:, :], 0.0), writes=("smbf",))
            p.op("dve", lambda e: e.memset(qsb[:, :], 0.0), writes=("qsb",))
        else:
            if True:
                pass
        if L1_COPS not in (2.9, 2.95):
            p.op("dve", lambda e: e.tensor_scalar(usb[:, :], ksb[:, :], -1.0, 1.0,
                                                  OP.mult, OP.add),
                 reads=("ksb",), writes=("usb",))
            p.op("dve", lambda e: e.tensor_scalar_max(ksb[:, :], ksb[:, :], 1e-38),
                 reads=("ksb",), writes=("ksb",))
            p.op("act", lambda e: e.activation(ksb[:, :], ksb[:, :], AF.Ln),
                 reads=("ksb",), writes=("ksb",))
            p.op("dve", lambda e: e.memset(fw1[:, :], 0.0), writes=("rbc",))
            p.op("dve", lambda e: e.memset(Lsb[:, 0:16], 0.0), writes=("Lsb",))
            p.op("dve", lambda e: e.tensor_tensor_scan(Lsb[:, 1:T + 1], ksb[:, :], fw1[:, :],
                                                       0.0, OP.add, OP.add),
                 reads=("ksb", "rbc"), writes=("Lsb",))
            pivA = Lsb[:, 0:T:16].unsqueeze(2).broadcast_to([128, T // 16, 16])
            pivB = Lsb[:, 16:T + 16:16].unsqueeze(2).broadcast_to([128, T // 16, 16])
            Lv = Lsb[:, 1:T + 1].rearrange("p (a b) -> p a b", b=16)
            p.op("dve", lambda e: e.tensor_tensor(fw1[:, :].rearrange("p (a b) -> p a b", b=16),
                                                  Lv, pivA, OP.subtract),
                 reads=("Lsb",), writes=("rbc",))
            p.op("act", lambda e: e.activation(af32[:, :], fw1[:, :], AF.Exp),
                 reads=("rbc",), writes=("afx", "xbf0", "xbf1"))
            p.op("dve", lambda e: e.tensor_scalar(fw1[:, :], fw1[:, :], -1.0, 82.0,
                                                  OP.mult, OP.min),
                 reads=("rbc",), writes=("rbc",))
            p.op("act", lambda e: e.activation(fw1[:, :], fw1[:, :], AF.Exp),
                 reads=("rbc",), writes=("rbc",))
            p.op("dve", lambda e: e.tensor_mul(ktl[:, :], usb[:, :], fw1[:, :]),
                 reads=("usb", "rbc"), writes=("ktl",))
            p.op("dve", lambda e: e.tensor_mul(qsb[:, :], qsb[:, :], af32[:, :]),
                 reads=("qsb", "afx"), writes=("qsb",))
            p.op("dve", lambda e: e.tensor_copy(abf[:, :], qsb[:, :]),
                 reads=("qsb",), writes=("abf",))
            p.op("dve", lambda e: e.tensor_tensor(fw1[:, :].rearrange("p (a b) -> p a b", b=16),
                                                  pivB, Lv, OP.subtract),
                 reads=("Lsb",), writes=("rbc",))
            p.op("act", lambda e: e.activation(fw1[:, :], fw1[:, :], AF.Exp),
                 reads=("rbc",), writes=("rbc",))
            p.op("dve", lambda e: e.tensor_mul(bbf[:, :], usb[:, :], fw1[:, :]),
                 reads=("usb", "rbc"), writes=("bbf",))
            p.op("dve", lambda e: e.tensor_tensor(dec[:, :], Lsb[:, 16:T + 16:16],
                                                  Lsb[:, 0:T:16], OP.subtract),
                 reads=("Lsb",), writes=("dec",))
            p.op("act", lambda e: e.activation(dec[:, :], dec[:, :], AF.Exp),
                 reads=("dec",), writes=("dec",))

            if DEBUG_TAPS and hp == 0:
                for nm, buf in (("ktl", ktl), ("abf", abf)):
                    dd = nc.dram_tensor("dbg_" + nm, [128, T], BF16, kind="ExternalOutput").ap()
                    p.dma("sp", dd, buf[:, :], reads=(nm,), tag="dbg")

        if L1_PHASES < 3 or hp >= L1_NHP:
            continue
        p.op("dve", lambda e: e.memset(bdK[:, :], 0.0), writes=("bdK",))
        p.op("dve", lambda e: e.memset(bdT[:, :], 0.0), writes=("bdT",))
        p.op("dve", lambda e: e.memset(vBDp[:, :], 0.0), writes=("vBDp",))
        p.op("dve", lambda e: e.memset(bBDp[:, :], 0.0), writes=("bBDp",))
        p.op("dve", lambda e: e.memset(smp[:, :], 0.0), writes=("smp",))
        p.op("dve", lambda e: e.memset(KVbd[:, :], 0.0), writes=("KVbd",))
        p.op("dve", lambda e: e.memset(KVbds[:, :], 0.0), writes=("KVbds",))
        for cc in range(min(T // 16, L1_NCHUNK)):
            sl = slice(cc * 16, (cc + 1) * 16)
            g8 = cc % 8
            p.op("dve", lambda e, sl=sl: e.tensor_copy(bdK[0:64, 0:16], ktl[0:64, sl]),
                 reads=("ktl",), writes=("bdK",))
            p.op("dve", lambda e, sl=sl: e.tensor_copy(bdK[64:128, 16:32], ktl[64:128, sl]),
                 reads=("ktl",), writes=("bdK",))
            p.op("dve", lambda e, sl=sl: e.tensor_copy(bdT[0:64, 0:16], vfm[0:64, sl]),
                 reads=("vfm",), writes=("bdT",))
            p.op("dve", lambda e, sl=sl: e.tensor_copy(bdT[64:128, 16:32], vfm[64:128, sl]),
                 reads=("vfm",), writes=("bdT",))
            p.op("dve", lambda e, sl=sl: e.tensor_copy(bdT[0:64, 32:48], bbf[0:64, sl]),
                 reads=("bbf",), writes=("bdT",))
            p.op("dve", lambda e, sl=sl: e.tensor_copy(bdT[64:128, 48:64], bbf[64:128, sl]),
                 reads=("bbf",), writes=("bdT",))
            p.op("ten", lambda e: e.matmul(pE[0:64, 0:128], bdT[:, 0:64],
                                           c.sb["identity"][:, :], start=True, stop=True),
                 reads=("bdT", "c_identity"), writes=("pE",))
            p.op("dve", lambda e: e.tensor_copy(vBDp[0:32, :], pE[0:32, 0:128]),
                 reads=("pE",), writes=("vBDp",))
            p.op("dve", lambda e: e.tensor_copy(bBDp[0:32, :], pE[32:64, 0:128]),
                 reads=("pE",), writes=("bBDp",))
            p.op("ten", lambda e, sl=sl: e.matmul(pD[0:32, 0:16], bdK[:, 0:32],
                                                  abf[:, sl], start=True, stop=True),
                 reads=("bdK", "abf"), writes=("pD",))
            p.op("dve", lambda e: e.select(smp[0:32, 0:16], c.sb["seltri32"][:, :],
                                           pD[0:32, 0:16], c.sb["zeros3216"][:, :]),
                 reads=("pD", "c_seltri32", "c_zeros3216"), writes=("smp",))
            if cc > 0:
                p.op("ten", lambda e, sl=sl, g8=g8: e.matmul(
                    pA[:, g8 * 16:(g8 + 1) * 16], KVbds[:, :], abf[:, sl],
                    start=True, stop=False),
                    reads=("KVbds", "abf"), writes=("pA",))
            p.op("ten", lambda e, cc=cc, g8=g8: e.matmul(
                pA[:, g8 * 16:(g8 + 1) * 16], vBDp[:, :], smp[:, 0:16],
                start=(cc == 0), stop=True),
                reads=("vBDp", "smp"), writes=("pA",))
            p.op("ten", lambda e: e.matmul(pC[:, 0:128], bBDp[:, :], vBDp[:, :],
                                           start=True, stop=True),
                 reads=("bBDp", "vBDp"), writes=("pC",))
            p.op("dve", lambda e, cc=cc: e.scalar_tensor_tensor(
                KVbd[:, :], KVbd[:, :], dec[:, cc:cc + 1], pC[:, 0:128],
                OP.mult, OP.add),
                reads=("KVbd", "dec", "pC"), writes=("KVbd",))
            p.op("dve", lambda e: e.tensor_copy(KVbds[:, :], KVbd[:, :]),
                 reads=("KVbd",), writes=("KVbds",))
            if g8 == 7 or cc == min(T // 16, L1_NCHUNK) - 1:
                gi = cc // 8
                p.op("dve", lambda e, gi=gi, hp=hp: e.tensor_copy(
                    hpre[hp][:, gi * 128:gi * 128 + 128], pA[:, 0:128]),
                    reads=("pA",), writes=(f"hpre{hp}",))

    for odp in range(4):
        xr = xrot[odp % 2]
        xk = f"xrot{odp % 2}"
        p.dma("sp", xr.rearrange("p (k t) -> p k t", k=2),
              xin_r[:, 2 * odp:2 * odp + 2, 0:T], writes=(xk,), tag="xres")
        for j in range(2):
            od = 2 * odp + j
            wdst = wocol[:, 0:1024]
            wk = "wocol"
            p.dma("pool", wdst.rearrange("p (k o) -> p k o", k=8),
                  c.d["out_w_p"][od], writes=(wk,), tag="wout")
            for tch in range(4):
                sl = slice(tch * 512, (tch + 1) * 512)
                pst = pA if tch % 2 == 0 else pB
                kt = "pA" if tch % 2 == 0 else "pB"
                for k in range(NKT):
                    p.op("ten", lambda e, k=k, sl=sl, wdst=wdst, pst=pst: e.matmul(
                        pst[:, 0:512], wdst[:, k * 128:(k + 1) * 128], hpre[k][:, sl],
                        start=(k == 0), stop=False),
                        reads=(wk, f"hpre{k}"), writes=(kt,))
                p.op("ten", lambda e, od=od, pst=pst: e.matmul(
                    pst[:, 0:512], rows["B"][0:1, D + od * 128:D + (od + 1) * 128],
                    c.sb["ones_row_f32"][0:1, 0:512], start=False, stop=True),
                    reads=("borow", "c_ones_row_f32"), writes=(kt,))
                p.op("dve", lambda e, j=j, sl=sl, xr=xr, pst=pst, kt=kt, xk=xk: e.scalar_tensor_tensor(
                    xr[:, j * T + sl.start:j * T + sl.stop], pst[:, 0:512], 1.0,
                    xr[:, j * T + sl.start:j * T + sl.stop], OP.mult, OP.add),
                    reads=(kt, xk), writes=(xk,))
        p.dma("sp", xout_r[:, 2 * odp:2 * odp + 2, 0:T],
              xr.rearrange("p (k t) -> p k t", k=2), reads=(xk,), tag="xstore")

EMITTERS = {0: emit_l0_ffn, 1: emit_l0_attn, 2: emit_l0_ffn,
            3: emit_l1_ffn, 4: emit_l1_attn, 5: emit_l1_ffn}

_NC_CACHE = {}


def kernel(**inputs):
    g = prep_inputs(inputs)
    x = np.asarray(inputs["L0_attn_x"])
    xT = [np.ascontiguousarray(x[b].T) for b in range(8)]

    for group in SUBLAYER_GROUPS:
        key = tuple(group)
        if key not in _NC_CACHE:
            _NC_CACHE[key] = build_program(group)
        nc = _NC_CACHE[key]
        per_core = []
        for b in range(8):
            m = dict(g)
            m["xT"] = xT[b]
            per_core.append(m)
        res = run_bass_kernel_spmd(nc, per_core, core_ids=list(range(8)))
        xT = [np.ascontiguousarray(res.results[b]["out"]) for b in range(8)]

    return np.stack([o.T for o in xT], axis=0).astype(np.float32)


# version 06
# speedup vs baseline: 1.1168x; 1.1168x over previous
import numpy as np
import ml_dtypes
import concourse.bass as bass
import concourse.mybir as mybir
from concourse.bass_utils import run_bass_kernel_spmd

F32 = mybir.dt.float32
BF16 = mybir.dt.bfloat16
AF = mybir.ActivationFunctionType
OP = mybir.AluOpType
BF = ml_dtypes.bfloat16

D = 1024
T = 2048
NKT = D // 128

SUBLAYER_GROUPS = [[0], [1], [2], [3], [4], [5]]
DEBUG_TAPS = False
L1_PHASES = 99
L1_NCHUNK = 128
L1_NHP = 8
L1_COPS = 9


ENGS = ("ten", "act", "dve", "pool", "sp")
DMA_Q = ("pool", "sp")


class Op:
    __slots__ = ("eng", "fn", "reads", "writes", "idx", "needs_inc", "seq",
                 "waits", "tag", "_deps")

    def __init__(self, eng, fn, reads, writes, tag=""):
        self.eng = eng
        self.fn = fn
        self.reads = tuple(reads)
        self.writes = tuple(writes)
        self.needs_inc = False
        self.seq = 0
        self.waits = []
        self.tag = tag
        self._deps = []


class Prog:
    def __init__(self):
        self.items = []
        self.per_eng = {e: [] for e in ENGS}

    def op(self, eng, fn, reads=(), writes=(), tag=""):
        o = Op(eng, fn, reads, writes, tag or "op")
        o.tag = f"{o.tag}|{len(self.per_eng[eng])}"
        self.per_eng[eng].append(o)
        self.items.append(o)
        return o

    def dma(self, q, out, in_, reads=(), writes=(), tag=""):
        assert q in DMA_Q
        return self.op(q, lambda e, out=out, in_=in_: e.dma_start(out=out, in_=in_),
                       reads=reads, writes=writes, tag=tag)

    def barrier(self):
        self.items.append("BARRIER")

    def schedule(self):
        last_w = {}
        readers = {}
        eng_last = {e: None for e in ENGS}
        pending_barrier = {}
        for it in self.items:
            if it == "BARRIER":
                prods = [op for op in eng_last.values() if op is not None]
                pending_barrier = {e: [pp for pp in prods if pp.eng != e] for e in ENGS}
                last_w = {}
                readers = {}
                continue
            o = it
            deps = list(pending_barrier.pop(o.eng, ()))
            for b in o.reads:
                w = last_w.get(b)
                if w is not None and w.eng != o.eng:
                    deps.append(w)
            for b in o.writes:
                w = last_w.get(b)
                if w is not None and w.eng != o.eng:
                    deps.append(w)
                for reng, rop in readers.get(b, {}).items():
                    if reng != o.eng:
                        deps.append(rop)
            for p in deps:
                p.needs_inc = True
            o._deps = deps
            for b in o.reads:
                readers.setdefault(b, {})[o.eng] = o
            for b in o.writes:
                last_w[b] = o
                readers[b] = {}
            eng_last[o.eng] = o
        cnt = {e: 0 for e in ENGS}
        for it in self.items:
            if it == "BARRIER":
                continue
            if it.eng in DMA_Q:
                it.needs_inc = True
            if it.needs_inc:
                cnt[it.eng] += 1
            it.seq = cnt[it.eng]
        waited = {}
        for it in self.items:
            if it == "BARRIER":
                continue
            need = {}
            for p in it._deps:
                need[p.eng] = max(need.get(p.eng, 0), p.seq)
            for peng in sorted(need):
                seq = need[peng]
                key = (it.eng, peng)
                if waited.get(key, 0) < seq:
                    it.waits.append((peng, seq))
                    waited[key] = seq

    def emit(self, nc):
        self.schedule()
        sems = {}
        ctxs = []
        for e in ENGS:
            cm = nc.semaphore("sem_" + e)
            ctxs.append(cm)
            sems[e] = cm.__enter__()
        for q in DMA_Q:
            cm = nc.semaphore("raw_" + q)
            ctxs.append(cm)
            sems["raw_" + q] = cm.__enter__()
        blk_cm = nc.Block()
        block = blk_cm.__enter__()

        def run_engine(eng_obj, eng_name):
            is_q = eng_name in DMA_Q
            for o in self.per_eng[eng_name]:
                for peng, seq in o.waits:
                    eng_obj.wait_ge(sems[peng], seq)
                instr = o.fn(eng_obj)
                instr.annotate(o.tag)
                if o.needs_inc:
                    if is_q:
                        instr.then_inc(sems["raw_" + eng_name], 16)
                        eng_obj.wait_ge(sems["raw_" + eng_name], o.seq * 16)
                        eng_obj.sem_inc(sems[eng_name], 1)
                    else:
                        instr.then_inc(sems[o.eng], 1)

        block.tensor(lambda e: run_engine(e, "ten"))
        block.scalar(lambda e: run_engine(e, "act"))
        block.vector(lambda e: run_engine(e, "dve"))
        block.gpsimd(lambda e: run_engine(e, "pool"))

        def run_sync(e):
            run_engine(e, "sp")
            for en in ("pool", "ten", "act", "dve"):
                tot = sum(1 for o in self.per_eng[en] if o.needs_inc)
                if tot:
                    e.wait_ge(sems[en], tot)

        block.sync(run_sync)
        blk_cm.__exit__(None, None, None)
        for cm in reversed(ctxs):
            cm.__exit__(None, None, None)


def _pack_cols(wT, I, O):
    a = np.ascontiguousarray(np.asarray(wT, dtype=np.float32))
    assert a.shape == (I, O)
    return np.ascontiguousarray(
        a.reshape(I // 128, 128, O // 128, 128).transpose(2, 1, 0, 3))


def _cols(a):
    a = np.asarray(a, dtype=np.float32)
    return np.ascontiguousarray(a.reshape(-1, 128).T)


def prep_inputs(kw):
    g = {}
    g["w_qkv_p"] = _pack_cols(np.asarray(kw["L0_attn_w_qkv"]).T, D, 3 * D)
    g["b_qkv"] = np.asarray(kw["L0_attn_b_qkv"], np.float32).reshape(1, -1)
    g["w_out_p"] = _pack_cols(np.asarray(kw["L0_attn_w_out"]).T, D, D)
    g["b_out"] = np.asarray(kw["L0_attn_b_out"], np.float32).reshape(1, -1)
    om = np.asarray(kw["L0_attn_omega"], np.float32)
    g["omegaT"] = np.ascontiguousarray(np.transpose(om, (1, 2, 0, 3)).reshape(16, 64, 64))
    g["qnodes2"] = np.asarray(kw["L0_attn_quad_nodes"], np.float32).reshape(2, 1)
    g["qweights2"] = np.asarray(kw["L0_attn_quad_weights"], np.float32).reshape(2, 1)
    g["w1_p"] = _pack_cols(np.asarray(kw["L0_ffn_w1_w"]).T, D, 4096)
    g["w2_p"] = _pack_cols(np.asarray(kw["L0_ffn_w2_w"]).T, D, 4096)
    g["h1_p"] = _pack_cols(np.asarray(kw["L0_ffn_h1_w"]).T, 4096, 4096)
    g["h2_p"] = _pack_cols(np.asarray(kw["L0_ffn_h2_w"]).T, 4096, 4096)
    g["w3_p"] = _pack_cols(np.asarray(kw["L0_ffn_w3_w"]).T, 4096, D)
    for s in ("w1_s", "w2_s", "h1_s", "h2_s", "w3_s"):
        g[s] = _cols(kw["L0_ffn_" + s])
    g["key_w_p"] = _pack_cols(np.asarray(kw["L1_attn_key_w"]).T, D, D)
    g["key_b"] = np.asarray(kw["L1_attn_key_b"], np.float32).reshape(1, -1)
    g["vq_w_p"] = _pack_cols(np.asarray(kw["L1_attn_vq_w"]).T, D, 2 * D)
    g["out_w_p"] = _pack_cols(np.asarray(kw["L1_attn_out_w"]).T, D, D)
    g["out_b"] = np.asarray(kw["L1_attn_out_b"], np.float32).reshape(1, -1)
    G1R = np.asarray(kw["L1_ffn_G1_R"], np.float32)[0]
    G2R = np.asarray(kw["L1_ffn_G2_R"], np.float32)[:, :, 0, :]
    G1I = np.asarray(kw["L1_ffn_G1_I"], np.float32)[0]
    G2I = np.asarray(kw["L1_ffn_G2_I"], np.float32)[:, :, 0, :]

    def expand(G1, G2):
        A = np.broadcast_to(G1.transpose(1, 2, 0)[:, :, None, :, None], (4, 32, 16, 32, 16))
        A = np.ascontiguousarray(A.reshape(4, 512, 512))
        B = np.broadcast_to(G2.transpose(0, 2, 1)[:, None, :, None, :], (4, 32, 16, 32, 16))
        B = np.ascontiguousarray(B.reshape(4, 512, 512))
        return A, B
    g["G1R_x"], g["G2R_x"] = expand(G1R, G2R)
    g["G1I_x"], g["G2I_x"] = expand(G1I, G2I)
    g["bias_ri"] = np.concatenate([np.asarray(kw["L1_ffn_bias_real"]),
                                   np.asarray(kw["L1_ffn_bias_imag"])]).astype(np.float32).reshape(1, -1)

    g["trimask"] = np.triu(np.ones((128, 128), np.float32)).astype(BF)
    g["identity"] = np.eye(128, dtype=np.float32).astype(BF)
    g["ones_col_f32"] = np.ones((128, 1), np.float32)
    g["ones_col_bf"] = np.ones((128, 1), np.float32).astype(BF)
    g["ones_row_bf"] = np.ones((1, 512), np.float32).astype(BF)
    g["ones_row_f32"] = np.ones((1, 1024), np.float32)
    g["eps_col"] = np.full((1, 1), 1e-5, np.float32)
    tri16 = np.triu(np.ones((16, 16), np.float32)).astype(np.uint8)
    g["seltri16"] = np.concatenate([tri16, tri16], axis=1)
    g["zeros1632"] = np.zeros((16, 32), np.float32)
    g["seltri32"] = np.concatenate([tri16, tri16], axis=0)
    g["zeros3216"] = np.zeros((32, 16), np.float32)
    tri16s = np.triu(np.ones((16, 16), np.float32), 1).astype(np.uint8)
    g["selstrict32"] = np.concatenate([tri16s, tri16s], axis=0)
    bM = np.zeros((128, 128), np.float32)
    bM[:64, :64] = 1.0
    bM[64:, 64:] = 1.0
    g["blockM"] = bM.astype(BF)
    bo2 = np.zeros((128, 2), np.float32)
    bo2[:64, 0] = 1.0
    bo2[64:, 1] = 1.0
    g["blockones2"] = bo2.astype(BF)
    g["eps30"] = np.full((128, 1), 1e-30, np.float32)
    ind2 = np.zeros((2, 128), np.float32)
    ind2[0, :64] = 1.0
    ind2[1, 64:] = 1.0
    g["ind2"] = ind2
    sel2 = np.zeros((2, 128), np.float32)
    rr = (np.arange(128) // 32) % 2
    sel2[0, rr == 0] = 1.0
    sel2[1, rr == 1] = 1.0
    g["sel2"] = sel2
    return g


DEV_INPUTS = [
    ("xT", [D, T], F32),
    ("w_qkv_p", [24, 128, 8, 128], F32), ("b_qkv", [1, 3 * D], F32),
    ("w_out_p", [8, 128, 8, 128], F32), ("b_out", [1, D], F32),
    ("omegaT", [16, 64, 64], F32), ("qnodes2", [2, 1], F32), ("qweights2", [2, 1], F32),
    ("w1_p", [32, 128, 8, 128], F32), ("w2_p", [32, 128, 8, 128], F32),
    ("h1_p", [32, 128, 32, 128], F32), ("h2_p", [32, 128, 32, 128], F32),
    ("w3_p", [8, 128, 32, 128], F32),
    ("w1_s", [128, 32], F32), ("w2_s", [128, 32], F32),
    ("h1_s", [128, 32], F32), ("h2_s", [128, 32], F32), ("w3_s", [128, 8], F32),
    ("key_w_p", [8, 128, 8, 128], F32), ("key_b", [1, D], F32),
    ("vq_w_p", [16, 128, 8, 128], F32),
    ("out_w_p", [8, 128, 8, 128], F32), ("out_b", [1, D], F32),
    ("G1R_x", [4, 512, 512], F32), ("G2R_x", [4, 512, 512], F32),
    ("G1I_x", [4, 512, 512], F32), ("G2I_x", [4, 512, 512], F32),
    ("bias_ri", [1, D], F32),
    ("trimask", [128, 128], BF16), ("identity", [128, 128], BF16),
    ("ones_col_f32", [128, 1], F32), ("ones_col_bf", [128, 1], BF16),
    ("ones_row_bf", [1, 512], BF16), ("ones_row_f32", [1, 1024], F32),
    ("eps_col", [1, 1], F32),
    ("ind2", [2, 128], F32), ("sel2", [2, 128], F32),
    ("blockones2", [128, 2], BF16), ("eps30", [128, 1], F32),
    ("seltri16", [16, 32], mybir.dt.uint8), ("zeros1632", [16, 32], F32),
    ("seltri32", [32, 16], mybir.dt.uint8), ("zeros3216", [32, 16], F32),
    ("selstrict32", [32, 16], mybir.dt.uint8), ("blockM", [128, 128], BF16),
]
CONST_NAMES = ("trimask", "identity", "ones_col_f32", "ones_col_bf", "ones_row_bf",
               "ones_row_f32", "eps_col", "ind2", "sel2", "blockones2", "eps30",
               "seltri16", "zeros1632", "seltri32", "zeros3216",
               "selstrict32", "blockM")


class Ctx:
    pass


def build_program(sublayers):
    nc = bass.Bass("TRN2", target_bir_lowering=False, detect_race_conditions=False)
    c = Ctx()
    c.nc = nc
    c.d = {}
    for name, shape, dt in DEV_INPUTS:
        c.d[name] = nc.dram_tensor(name, shape, dt, kind="ExternalInput").ap()
    c.d["out"] = nc.dram_tensor("out", [D, T], F32, kind="ExternalOutput").ap()
    c.d["xs0"] = nc.dram_tensor("xs0", [D, T], F32, kind="Internal").ap()
    c.d["xs1"] = nc.dram_tensor("xs1", [D, T], F32, kind="Internal").ap()

    p = Prog()
    c.p = p
    sb_cms = []
    c.sb = {}
    for nm in CONST_NAMES:
        shape = dict((n, s) for n, s, _ in DEV_INPUTS)[nm]
        dt = dict((n, dd) for n, _, dd in DEV_INPUTS)[nm]
        cm = nc.sbuf_tensor("c_" + nm, shape, dt)
        sb_cms.append(cm)
        c.sb[nm] = cm.__enter__()
        p.dma("sp", c.sb[nm][:, :], c.d[nm], writes=("c_" + nm,), tag="const")

    cur = "xT"
    scratch = ["xs0", "xs1"]
    for i, sl in enumerate(sublayers):
        nxt = "out" if i == len(sublayers) - 1 else scratch[i % 2]
        EMITTERS[sl](c, c.d[cur], c.d[nxt])
        p.barrier()
        cur = nxt

    p.emit(nc)
    return nc


def emit_l0_ffn(c, x_in, x_out):
    nc, p = c.nc, c.p
    TC = 1024
    NCH = T // TC
    sb_cms = []

    def alloc(name, shape, dt):
        cm = nc.sbuf_tensor(name, shape, dt)
        sb_cms.append(cm)
        return cm.__enter__()

    xrot = [alloc(f"xrot{i}", [128, 2 * TC], F32) for i in range(2)]
    ln = [alloc(f"lnt{k}", [128, TC], BF16) for k in range(NKT)]
    hbuf = alloc("hbuf", [128, 32 * TC], BF16)
    gbuf = alloc("gbuf", [128, 32 * TC], BF16)
    wcol = alloc("wcol", [128, 2 * 32 * 128], BF16)
    wsml = alloc("wsml", [128, 2 * 2048], BF16)
    scol = alloc("scol", [128, 128], F32)
    s3col = alloc("s3col", [128, 8], F32)
    wsrow = alloc("wsrow", [1, 2 * 2 * 128], BF16)
    rowA = alloc("rowA", [1, TC], F32)
    rowB = alloc("rowB", [1, TC], F32)
    rowC = alloc("rowC", [1, TC], F32)
    rowD = alloc("rowD", [1, TC], BF16)
    ps = []
    for i in range(4):
        cm = nc.psum_tensor(f"ps{i}", [128, TC], F32)
        sb_cms.append(cm)
        ps.append(cm.__enter__())

    xbf = [hbuf[:, k * TC:(k + 1) * TC] for k in range(NKT)]
    x2v = [hbuf[:, (8 + k) * TC:(9 + k) * TC] for k in range(NKT)]
    silu_v = [gbuf[:, (i * 2) * TC:(i * 2 + 2) * TC].bitcast(F32) for i in range(2)]

    p.dma("sp", scol[:, 0:32], c.d["w1_s"], writes=("scol",))
    p.dma("sp", scol[:, 32:64], c.d["w2_s"], writes=("scol",))
    p.dma("sp", scol[:, 64:96], c.d["h1_s"], writes=("scol",))
    p.dma("sp", scol[:, 96:128], c.d["h2_s"], writes=("scol",))
    p.dma("sp", s3col[:, :], c.d["w3_s"], writes=("s3col",))
    p.op("dve", lambda e: e.tensor_scalar_mul(s3col[:, :], s3col[:, :], 0.5),
         reads=("s3col",), writes=("s3col",))

    xin_r = x_in.rearrange("(k p) t -> p k t", p=128)
    xout_r = x_out.rearrange("(k p) t -> p k t", p=128)

    for ch in range(NCH):
        tsl = slice(ch * TC, (ch + 1) * TC)

        for kk in range(4):
            xr = xrot[kk % 2]
            xk = f"xrot{kk % 2}"
            p.dma("sp", xr.rearrange("p (k t) -> p k t", k=2),
                  xin_r[:, 2 * kk:2 * kk + 2, tsl], writes=(xk,), tag="xload")
            for j in range(2):
                k = 2 * kk + j
                p.op("dve", lambda e, xr=xr, j=j, k=k: e.tensor_copy(
                    xbf[k], xr[:, j * TC:(j + 1) * TC]),
                    reads=(xk,), writes=(f"h{k}",))
                p.op("dve", lambda e, k=k: e.tensor_mul(x2v[k], xbf[k], xbf[k]),
                    reads=(f"h{k}",), writes=(f"h{8 + k}",))
        for sc in range(TC // 512):
            sl = slice(sc * 512, (sc + 1) * 512)
            for k in range(NKT):
                p.op("ten", lambda e, k=k, sl=sl: e.matmul(
                    ps[0][0:1, sl], c.sb["ones_col_bf"][:, :], xbf[k][:, sl],
                    start=(k == 0), stop=(k == NKT - 1)),
                    reads=(f"h{k}", "c_ones_col_bf"), writes=("ps0",))
                p.op("ten", lambda e, k=k, sl=sl: e.matmul(
                    ps[1][0:1, sl], c.sb["ones_col_bf"][:, :], x2v[k][:, sl],
                    start=(k == 0), stop=(k == NKT - 1)),
                    reads=(f"h{8 + k}", "c_ones_col_bf"), writes=("ps1",))
            p.op("dve", lambda e, sl=sl: e.tensor_copy(rowA[0:1, sl], ps[0][0:1, sl]),
                 reads=("ps0",), writes=("rowA",))
            p.op("dve", lambda e, sl=sl: e.tensor_copy(rowB[0:1, sl], ps[1][0:1, sl]),
                 reads=("ps1",), writes=("rowB",))
        p.op("dve", lambda e: e.tensor_scalar_mul(rowA[0:1, :], rowA[0:1, :], 1.0 / D),
             reads=("rowA",), writes=("rowA",))
        p.op("dve", lambda e: e.scalar_tensor_tensor(rowC[0:1, :], rowA[0:1, :], -1.0,
                                                     rowA[0:1, :], OP.mult, OP.mult),
             reads=("rowA",), writes=("rowC",))
        p.op("dve", lambda e: e.scalar_tensor_tensor(rowB[0:1, :], rowB[0:1, :], 1.0 / D,
                                                     rowC[0:1, :], OP.mult, OP.add),
             reads=("rowB", "rowC"), writes=("rowB",))
        p.op("act", lambda e: e.activation(rowC[0:1, :], rowB[0:1, :], AF.Ln,
                                           bias=c.sb["eps_col"][0:1, 0:1], scale=1.0),
             reads=("rowB", "c_eps_col"), writes=("rowC",))
        p.op("act", lambda e: e.activation(rowC[0:1, :], rowC[0:1, :], AF.Exp, scale=-0.5),
             reads=("rowC",), writes=("rowC",))
        p.op("dve", lambda e: e.scalar_tensor_tensor(rowA[0:1, :], rowA[0:1, :], -1.0,
                                                     rowC[0:1, :], OP.mult, OP.mult),
             reads=("rowA", "rowC"), writes=("rowA",))
        p.op("dve", lambda e: e.tensor_copy(rowD[0:1, :], rowA[0:1, :]),
             reads=("rowA",), writes=("rowD",))

        for sc in range(TC // 512):
            sl = slice(sc * 512, (sc + 1) * 512)
            p.op("ten", lambda e, sl=sl: e.matmul(
                ps[0][:, sl], c.sb["ones_row_f32"][0:1, 0:128], rowC[0:1, sl],
                start=True, stop=True),
                reads=("rowC", "c_ones_row_f32"), writes=("ps0",))
        for kk in range(4):
            xr = xrot[kk % 2]
            xk = f"xrot{kk % 2}"
            p.dma("sp", xr.rearrange("p (k t) -> p k t", k=2),
                  xin_r[:, 2 * kk:2 * kk + 2, tsl], writes=(xk,), tag="xload2")
            for j in range(2):
                k = 2 * kk + j
                p.op("dve", lambda e, xr=xr, j=j, k=k: e.tensor_mul(
                    ln[k][:, :], xr[:, j * TC:(j + 1) * TC], ps[0][:, :]),
                    reads=(xk, "ps0"), writes=(f"ln{k}",))

        for o in range(32):
            ob = o % 2
            w1dst = wsml[:, ob * 2048:ob * 2048 + 1024]
            w2dst = wsml[:, ob * 2048 + 1024:ob * 2048 + 2048]
            k1, k2 = f"wsml{ob}a", f"wsml{ob}b"
            p.dma("pool", w1dst.rearrange("p (k o) -> p k o", k=8),
                  c.d["w1_p"][o], writes=(k1,), tag="w1col")
            p.dma("pool", w2dst.rearrange("p (k o) -> p k o", k=8),
                  c.d["w2_p"][o], writes=(k2,), tag="w2col")
            pst1, kt1 = ps[2 + ob], f"ps{2 + ob}"
            pst2, kt2 = ps[0 + ob], f"ps{0 + ob}"
            for wd, wk, roff in ((w1dst, k1, ob * 256), (w2dst, k2, ob * 256 + 128)):
                for k in range(NKT):
                    p.op("ten", lambda e, k=k, wd=wd: e.matmul(
                        ps[1][0:1, 0:128], c.sb["ones_col_bf"][:, :], wd[:, k * 128:(k + 1) * 128],
                        start=(k == 0), stop=(k == NKT - 1)),
                        reads=(wk, "c_ones_col_bf"), writes=("ps1",))
                p.op("dve", lambda e, roff=roff: e.tensor_copy(
                    wsrow[0:1, roff:roff + 128], ps[1][0:1, 0:128]),
                    reads=("ps1",), writes=(f"wsrow{roff}",))
            for half in range(2):
                hsl = slice(half * 512, (half + 1) * 512)
                for k in range(NKT):
                    p.op("ten", lambda e, k=k, hsl=hsl, w1dst=w1dst, pst1=pst1: e.matmul(
                        pst1[:, hsl], w1dst[:, k * 128:(k + 1) * 128], ln[k][:, hsl],
                        start=(k == 0), stop=False),
                        reads=(k1, f"ln{k}"), writes=(kt1,))
                p.op("ten", lambda e, hsl=hsl, o=o, pst1=pst1, ob=ob: e.matmul(
                    pst1[:, hsl], wsrow[0:1, ob * 256:ob * 256 + 128], rowD[0:1, hsl],
                    start=False, stop=True),
                    reads=(f"wsrow{ob * 256}", "rowD"), writes=(kt1,))
                for k in range(NKT):
                    p.op("ten", lambda e, k=k, hsl=hsl, w2dst=w2dst, pst2=pst2: e.matmul(
                        pst2[:, hsl], w2dst[:, k * 128:(k + 1) * 128], ln[k][:, hsl],
                        start=(k == 0), stop=False),
                        reads=(k2, f"ln{k}"), writes=(kt2,))
                p.op("ten", lambda e, hsl=hsl, o=o, pst2=pst2, ob=ob: e.matmul(
                    pst2[:, hsl], wsrow[0:1, ob * 256 + 128:ob * 256 + 256], rowD[0:1, hsl],
                    start=False, stop=True),
                    reads=(f"wsrow{ob * 256 + 128}", "rowD"), writes=(kt2,))
            p.op("act", lambda e, o=o, pst1=pst1: e.activation(
                silu_v[o % 2], pst1[:, :], AF.Silu, scale=scol[:, o:o + 1]),
                reads=(kt1, "scol"), writes=(f"g{(o % 2) * 2}", f"g{(o % 2) * 2 + 1}"))
            p.op("dve", lambda e, o=o, pst2=pst2: e.scalar_tensor_tensor(
                hbuf[:, o * TC:(o + 1) * TC], pst2[:, :], scol[:, 32 + o:32 + o + 1],
                silu_v[o % 2], OP.mult, OP.mult),
                reads=(kt2, f"g{(o % 2) * 2}", f"g{(o % 2) * 2 + 1}", "scol"), writes=(f"h{o}",))

        for o in range(32):
            ob = o % 2
            wdst = wcol[:, ob * 4096:(ob + 1) * 4096]
            wk = f"wcol{ob}"
            p.dma("pool", wdst.rearrange("p (k o) -> p k o", k=32),
                  c.d["h1_p"][o], writes=(wk,), tag="h1col")
            pst, kt = ps[2 + ob], f"ps{2 + ob}"
            for half in range(2):
                hsl = slice(half * 512, (half + 1) * 512)
                for k in range(32):
                    p.op("ten", lambda e, k=k, hsl=hsl, wdst=wdst, pst=pst: e.matmul(
                        pst[:, hsl], wdst[:, k * 128:(k + 1) * 128],
                        hbuf[:, k * TC + hsl.start:k * TC + hsl.stop],
                        start=(k == 0), stop=(k == 31)),
                        reads=(wk, f"h{k}"), writes=(kt,))
            p.op("act", lambda e, o=o, pst=pst: e.activation(
                gbuf[:, o * TC:(o + 1) * TC], pst[:, :], AF.Gelu,
                scale=scol[:, 64 + o:64 + o + 1]),
                reads=(kt, "scol"), writes=(f"g{o}",))

        for o in range(32):
            ob = o % 2
            wdst = wcol[:, ob * 4096:(ob + 1) * 4096]
            wk = f"wcol{ob}"
            p.dma("pool", wdst.rearrange("p (k o) -> p k o", k=32),
                  c.d["h2_p"][o], writes=(wk,), tag="h2col")
            pst, kt = ps[2 + ob], f"ps{2 + ob}"
            for half in range(2):
                hsl = slice(half * 512, (half + 1) * 512)
                for k in range(32):
                    p.op("ten", lambda e, k=k, hsl=hsl, wdst=wdst, pst=pst: e.matmul(
                        pst[:, hsl], wdst[:, k * 128:(k + 1) * 128],
                        gbuf[:, k * TC + hsl.start:k * TC + hsl.stop],
                        start=(k == 0), stop=(k == 31)),
                        reads=(wk, f"g{k}"), writes=(kt,))
            p.op("act", lambda e, o=o, pst=pst: e.activation(
                hbuf[:, o * TC:(o + 1) * TC], pst[:, :], AF.Gelu,
                scale=scol[:, 96 + o:96 + o + 1]),
                reads=(kt, "scol"), writes=(f"h{o}",))

        for op2 in range(4):
            xr = xrot[op2 % 2]
            xk = f"xrot{op2 % 2}"
            p.dma("sp", xr.rearrange("p (k t) -> p k t", k=2),
                  xin_r[:, 2 * op2:2 * op2 + 2, tsl], writes=(xk,), tag="xres")
            for j in range(2):
                od = 2 * op2 + j
                ob = od % 2
                wdst = wcol[:, ob * 4096:(ob + 1) * 4096]
                wk = f"wcol{ob}"
                p.dma("pool", wdst.rearrange("p (k o) -> p k o", k=32),
                      c.d["w3_p"][od], writes=(wk,), tag="w3col")
                pst, kt = ps[2 + ob], f"ps{2 + ob}"
                for half in range(2):
                    hsl = slice(half * 512, (half + 1) * 512)
                    for k in range(32):
                        p.op("ten", lambda e, k=k, hsl=hsl, wdst=wdst, pst=pst: e.matmul(
                            pst[:, hsl], wdst[:, k * 128:(k + 1) * 128],
                            hbuf[:, k * TC + hsl.start:k * TC + hsl.stop],
                            start=(k == 0), stop=(k == 31)),
                            reads=(wk, f"h{k}"), writes=(kt,))
                p.op("dve", lambda e, od=od, j=j, xr=xr, pst=pst: e.scalar_tensor_tensor(
                    xr[:, j * TC:(j + 1) * TC], pst[:, :], s3col[:, od:od + 1],
                    xr[:, j * TC:(j + 1) * TC], OP.mult, OP.add),
                    reads=(kt, "s3col", xk), writes=(xk,))
            p.dma("sp", xout_r[:, 2 * op2:2 * op2 + 2, tsl],
                  xr.rearrange("p (k t) -> p k t", k=2), reads=(xk,), tag="xstore")

    for cm in reversed(sb_cms):
        cm.__exit__(None, None, None)



def emit_l0_attn(c, x_in, x_out):
    nc, p = c.nc, c.p
    sb_cms = []

    def alloc(name, shape, dt):
        cm = nc.sbuf_tensor(name, shape, dt)
        sb_cms.append(cm)
        return cm.__enter__()

    xrot = [alloc(f"xrot{i}", [128, 2 * T], F32) for i in range(2)]
    xbf2 = alloc("xbf2", [128, 2 * T], BF16)
    ln = [alloc(f"lnt{k}", [128, T], BF16) for k in range(NKT)]
    rows = {"A": alloc("rowA", [1, T], F32), "B": alloc("rowB", [1, T], F32),
            "C": alloc("rowC", [1, T], F32), "D": alloc("rowD", [1, T], BF16)}
    rbc = alloc("rbc", [128, T], F32)
    apre = [alloc(f"apre{i}", [128, T], BF16) for i in range(8)]
    qzb = alloc("qzb", [128, T], BF16)
    kzb = alloc("kzb", [128, T], BF16)
    qf = alloc("qf", [128, T], BF16)
    kf = alloc("kf", [128, T], BF16)
    vfm = alloc("vfm", [128, T], BF16)
    z2b = vfm
    om_sb = alloc("om_sb", [128, 64], BF16)
    wqcol = alloc("wqcol", [128, 2 * 1024], BF16)
    s2row = alloc("s2row", [2, 512], F32)
    rnbc = alloc("rnbc", [128, 1024], F32)
    fw1 = None
    fw2 = None
    tbf = alloc("tbf", [128, 1024], BF16)
    brow = alloc("brow", [1, 3 * D], F32)
    borow = alloc("borow", [1, D], F32)
    qcol2 = alloc("qcol2", [2, 4], F32)
    qcols = alloc("qcols", [128, 4], F32)
    kftok = alloc("kftok", [128, 128], BF16)
    wsq = alloc("wsq", [1, 256], BF16)
    vtok = alloc("vtok", [128, 16 * 128], BF16)
    smb = [alloc(f"smb{h}", [128, 128], BF16) for h in range(2)]
    KVsb = alloc("KVsb", [128, 64], BF16)
    kssb = alloc("kssb", [128, 1], BF16)
    KVf = alloc("KVf", [128, 64], F32)
    ksf = alloc("ksf", [128, 1], F32)
    ctxsb = alloc("ctxsb", [128, 512], F32)
    denr = alloc("denr", [1, 512], F32)
    wocol = alloc("wocol", [128, 1024], BF16)
    ps_cms = []

    def palloc(name, shape, dt=F32):
        cm = nc.psum_tensor(name, shape, dt)
        ps_cms.append(cm)
        sb_cms.append(cm)
        return cm.__enter__()

    fw1 = rbc[:, 0:1024]
    fw2 = rbc[:, 1024:2048]
    xin_r = x_in.rearrange("(k p) t -> p k t", p=128)
    xout_r = x_out.rearrange("(k p) t -> p k t", p=128)

    p.dma("sp", brow[0:1, :], c.d["b_qkv"], writes=("brow",))
    p.dma("sp", borow[0:1, :], c.d["b_out"], writes=("borow",))
    p.dma("sp", qcol2[:, 0:1], c.d["qnodes2"], writes=("qcol2",))
    p.dma("sp", qcol2[:, 1:2], c.d["qweights2"], writes=("qcol2",))

    ln_stream(c, p, xin_r, slice(0, T), T, xrot, ln, rows, rbc, xbf2)

    psq = palloc("psq", [128, 1024])
    psk = palloc("psk", [128, 1024])
    psp = palloc("psp", [128, 1024])
    psv = palloc("psv", [128, 512])
    psw = palloc("psw", [128, 512])

    p.op("dve", lambda e: e.tensor_scalar(qcol2[0:2, 2:3], qcol2[0:2, 0:1], 0.0, 2.0,
                                          OP.max, OP.mult), reads=("qcol2",), writes=("qcol2",))
    p.op("act", lambda e: e.activation(qcol2[0:2, 2:3], qcol2[0:2, 2:3], AF.Ln,
                                       bias=c.sb["eps30"][0:2, 0:1]),
         reads=("qcol2", "c_eps30"), writes=("qcol2",))
    p.op("act", lambda e: e.activation(qcol2[0:2, 2:3], qcol2[0:2, 2:3], AF.Exp, scale=0.5),
         reads=("qcol2",), writes=("qcol2",))
    p.op("dve", lambda e: e.tensor_scalar_mul(qcol2[0:2, 3:4], qcol2[0:2, 0:1], -1.0),
         reads=("qcol2",), writes=("qcol2",))
    p.op("dve", lambda e: e.tensor_scalar_max(qcol2[0:2, 1:2], qcol2[0:2, 1:2], 0.0),
         reads=("qcol2",), writes=("qcol2",))
    p.op("act", lambda e: e.activation(qcol2[0:2, 1:2], qcol2[0:2, 1:2], AF.Ln,
                                       bias=c.sb["eps30"][0:2, 0:1]),
         reads=("qcol2", "c_eps30"), writes=("qcol2",))
    p.op("dve", lambda e: e.tensor_scalar(qcol2[0:2, 1:2], qcol2[0:2, 1:2], 0.25,
                                          -1.7328679513998633, OP.mult, OP.add),
         reads=("qcol2",), writes=("qcol2",))
    for i, src in ((0, 2), (1, 1), (2, 3)):
        p.op("ten", lambda e, i=i, src=src: e.matmul(
            psw[:, i:i + 1], c.sb["sel2"][:, :], qcol2[0:2, src:src + 1],
            start=True, stop=True),
            reads=("qcol2", "c_sel2"), writes=("psw",))
        p.op("dve", lambda e, i=i: e.tensor_copy(qcols[:, i:i + 1], psw[:, i:i + 1]),
             reads=("psw",), writes=("qcols",))

    for hp in range(8):
        hb = hp % 2
        p.dma("pool", om_sb[0:64, :], c.d["omegaT"][2 * hp], writes=("om_sb",))
        p.dma("pool", om_sb[64:128, :], c.d["omegaT"][2 * hp + 1], writes=("om_sb",))
        for w_i, (wname, pst, kt, zb, zk, boff) in enumerate(
                ((hp, psq, "psq", qzb, "qzb", 0),
                 (8 + hp, psk, "psk", kzb, "kzb", 1024),
                 (16 + hp, psv, "psv", vfm, "vfm", 2048))):
            wdst = wqcol[:, w_i % 2 * 1024:(w_i % 2 + 1) * 1024]
            wk = f"wqcol{w_i % 2}"
            p.dma("pool", wdst.rearrange("p (k o) -> p k o", k=8),
                  c.d["w_qkv_p"][wname], writes=(wk,), tag="wqkv")
            wb = w_i % 2
            for k in range(NKT):
                p.op("ten", lambda e, k=k, wdst=wdst: e.matmul(
                    psw[0:1, 0:128], c.sb["ones_col_bf"][:, :], wdst[:, k * 128:(k + 1) * 128],
                    start=(k == 0), stop=(k == NKT - 1)),
                    reads=(wk, "c_ones_col_bf"), writes=("psw",))
            p.op("dve", lambda e, wb=wb: e.tensor_copy(wsq[0:1, wb * 128:(wb + 1) * 128],
                                                       psw[0:1, 0:128]),
                 reads=("psw",), writes=(f"wsq{wb}",))
            if w_i < 2:
                for sc in range(4):
                    sl = slice(sc * 512, (sc + 1) * 512)
                    psl = slice((sc % 2) * 512, (sc % 2 + 1) * 512)
                    for k in range(NKT):
                        p.op("ten", lambda e, k=k, sl=sl, psl=psl, wdst=wdst, pst=pst: e.matmul(
                            pst[:, psl], wdst[:, k * 128:(k + 1) * 128], ln[k][:, sl],
                            start=(k == 0), stop=False),
                            reads=(wk, f"ln{k}"), writes=(kt,))
                    p.op("ten", lambda e, sl=sl, psl=psl, pst=pst, wb=wb: e.matmul(
                        pst[:, psl], wsq[0:1, wb * 128:(wb + 1) * 128], rows["D"][0:1, sl],
                        start=False, stop=False),
                        reads=(f"wsq{wb}", "rowD"), writes=(kt,))
                    p.op("ten", lambda e, sl=sl, psl=psl, pst=pst, boff=boff, hp=hp: e.matmul(
                        pst[:, psl], brow[0:1, boff + hp * 128:boff + (hp + 1) * 128],
                        c.sb["ones_row_f32"][0:1, 0:512], start=False, stop=True),
                        reads=("brow", "c_ones_row_f32"), writes=(kt,))
                    if sc % 2 == 1:
                        ch2 = sc // 2
                        p.op("act", lambda e, ch2=ch2, pst=pst, zb=zb: e.activation(
                            zb[:, ch2 * 1024:(ch2 + 1) * 1024], pst[:, :], AF.Identity),
                            reads=(kt,), writes=(zk,))
            else:
                for sc in range(4):
                    sl = slice(sc * 512, (sc + 1) * 512)
                    for k in range(NKT):
                        p.op("ten", lambda e, k=k, sl=sl, wdst=wdst, pst=pst: e.matmul(
                            pst[:, 0:512], wdst[:, k * 128:(k + 1) * 128], ln[k][:, sl],
                            start=(k == 0), stop=False),
                            reads=(wk, f"ln{k}"), writes=(kt,))
                    p.op("ten", lambda e, sl=sl, pst=pst, wb=wb: e.matmul(
                        pst[:, 0:512], wsq[0:1, wb * 128:(wb + 1) * 128], rows["D"][0:1, sl],
                        start=False, stop=False),
                        reads=(f"wsq{wb}", "rowD"), writes=(kt,))
                    p.op("ten", lambda e, pst=pst, hp=hp: e.matmul(
                        pst[:, 0:512], brow[0:1, 2048 + hp * 128:2048 + (hp + 1) * 128],
                        c.sb["ones_row_f32"][0:1, 0:512], start=False, stop=True),
                        reads=("brow", "c_ones_row_f32"), writes=(kt,))
                    p.op("act", lambda e, sc=sc, pst=pst: e.activation(
                        vfm[:, sc * 512:(sc + 1) * 512], pst[:, :], AF.Identity),
                        reads=(kt,), writes=("vfm",))
        for tt in range(16):
            p.op("ten", lambda e, tt=tt: e.transpose(
                psp[:, 0:64].bitcast(BF16), vfm[:, tt * 128:(tt + 1) * 128],
                c.sb["identity"][:, :]),
                reads=("vfm", "c_identity"), writes=("psp",))
            p.op("dve", lambda e, tt=tt: e.tensor_copy(
                vtok[:, tt * 128:(tt + 1) * 128], psp[:, 0:64].bitcast(BF16)),
                reads=("psp",), writes=(f"vtok{tt}",))

        if DEBUG_TAPS and hp == 0:
            for ki in range(2):
                dd = nc.dram_tensor(f"dbg_ln{ki}", [128, T], BF16, kind="ExternalOutput").ap()
                p.dma("sp", dd, ln[ki][:, :], reads=(f"ln{ki}",), tag="dbg")
            ddr = nc.dram_tensor("dbg_rows", [1, T], F32, kind="ExternalOutput").ap()
            p.dma("sp", ddr, rows["C"][0:1, :], reads=("rowC",), tag="dbg")
            ddr2 = nc.dram_tensor("dbg_nmr", [1, T], F32, kind="ExternalOutput").ap()
            p.dma("sp", ddr2, rows["A"][0:1, :], reads=("rowA",), tag="dbg")
            for nm, buf in (("qzb", qzb), ("kzb", kzb), ("vfm", vfm)):
                dd = nc.dram_tensor("dbg_" + nm, [128, T], BF16, kind="ExternalOutput").ap()
                p.dma("sp", dd, buf[:, :], reads=(nm if nm != "vfm" else "vfm",), tag="dbg")
        if DEBUG_TAPS and hp == 0:
            ddo = nc.dram_tensor("dbg_om", [128, 64], BF16, kind="ExternalOutput").ap()
            p.dma("sp", ddo, om_sb[:, :], reads=("om_sb",), tag="dbg")
        for qk, (zb, zk, fdst, fk) in enumerate(((qzb, "qzb", qf, "qf"),
                                                 (kzb, "kzb", kf, "kf"))):
            p.op("dve", lambda e, zb=zb: e.tensor_mul(z2b[:, :], zb[:, :], zb[:, :]),
                 reads=(zk,), writes=("vfm",))
            for ch2 in range(2):
                csl = slice(ch2 * 1024, (ch2 + 1) * 1024)
                for sc in range(2):
                    ssl = slice(ch2 * 1024 + sc * 512, ch2 * 1024 + (sc + 1) * 512)
                    p.op("ten", lambda e, ssl=ssl: e.matmul(
                        psw[0:2, 0:512], c.sb["blockones2"][:, :], z2b[:, ssl],
                        start=True, stop=True),
                        reads=("vfm", "c_blockones2"), writes=("psw",))
                    p.op("dve", lambda e: e.tensor_copy(
                        s2row[0:2, 0:512], psw[0:2, 0:512]),
                        reads=("psw",), writes=("s2row",))
                    p.op("ten", lambda e: e.matmul(
                        psw[:, 0:512], c.sb["ind2"][:, :], s2row[0:2, 0:512],
                        start=True, stop=True),
                        reads=("s2row", "c_ind2"), writes=("psw",))
                    p.op("dve", lambda e, sc=sc: e.tensor_scalar_max(
                        rnbc[:, sc * 512:(sc + 1) * 512], psw[:, 0:512], 1e-24),
                        reads=("psw",), writes=("rnbc",))
                    p.op("act", lambda e, sc=sc: e.activation(
                        rnbc[:, sc * 512:(sc + 1) * 512], rnbc[:, sc * 512:(sc + 1) * 512],
                        AF.Ln), reads=("rnbc",), writes=("rnbc",))
                    p.op("act", lambda e, sc=sc: e.activation(
                        rnbc[:, sc * 512:(sc + 1) * 512], rnbc[:, sc * 512:(sc + 1) * 512],
                        AF.Exp, scale=-0.5), reads=("rnbc",), writes=("rnbc",))
                    for h in range(2):
                        p.op("ten", lambda e, h=h, ssl=ssl, sc=sc, zb=zb: e.matmul(
                            psp[h * 64:(h + 1) * 64, sc * 512:(sc + 1) * 512],
                            om_sb[h * 64:(h + 1) * 64, :], zb[h * 64:(h + 1) * 64, ssl],
                            start=True, stop=True),
                            reads=("om_sb", zk), writes=("psp",))
                if DEBUG_TAPS and hp == 0 and qk == 0 and ch2 == 0:
                    dd0 = nc.dram_tensor("dbg_praw", [128, 1024], F32, kind="ExternalOutput").ap()
                    p.op("dve", lambda e: e.tensor_copy(tbf[:, :].bitcast(F32)[:, 0:512], psp[:, 0:512]),
                         reads=("psp",), writes=("tbf",))
                    p.dma("sp", dd0[:, 0:512], tbf[:, :].bitcast(F32)[:, 0:512], reads=("tbf",), tag="dbg")
                    p.op("dve", lambda e: e.tensor_copy(tbf[:, :].bitcast(F32)[:, 0:512], psp[:, 512:1024]),
                         reads=("psp",), writes=("tbf",))
                    p.dma("sp", dd0[:, 512:1024], tbf[:, :].bitcast(F32)[:, 0:512], reads=("tbf",), tag="dbg")
                p.op("dve", lambda e: e.tensor_mul(fw1[:, :], psp[:, :], rnbc[:, :]),
                     reads=("psp", "rnbc"), writes=("rbc",))
                if DEBUG_TAPS and hp == 0 and qk == 0 and ch2 == 0:
                    dd1 = nc.dram_tensor("dbg_p", [128, 1024], F32, kind="ExternalOutput").ap()
                    p.dma("sp", dd1, fw1[:, :], reads=("rbc",), tag="dbg")
                    dd2 = nc.dram_tensor("dbg_rn", [128, 1024], F32, kind="ExternalOutput").ap()
                    p.dma("sp", dd2, rnbc[:, :], reads=("rnbc",), tag="dbg")
                p.op("dve", lambda e: e.tensor_scalar(fw2[:, :], fw1[:, :], qcols[:, 0:1],
                                                      qcols[:, 2:3], OP.mult, OP.add),
                     reads=("rbc", "qcols"), writes=("rbc",))
                p.op("dve", lambda e: e.tensor_scalar(fw2[:, :], fw2[:, :], 20.0, -20.0,
                                                      OP.min, OP.max),
                     reads=("rbc",), writes=("rbc",))
                p.op("act", lambda e: e.activation(fw2[:, :], fw2[:, :], AF.Exp,
                                                   bias=qcols[:, 1:2], scale=0.5),
                     reads=("rbc", "qcols"), writes=("rbc",))
                p.op("dve", lambda e: e.tensor_mul(tbf[:, :], fw1[:, :], fw2[:, :]),
                     reads=("rbc",), writes=("tbf",))
                p.op("act", lambda e, fdst=fdst, csl=csl: e.activation(
                    fdst[:, csl], tbf[:, :], AF.Square),
                    reads=("tbf",), writes=(fk,))

        if DEBUG_TAPS and hp == 0:
            for nm, buf in (("qf", qf), ("kf", kf)):
                dd = nc.dram_tensor("dbg_" + nm, [128, T], BF16, kind="ExternalOutput").ap()
                p.dma("sp", dd, buf[:, :], reads=(nm,), tag="dbg")
            dd = nc.dram_tensor("dbg_vtok", [128, 16 * 128], BF16, kind="ExternalOutput").ap()
            p.dma("sp", dd, vtok[:, :], reads=tuple(f"vtok{i}" for i in range(16)), tag="dbg")
        p.op("dve", lambda e: e.memset(KVf[:, :], 0.0), writes=("KVf",))
        p.op("dve", lambda e: e.memset(ksf[:, :], 0.0), writes=("ksf",))
        for g in range(4):
            for ccl in range(4):
                cc = g * 4 + ccl
                sl = slice(cc * 128, (cc + 1) * 128)
                p.op("ten", lambda e, sl=sl: e.transpose(
                    psk[:, 0:64].bitcast(BF16), kf[:, sl], c.sb["identity"][:, :]),
                    reads=("kf", "c_identity"), writes=("psk",))
                p.op("dve", lambda e: e.tensor_copy(kftok[:, :], psk[:, 0:64].bitcast(BF16)),
                     reads=("psk",), writes=("kftok",))
                for h in range(2):
                    hsl = slice(h * 64, (h + 1) * 64)
                    p.op("ten", lambda e, h=h, sl=sl, hsl=hsl: e.matmul(
                        psp[:, h * 512:h * 512 + 128], kf[hsl, sl], qf[hsl, sl],
                        start=True, stop=True),
                        reads=("kf", "qf"), writes=("psp",))
                    p.op("dve", lambda e, h=h: e.tensor_mul(
                        smb[h][:, :], psp[:, h * 512:h * 512 + 128], c.sb["trimask"][:, :]),
                        reads=("psp", "c_trimask"), writes=(f"smb{h}",))
                    dpart = slice(0, 1) if h == 0 else slice(32, 33)
                    p.op("ten", lambda e, h=h, ccl=ccl, dpart=dpart: e.matmul(
                        psw[dpart, ccl * 128:(ccl + 1) * 128], c.sb["ones_col_bf"][:, :],
                        smb[h][:, :], start=True, stop=(cc % 16 == 0)),
                        reads=(f"smb{h}", "c_ones_col_bf"), writes=("psw",))
                    if cc % 16 != 0:
                        p.op("ten", lambda e, h=h, sl=sl, ccl=ccl, hsl=hsl, dpart=dpart: e.matmul(
                            psw[dpart, ccl * 128:(ccl + 1) * 128], kssb[hsl, 0:1],
                            qf[hsl, sl], start=False, stop=True),
                            reads=("kssb", "qf"), writes=("psw",))
                    if cc % 16 != 0:
                        p.op("ten", lambda e, h=h, sl=sl, hsl=hsl: e.matmul(
                            psq[hsl, 0:128], KVsb[hsl, :], qf[hsl, sl],
                            start=True, stop=False),
                            reads=("KVsb", "qf"), writes=("psq",))
                    p.op("ten", lambda e, h=h, cc=cc, hsl=hsl: e.matmul(
                        psq[hsl, 0:128], vtok[:, cc % 16 * 128 + h * 64:cc % 16 * 128 + (h + 1) * 64],
                        smb[h][:, :], start=(cc % 16 == 0), stop=True),
                        reads=(f"vtok{cc % 16}", f"smb{h}"), writes=("psq",))
                    p.op("ten", lambda e, h=h, cc=cc, hsl=hsl: e.matmul(
                        psv[hsl, 0:64], kftok[:, h * 64:(h + 1) * 64],
                        vtok[:, cc % 16 * 128 + h * 64:cc % 16 * 128 + (h + 1) * 64],
                        start=True, stop=False),
                        reads=("kftok", f"vtok{cc % 16}"), writes=("psv",))
                    p.op("ten", lambda e, h=h, cc=cc, hsl=hsl: e.matmul(
                        psv[hsl, 64:65], kftok[:, h * 64:(h + 1) * 64],
                        c.sb["ones_col_bf"][:, :], start=False,
                        stop=True),
                        reads=("kftok", "c_ones_col_bf"), writes=("psv",))
                p.op("dve", lambda e: e.tensor_add(KVf[:, :], KVf[:, :], psv[:, 0:64]),
                     reads=("psv", "KVf"), writes=("KVf",))
                p.op("dve", lambda e: e.tensor_add(ksf[:, :], ksf[:, :], psv[:, 64:65]),
                     reads=("psv", "ksf"), writes=("ksf",))
                p.op("dve", lambda e: e.tensor_copy(KVsb[:, :], KVf[:, :]),
                     reads=("KVf",), writes=("KVsb",))
                p.op("dve", lambda e: e.tensor_copy(kssb[:, :], ksf[:, :]),
                     reads=("ksf",), writes=("kssb",))
                if DEBUG_TAPS and hp == 0 and cc == 0:
                    ddk = nc.dram_tensor("dbg_KVf", [128, 64], F32, kind="ExternalOutput").ap()
                    p.dma("sp", ddk, KVf[:, :], reads=("KVf",), tag="dbg")
                    ddk2 = nc.dram_tensor("dbg_ksf", [128, 1], F32, kind="ExternalOutput").ap()
                    p.dma("sp", ddk2, ksf[:, :], reads=("ksf",), tag="dbg")
                    ddk3 = nc.dram_tensor("dbg_kftok", [128, 128], BF16, kind="ExternalOutput").ap()
                    p.dma("sp", ddk3, kftok[:, :], reads=("kftok",), tag="dbg")
                p.op("dve", lambda e, ccl=ccl: e.tensor_copy(
                    ctxsb[:, ccl * 128:(ccl + 1) * 128], psq[:, 0:128]),
                    reads=("psq",), writes=("ctxsb",))
            for h in range(2):
                dpart = slice(0, 1) if h == 0 else slice(32, 33)
                p.op("dve", lambda e, h=h, dpart=dpart: e.tensor_scalar_max(
                    denr[0:1, 0:512], psw[dpart, 0:512], 1e-6),
                    reads=("psw",), writes=("denr",))
                p.op("dve", lambda e: e.reciprocal(
                    denr[0:1, 0:512], denr[0:1, 0:512]),
                    reads=("denr",), writes=("denr",))
                p.op("ten", lambda e, h=h: e.matmul(
                    psk[h * 64:(h + 1) * 64, 0:512], c.sb["ones_row_f32"][0:1, 0:64],
                    denr[0:1, 0:512], start=True, stop=True),
                    reads=("denr", "c_ones_row_f32"), writes=("psk",))
            p.op("dve", lambda e, g=g, hp=hp: e.tensor_mul(
                apre[hp][:, g * 512:(g + 1) * 512], ctxsb[:, :], psk[:, 0:512]),
                reads=("ctxsb", "psk"), writes=(f"apre{hp}",))

    if DEBUG_TAPS:
        for i in range(8):
            dd = nc.dram_tensor(f"dbg_apre{i}", [128, T], BF16, kind="ExternalOutput").ap()
            p.dma("sp", dd, apre[i][:, :], reads=(f"apre{i}",), tag="dbg")
    for odp in range(4):
        xr = xrot[odp % 2]
        xk = f"xrot{odp % 2}"
        p.dma("sp", xr.rearrange("p (k t) -> p k t", k=2),
              xin_r[:, 2 * odp:2 * odp + 2, 0:T], writes=(xk,), tag="xres")
        for j in range(2):
            od = 2 * odp + j
            wdst = wocol[:, 0:1024]
            wk = "wocol"
            p.dma("pool", wdst.rearrange("p (k o) -> p k o", k=8),
                  c.d["w_out_p"][od], writes=(wk,), tag="wout")
            for tch in range(4):
                sl = slice(tch * 512, (tch + 1) * 512)
                pst = psq if tch % 2 == 0 else psk
                kt = "psq" if tch % 2 == 0 else "psk"
                for k in range(NKT):
                    p.op("ten", lambda e, k=k, sl=sl, wdst=wdst, pst=pst: e.matmul(
                        pst[:, 0:512], wdst[:, k * 128:(k + 1) * 128], apre[k][:, sl],
                        start=(k == 0), stop=False),
                        reads=(wk, f"apre{k}"), writes=(kt,))
                p.op("ten", lambda e, od=od, pst=pst: e.matmul(
                    pst[:, 0:512], borow[0:1, od * 128:(od + 1) * 128],
                    c.sb["ones_row_f32"][0:1, 0:512], start=False, stop=True),
                    reads=("borow", "c_ones_row_f32"), writes=(kt,))
                p.op("dve", lambda e, j=j, sl=sl, xr=xr, pst=pst: e.scalar_tensor_tensor(
                    xr[:, j * T + sl.start:j * T + sl.stop], pst[:, 0:512], 1.0,
                    xr[:, j * T + sl.start:j * T + sl.stop], OP.mult, OP.add),
                    reads=(kt, xk), writes=(xk,))
        p.dma("sp", xout_r[:, 2 * odp:2 * odp + 2, 0:T],
              xr.rearrange("p (k t) -> p k t", k=2), reads=(xk,), tag="xstore")


def ln_stream(c, p, x_in_r, tsl, TC, xrot, ln, rows, rbc, xbf2):
    nc = c.nc
    psA_cm = nc.psum_tensor("lnpsA", [128, 2048], F32)
    psA = psA_cm.__enter__()
    psB_cm = nc.psum_tensor("lnpsB", [128, 2048], F32)
    psB = psB_cm.__enter__()
    pA, pB = "psA", "psB"
    for kk in range(4):
        xr = xrot[kk % 2]
        xk = f"xrot{kk % 2}"
        p.dma("sp", xr.rearrange("p (k t) -> p k t", k=2),
              x_in_r[:, 2 * kk:2 * kk + 2, tsl], writes=(xk,), tag="lnx")
        for j in range(2):
            k = 2 * kk + j
            xb = xbf2[:, j * TC:(j + 1) * TC]
            xbk = f"xbf{j}"
            p.op("dve", lambda e, xr=xr, j=j, xb=xb: e.tensor_copy(
                xb, xr[:, j * TC:(j + 1) * TC]), reads=(xk,), writes=(xbk,))
            nch = TC // 512
            for sc in range(nch):
                sl = slice(sc * 512, (sc + 1) * 512)
                bsl = sl
                p.op("ten", lambda e, xb=xb, sl=sl, bsl=bsl, k=k: e.matmul(
                    psA[0:1, bsl], c.sb["ones_col_bf"][:, :], xb[:, sl],
                    start=(k == 0), stop=(k == NKT - 1)),
                    reads=(xbk, "c_ones_col_bf"), writes=(pA,))
            x2 = xbf2[:, (1 - j) * TC:(2 - j) * TC]
            x2k = f"xbf{1 - j}"
            p.op("dve", lambda e, xb=xb, x2=x2: e.tensor_mul(x2, xb, xb),
                 reads=(xbk,), writes=(x2k,))
            for sc in range(nch):
                sl = slice(sc * 512, (sc + 1) * 512)
                bsl = sl
                p.op("ten", lambda e, x2=x2, sl=sl, bsl=bsl, k=k: e.matmul(
                    psB[0:1, bsl], c.sb["ones_col_bf"][:, :], x2[:, sl],
                    start=(k == 0), stop=(k == NKT - 1)),
                    reads=(x2k, "c_ones_col_bf"), writes=(pB,))
    rowA, rowB, rowC, rowD = rows["A"], rows["B"], rows["C"], rows["D"]
    for sc in range(TC // 512):
        sl = slice(sc * 512, (sc + 1) * 512)
        bsl = sl
        p.op("dve", lambda e, sl=sl, bsl=bsl: e.tensor_copy(rowA[0:1, sl], psA[0:1, bsl]),
             reads=(pA,), writes=("rowA",))
        p.op("dve", lambda e, sl=sl, bsl=bsl: e.tensor_copy(rowB[0:1, sl], psB[0:1, bsl]),
             reads=(pB,), writes=("rowB",))
    p.op("dve", lambda e: e.tensor_scalar_mul(rowA[0:1, :TC], rowA[0:1, :TC], 1.0 / D),
         reads=("rowA",), writes=("rowA",))
    p.op("dve", lambda e: e.scalar_tensor_tensor(rowC[0:1, :TC], rowA[0:1, :TC], -1.0,
                                                 rowA[0:1, :TC], OP.mult, OP.mult),
         reads=("rowA",), writes=("rowC",))
    p.op("dve", lambda e: e.scalar_tensor_tensor(rowB[0:1, :TC], rowB[0:1, :TC], 1.0 / D,
                                                 rowC[0:1, :TC], OP.mult, OP.add),
         reads=("rowB", "rowC"), writes=("rowB",))
    p.op("act", lambda e: e.activation(rowC[0:1, :TC], rowB[0:1, :TC], AF.Ln,
                                       bias=c.sb["eps_col"][0:1, 0:1], scale=1.0),
         reads=("rowB", "c_eps_col"), writes=("rowC",))
    p.op("act", lambda e: e.activation(rowC[0:1, :TC], rowC[0:1, :TC], AF.Exp, scale=-0.5),
         reads=("rowC",), writes=("rowC",))
    p.op("dve", lambda e: e.scalar_tensor_tensor(rowA[0:1, :TC], rowA[0:1, :TC], -1.0,
                                                 rowC[0:1, :TC], OP.mult, OP.mult),
         reads=("rowA", "rowC"), writes=("rowA",))
    p.op("dve", lambda e: e.tensor_copy(rowD[0:1, :TC], rowA[0:1, :TC]),
         reads=("rowA",), writes=("rowD",))
    for sc in range(TC // 512):
        sl = slice(sc * 512, (sc + 1) * 512)
        bsl = sl
        p.op("ten", lambda e, sl=sl, bsl=bsl: e.matmul(
            psA[:, sc * 512:(sc + 1) * 512][:, 0:512] if False else psA[:, bsl],
            c.sb["ones_row_f32"][0:1, 0:128], rowC[0:1, sl],
            start=True, stop=True),
            reads=("rowC", "c_ones_row_f32"), writes=(pA,))
        p.op("dve", lambda e, sl=sl, bsl=bsl: e.tensor_copy(rbc[:, sl], psA[:, bsl]),
             reads=(pA,), writes=("rbc",))
    for kk in range(4):
        xr = xrot[kk % 2]
        xk = f"xrot{kk % 2}"
        p.dma("sp", xr.rearrange("p (k t) -> p k t", k=2),
              x_in_r[:, 2 * kk:2 * kk + 2, tsl], writes=(xk,), tag="lnx2")
        for j in range(2):
            k = 2 * kk + j
            p.op("dve", lambda e, xr=xr, j=j, k=k: e.tensor_mul(
                ln[k][:, :TC], xr[:, j * TC:(j + 1) * TC], rbc[:, :TC]),
                reads=(xk, "rbc"), writes=(f"ln{k}",))
    psB_cm.__exit__(None, None, None)
    psA_cm.__exit__(None, None, None)


def emit_l1_ffn(c, x_in, x_out):
    nc, p = c.nc, c.p
    TC = T
    sb_cms = []

    def alloc(name, shape, dt):
        cm = nc.sbuf_tensor(name, shape, dt)
        sb_cms.append(cm)
        return cm.__enter__()

    xrot = [alloc(f"xrot{i}", [128, 2 * TC], F32) for i in range(2)]
    xbf2 = alloc("xbf2", [128, 2 * TC], BF16)
    ln = [alloc(f"lnt{k}", [128, TC], BF16) for k in range(NKT)]
    rows = {"A": alloc("rowA", [1, TC], F32), "B": alloc("rowB", [1, TC], F32),
            "C": alloc("rowC", [1, TC], F32), "D": alloc("rowD", [1, TC], BF16)}
    rbc = alloc("rbc", [128, TC], F32)
    gA = alloc("gA", [128, 512], F32)
    gB = alloc("gB", [128, 512], F32)
    gT = alloc("gT", [128, 512], F32)
    Wf = [alloc(f"Wf{rt}", [128, 4 * 512], F32) for rt in range(2)]
    WB = [alloc(f"WB{i}", [128, 4 * 512], BF16) for i in range(3)]
    wsx = alloc("wsx", [1, 1024], BF16)
    bias_sb = alloc("bias_sb", [1, 1024], F32)
    p.dma("sp", bias_sb[0:1, :], c.d["bias_ri"], writes=("bias_sb",))
    xin_r = x_in.rearrange("(k p) t -> p k t", p=128)
    xout_r = x_out.rearrange("(k p) t -> p k t", p=128)

    ln_stream(c, p, xin_r, slice(0, T), TC, xrot, ln, rows, rbc, xbf2)
    ps_cms = []
    pso = []
    for i in range(2):
        cm = nc.psum_tensor(f"pso{i}", [128, 512], F32)
        ps_cms.append(cm)
        pso.append(cm.__enter__())
    psB_cm2 = nc.psum_tensor("psB2", [128, 512], F32)
    ps_cms.append(psB_cm2)
    psB = psB_cm2.__enter__()
    sb_cms.extend(ps_cms)

    for rt in range(2):
        g1 = c.d["G1R_x"] if rt == 0 else c.d["G1I_x"]
        g2 = c.d["G2R_x"] if rt == 0 else c.d["G2I_x"]
        for I in range(4):
            acc = Wf[rt][:, I * 512:(I + 1) * 512]
            ak = f"Wf{rt}_{I}"
            for ci in range(4):
                p.dma("sp", gA[:, :], g1[ci, I * 128:(I + 1) * 128, :], writes=("gA",))
                p.dma("sp", gB[:, :], g2[ci, I * 128:(I + 1) * 128, :], writes=("gB",))
                if ci == 0:
                    p.op("dve", lambda e, acc=acc: e.tensor_mul(acc, gA[:, :], gB[:, :]),
                         reads=("gA", "gB"), writes=(ak,))
                else:
                    p.op("dve", lambda e: e.tensor_mul(gT[:, :], gA[:, :], gB[:, :]),
                         reads=("gA", "gB"), writes=("gT",))
                    p.op("dve", lambda e, acc=acc: e.tensor_add(acc, acc, gT[:, :]),
                         reads=("gT", ak), writes=(ak,))
            p.op("dve", lambda e, acc=acc, rt=rt, I=I: e.tensor_copy(
                WB[rt][:, I * 512:(I + 1) * 512], acc),
                reads=(ak,), writes=(f"WB{rt}_{I}",))
            if rt == 1:
                p.op("dve", lambda e, acc=acc, I=I: e.tensor_scalar_mul(
                    WB[2][:, I * 512:(I + 1) * 512], acc, -1.0),
                    reads=(ak,), writes=(f"WB2_{I}",))

    for half in range(2):
        Wa = WB[0] if half == 0 else WB[1]
        Wak = "WB0" if half == 0 else "WB1"
        Wb = WB[2] if half == 0 else WB[0]
        Wbk = "WB2" if half == 0 else "WB0"
        for O in range(4):
            for I in range(4):
                p.op("ten", lambda e, Wa=Wa, I=I, O=O: e.matmul(
                    psB[0:1, 0:128], c.sb["ones_col_bf"][:, :],
                    Wa[:, I * 512 + O * 128:I * 512 + (O + 1) * 128],
                    start=(I == 0), stop=False),
                    reads=(f"{Wak}_{I}", "c_ones_col_bf"), writes=("psB",))
            for I in range(4):
                p.op("ten", lambda e, Wb=Wb, I=I, O=O: e.matmul(
                    psB[0:1, 0:128], c.sb["ones_col_bf"][:, :],
                    Wb[:, I * 512 + O * 128:I * 512 + (O + 1) * 128],
                    start=False, stop=(I == 3)),
                    reads=(f"{Wbk}_{I}", "c_ones_col_bf"), writes=("psB",))
            p.op("dve", lambda e, half=half, O=O: e.tensor_copy(
                wsx[0:1, half * 512 + O * 128:half * 512 + (O + 1) * 128], psB[0:1, 0:128]),
                reads=("psB",), writes=("wsx",))

    for odp in range(4):
        xr = xrot[odp % 2]
        xk = f"xrot{odp % 2}"
        p.dma("sp", xr.rearrange("p (k t) -> p k t", k=2),
              xin_r[:, 2 * odp:2 * odp + 2, 0:T], writes=(xk,), tag="xres")
        for j in range(2):
            od = 2 * odp + j
            half = od // 4
            O = od % 4
            Wa = WB[0] if half == 0 else WB[1]
            Wak = "WB0" if half == 0 else "WB1"
            Wb = WB[2] if half == 0 else WB[0]
            Wbk = "WB2" if half == 0 else "WB0"
            for tch in range(4):
                sl = slice(tch * 512, (tch + 1) * 512)
                pst = pso[tch % 2]
                kt = f"pso{tch % 2}"
                for I in range(4):
                    p.op("ten", lambda e, Wa=Wa, I=I, O=O, sl=sl, pst=pst: e.matmul(
                        pst[:, 0:512], Wa[:, I * 512 + O * 128:I * 512 + (O + 1) * 128],
                        ln[I][:, sl], start=(I == 0), stop=False),
                        reads=(f"{Wak}_{I}", f"ln{I}"), writes=(kt,))
                for I in range(4):
                    p.op("ten", lambda e, Wb=Wb, I=I, O=O, sl=sl, pst=pst: e.matmul(
                        pst[:, 0:512], Wb[:, I * 512 + O * 128:I * 512 + (O + 1) * 128],
                        ln[4 + I][:, sl], start=False, stop=False),
                        reads=(f"{Wbk}_{I}", f"ln{4 + I}"), writes=(kt,))
                p.op("ten", lambda e, od=od, sl=sl, pst=pst: e.matmul(
                    pst[:, 0:512], wsx[0:1, od * 128:(od + 1) * 128], rows["D"][0:1, sl],
                    start=False, stop=False),
                    reads=("wsx", "rowD"), writes=(kt,))
                p.op("ten", lambda e, od=od, pst=pst: e.matmul(
                    pst[:, 0:512], bias_sb[0:1, od * 128:(od + 1) * 128],
                    c.sb["ones_row_f32"][0:1, 0:512], start=False, stop=True),
                    reads=("bias_sb", "c_ones_row_f32"), writes=(kt,))
                p.op("dve", lambda e, j=j, sl=sl, xr=xr, pst=pst: e.scalar_tensor_tensor(
                    xr[:, j * TC + sl.start:j * TC + sl.stop], pst[:, 0:512], 0.5,
                    xr[:, j * TC + sl.start:j * TC + sl.stop], OP.mult, OP.add),
                    reads=(kt, xk), writes=(xk,))
        p.dma("sp", xout_r[:, 2 * odp:2 * odp + 2, 0:T],
              xr.rearrange("p (k t) -> p k t", k=2), reads=(xk,), tag="xstore")


def emit_l1_attn(c, x_in, x_out):
    nc, p = c.nc, c.p
    sb_cms = []

    def alloc(name, shape, dt):
        cm = nc.sbuf_tensor(name, shape, dt)
        sb_cms.append(cm)
        return cm.__enter__()

    abf = alloc("abf", [128, T], BF16)
    ktl = alloc("ktl", [128, T], BF16)
    xrot = [alloc(f"xrot{i}", [128, 2 * T], F32) for i in range(2)]
    xbf2 = alloc("xbf2", [128, 2 * T], BF16)
    ln = [alloc(f"lnt{k}", [128, T], BF16) for k in range(NKT)]
    rows = {"A": alloc("rowA", [1, T], F32), "B": alloc("rowB", [1, T], F32),
            "C": alloc("rowC", [1, T], F32), "D": alloc("rowD", [1, T], BF16)}
    rbc = alloc("rbc", [128, T], F32)
    hpre = [alloc(f"hpre{i}", [128, T], BF16) for i in range(8)]
    ksb = alloc("ksb", [128, T], F32)
    usb = alloc("usb", [128, T], F32)
    qsb = alloc("qsb", [128, T], F32)
    Lsb = alloc("Lsb", [128, T + 16], F32)
    ktl2 = ksb[:, :].bitcast(BF16)[:, 0:T]
    a2bf = ksb[:, :].bitcast(BF16)[:, T:2 * T]
    duqb = Lsb[:, :].bitcast(BF16)[:, 0:T]
    bbf = alloc("bbf", [128, T], BF16)
    vfm = alloc("vfm", [128, T], BF16)
    dec = alloc("dec", [128, 128], F32)
    vBDp = alloc("vBDp", [128, 128], BF16)
    bBDp = alloc("bBDp", [128, 128], BF16)
    smp = alloc("smp", [128, 16], BF16)
    KVbd = alloc("KVbd", [128, 128], F32)
    KVbds = alloc("KVbds", [128, 128], BF16)
    wqcol = alloc("wqcol", [128, 2 * 1024], BF16)
    wsq = alloc("wsq", [1, 256], BF16)
    wocol = alloc("wocol", [128, 1024], BF16)

    def palloc(name, shape, dt=F32):
        cm = nc.psum_tensor(name, shape, dt)
        sb_cms.append(cm)
        return cm.__enter__()

    xin_r = x_in.rearrange("(k p) t -> p k t", p=128)
    xout_r = x_out.rearrange("(k p) t -> p k t", p=128)

    fw1 = rbc
    af32 = xbf2[:, :].bitcast(F32)
    brow = rows["B"][0:1, 0:D]
    borow = rows["B"][0:1, D:2 * D]

    for i in range(8):
        p.op("dve", lambda e, i=i: e.memset(hpre[i][:, :], 0.0), writes=(f"hpre{i}",))
    p.op("dve", lambda e: e.memset(vfm[:, :], 0.0), writes=("vfm",))
    p.op("dve", lambda e: e.memset(abf[:, :], 0.0), writes=("abf",))
    p.op("dve", lambda e: e.memset(ktl[:, :], 0.0), writes=("ktl",))
    p.op("dve", lambda e: e.memset(bbf[:, :], 0.0), writes=("bbf",))

    ln_stream(c, p, xin_r, slice(0, T), T, xrot, ln, rows, rbc, xbf2)
    p.dma("sp", brow, c.d["key_b"], reads=("rowB",), writes=("brow",))
    p.dma("sp", borow, c.d["out_b"], reads=("rowB",), writes=("borow",))

    pA = palloc("pA", [128, 1024])
    pB = palloc("pB", [128, 1024])
    pC = palloc("pC", [128, 512])
    pD = palloc("pD", [128, 512])
    pE = palloc("pE", [128, 512])

    for hp in range(8):
        _phA = [] if L1_COPS in (2.95,) else list(enumerate(
                (("key_w_p", hp, pA, "pA", 0),
                 ("vq_w_p", 8 + hp, pB, "pB", None),
                 ("vq_w_p", hp, pC, "pC", None))))
        for w_i, (wtens, wname, pst, kt, boff) in _phA:
            wdst = wqcol[:, (w_i % 2) * 1024:(w_i % 2 + 1) * 1024]
            wk = f"wqcol{w_i % 2}"
            p.dma("pool", wdst.rearrange("p (k o) -> p k o", k=8),
                  c.d[wtens][wname], writes=(wk,), tag="wl1")
            wb = w_i % 2
            for k in range(NKT):
                p.op("ten", lambda e, k=k, wdst=wdst: e.matmul(
                    pD[0:1, 0:128], c.sb["ones_col_bf"][:, :], wdst[:, k * 128:(k + 1) * 128],
                    start=(k == 0), stop=(k == NKT - 1)),
                    reads=(wk, "c_ones_col_bf"), writes=("pD",))
            p.op("dve", lambda e, wb=wb: e.tensor_copy(wsq[0:1, wb * 128:(wb + 1) * 128],
                                                       pD[0:1, 0:128]),
                 reads=("pD",), writes=(f"wsq{wb}",))
            for sc in range(4):
                sl = slice(sc * 512, (sc + 1) * 512)
                psl = slice((sc % 2) * 512, (sc % 2 + 1) * 512) if w_i < 2 else slice(0, 512)
                for k in range(NKT):
                    p.op("ten", lambda e, k=k, sl=sl, psl=psl, wdst=wdst, pst=pst: e.matmul(
                        pst[:, psl], wdst[:, k * 128:(k + 1) * 128], ln[k][:, sl],
                        start=(k == 0), stop=False),
                        reads=(wk, f"ln{k}"), writes=(kt,))
                p.op("ten", lambda e, sl=sl, psl=psl, pst=pst, wb=wb, boff=boff: e.matmul(
                    pst[:, psl], wsq[0:1, wb * 128:(wb + 1) * 128], rows["D"][0:1, sl],
                    start=False, stop=(boff is None)),
                    reads=(f"wsq{wb}", "rowD"), writes=(kt,))
                if boff is not None:
                    p.op("ten", lambda e, sl=sl, psl=psl, pst=pst, boff=boff, hp=hp: e.matmul(
                        pst[:, psl], brow[0:1, hp * 128:(hp + 1) * 128],
                        c.sb["ones_row_f32"][0:1, 0:512], start=False, stop=True),
                        reads=("brow", "c_ones_row_f32"), writes=(kt,))
                if w_i == 0 and sc % 2 == 1:
                    ch2 = sc // 2
                    p.op("act", lambda e, ch2=ch2, pst=pst: e.activation(
                        ksb[:, ch2 * 1024:(ch2 + 1) * 1024], pst[:, :], AF.Sigmoid),
                        reads=(kt,), writes=("ksb",))
                elif w_i == 1 and sc % 2 == 1:
                    ch2 = sc // 2
                    p.op("act", lambda e, ch2=ch2, pst=pst: e.activation(
                        qsb[:, ch2 * 1024:(ch2 + 1) * 1024], pst[:, :], AF.Identity),
                        reads=(kt,), writes=("qsb",))
                elif w_i == 2:
                    p.op("act", lambda e, sc=sc, pst=pst: e.activation(
                        vfm[:, sc * 512:(sc + 1) * 512], pst[:, :], AF.Identity),
                        reads=(kt,), writes=("vfm",))

        if L1_PHASES < 2:
            continue
        if L1_COPS in (2.9, 2.95):
            p.op("dve", lambda e: e.tensor_copy(ktl[:, :], ln[0][:, :]),
                 reads=("ln0",), writes=("ktl",))
            p.op("dve", lambda e: e.tensor_copy(abf[:, :], ln[1][:, :]),
                 reads=("ln1",), writes=("abf",))
            p.op("dve", lambda e: e.tensor_copy(bbf[:, :], ln[2][:, :]),
                 reads=("ln2",), writes=("bbf",))
            p.op("dve", lambda e: e.memset(dec[:, :], 1.0), writes=("dec",))
            p.op("dve", lambda e: e.memset(KVf[:, :], 0.0), writes=("KVf",))
            p.op("dve", lambda e: e.memset(KVsb[:, :], 0.0), writes=("KVsb",))
            p.op("dve", lambda e: e.memset(vtok16[:, :], 0.0), writes=("vtok16",))
            p.op("dve", lambda e: e.memset(btok16[:, :], 0.0), writes=("btok16",))
            p.op("dve", lambda e: e.memset(smbf[:, :], 0.0), writes=("smbf",))
            p.op("dve", lambda e: e.memset(qsb[:, :], 0.0), writes=("qsb",))
        else:
            if True:
                pass
        if L1_COPS not in (2.9, 2.95):
            p.op("dve", lambda e: e.tensor_scalar(usb[:, :], ksb[:, :], -1.0, 1.0,
                                                  OP.mult, OP.add),
                 reads=("ksb",), writes=("usb",))
            p.op("dve", lambda e: e.tensor_mul(abf[:, :], usb[:, :], qsb[:, :]),
                 reads=("usb", "qsb"), writes=("abf",))
            _duq_tiles = ((pB, "pB", 0), (pB, "pB", 512), (pC, "pC", 0), (pD, "pD", 0))
            for i, (bank, bkey, off) in enumerate(_duq_tiles):
                p.op("ten", lambda e, bank=bank, off=off, i=i: e.matmul(
                    bank[:, off:off + 512], c.sb["blockM"][:, :],
                    abf[:, i * 512:(i + 1) * 512], start=True, stop=True),
                    reads=("c_blockM", "abf"), writes=(bkey,))
            p.op("dve", lambda e: e.tensor_scalar_max(ksb[:, :], ksb[:, :], 1e-38),
                 reads=("ksb",), writes=("ksb",))
            p.op("act", lambda e: e.activation(ksb[:, :], ksb[:, :], AF.Ln),
                 reads=("ksb",), writes=("ksb",))
            p.op("dve", lambda e: e.memset(fw1[:, :], 0.0), writes=("rbc",))
            p.op("dve", lambda e: e.memset(Lsb[:, 0:16], 0.0), writes=("Lsb",))
            p.op("dve", lambda e: e.tensor_tensor_scan(Lsb[:, 1:T + 1], ksb[:, :], fw1[:, :],
                                                       0.0, OP.add, OP.add),
                 reads=("ksb", "rbc"), writes=("Lsb",))
            pivA = Lsb[:, 0:T:16].unsqueeze(2).broadcast_to([128, T // 16, 16])
            pivB = Lsb[:, 16:T + 16:16].unsqueeze(2).broadcast_to([128, T // 16, 16])
            Lv = Lsb[:, 1:T + 1].rearrange("p (a b) -> p a b", b=16)
            p.op("dve", lambda e: e.tensor_tensor(fw1[:, :].rearrange("p (a b) -> p a b", b=16),
                                                  Lv, pivA, OP.subtract),
                 reads=("Lsb",), writes=("rbc",))
            p.op("dve", lambda e: e.tensor_scalar(af32[:, :], fw1[:, :], 43.0, 0.0,
                                                  OP.add, OP.min),
                 reads=("rbc",), writes=("afx", "xbf0", "xbf1"))
            p.op("act", lambda e: e.activation(af32[:, :], af32[:, :], AF.Exp),
                 reads=("afx",), writes=("afx",))
            p.op("dve", lambda e: e.tensor_mul(a2bf[:, :], qsb[:, :], af32[:, :]),
                 reads=("qsb", "afx"), writes=("ksb",))
            p.op("act", lambda e: e.activation(af32[:, :], fw1[:, :], AF.Exp),
                 reads=("rbc",), writes=("afx",))
            p.op("dve", lambda e: e.tensor_mul(qsb[:, :], qsb[:, :], af32[:, :]),
                 reads=("qsb", "afx"), writes=("qsb",))
            p.op("dve", lambda e: e.tensor_copy(abf[:, :], qsb[:, :]),
                 reads=("qsb",), writes=("abf",))
            p.op("dve", lambda e: e.tensor_scalar(af32[:, :], fw1[:, :], -1.0, -43.0,
                                                  OP.mult, OP.add),
                 reads=("rbc",), writes=("afx",))
            p.op("dve", lambda e: e.tensor_scalar_min(fw1[:, :], af32[:, :], 43.0),
                 reads=("afx",), writes=("rbc",))
            p.op("act", lambda e: e.activation(fw1[:, :], fw1[:, :], AF.Exp),
                 reads=("rbc",), writes=("rbc",))
            p.op("dve", lambda e: e.tensor_scalar_min(af32[:, :], af32[:, :], 0.0),
                 reads=("afx",), writes=("afx",))
            p.op("act", lambda e: e.activation(af32[:, :], af32[:, :], AF.Exp),
                 reads=("afx",), writes=("afx",))
            p.op("dve", lambda e: e.scalar_tensor_tensor(
                ktl[:, :], af32[:, :], 4.727392e18, usb[:, :], OP.mult, OP.mult),
                reads=("afx", "usb"), writes=("ktl",))
            p.op("dve", lambda e: e.tensor_sub(fw1[:, :], fw1[:, :], af32[:, :]),
                 reads=("rbc", "afx"), writes=("rbc",))
            p.op("dve", lambda e: e.tensor_mul(ktl2[:, :], usb[:, :], fw1[:, :]),
                 reads=("usb", "rbc"), writes=("ksb",))
            p.op("dve", lambda e: e.tensor_tensor(fw1[:, :].rearrange("p (a b) -> p a b", b=16),
                                                  pivB, Lv, OP.subtract),
                 reads=("Lsb",), writes=("rbc",))
            p.op("act", lambda e: e.activation(fw1[:, :], fw1[:, :], AF.Exp),
                 reads=("rbc",), writes=("rbc",))
            p.op("dve", lambda e: e.tensor_mul(bbf[:, :], usb[:, :], fw1[:, :]),
                 reads=("usb", "rbc"), writes=("bbf",))
            p.op("dve", lambda e: e.tensor_tensor(dec[:, :], Lsb[:, 16:T + 16:16],
                                                  Lsb[:, 0:T:16], OP.subtract),
                 reads=("Lsb",), writes=("dec",))
            p.op("act", lambda e: e.activation(dec[:, :], dec[:, :], AF.Exp),
                 reads=("dec",), writes=("dec",))
            for i, (bank, bkey, off) in enumerate(_duq_tiles):
                p.op("dve", lambda e, bank=bank, off=off, i=i: e.tensor_copy(
                    duqb[:, i * 512:(i + 1) * 512], bank[:, off:off + 512]),
                    reads=(bkey,), writes=("Lsb",))

            if DEBUG_TAPS and hp == 0:
                for nm, buf in (("ktl", ktl), ("abf", abf)):
                    dd = nc.dram_tensor("dbg_" + nm, [128, T], BF16, kind="ExternalOutput").ap()
                    p.dma("sp", dd, buf[:, :], reads=(nm,), tag="dbg")

        if L1_PHASES < 3 or hp >= L1_NHP:
            continue
        p.op("dve", lambda e: e.memset(vBDp[:, :], 0.0), writes=("vBDp",))
        p.op("dve", lambda e: e.memset(bBDp[:, :], 0.0), writes=("bBDp",))
        p.op("dve", lambda e: e.memset(smp[:, :], 0.0), writes=("smp",))
        p.op("dve", lambda e: e.memset(KVbd[:, :], 0.0), writes=("KVbd",))
        p.op("dve", lambda e: e.memset(KVbds[:, :], 0.0), writes=("KVbds",))
        B0 = xrot[0][:, :].bitcast(BF16)
        B1 = xrot[1][:, :].bitcast(BF16)
        p.op("dve", lambda e: e.memset(B0[:, :], 0.0), writes=("xrot0",))
        p.op("dve", lambda e: e.memset(B1[:, :], 0.0), writes=("xrot1",))
        for (srcb, srck, pr, dst, lo) in (
                (ktl, "ktl", slice(0, 64), B0[0:64, 0:4096], 0),
                (ktl, "ktl", slice(64, 128), B0[64:128, 0:4096], 16),
                (ktl2, "ksb", slice(0, 64), B0[0:64, 4096:8192], 0),
                (ktl2, "ksb", slice(64, 128), B0[64:128, 4096:8192], 16)):
            p.op("dve", lambda e, srcb=srcb, pr=pr, dst=dst, lo=lo: e.tensor_copy(
                dst.rearrange("p (c u) -> p c u", u=32)[:, :, lo:lo + 16],
                srcb[pr, :].rearrange("p (c s) -> p c s", s=16)),
                reads=(srck,), writes=("xrot0",))
        for (srcb, srck, pr, lo) in ((vfm, "vfm", slice(0, 64), 0),
                                     (vfm, "vfm", slice(64, 128), 16),
                                     (bbf, "bbf", slice(0, 64), 32),
                                     (bbf, "bbf", slice(64, 128), 48)):
            p.op("dve", lambda e, srcb=srcb, pr=pr, lo=lo: e.tensor_copy(
                B1[pr, :].rearrange("p (c u) -> p c u", u=64)[:, :, lo:lo + 16],
                srcb[pr, :].rearrange("p (c s) -> p c s", s=16)),
                reads=(srck,), writes=("xrot1",))
        for cc in range(min(T // 16, L1_NCHUNK)):
            sl = slice(cc * 16, (cc + 1) * 16)
            g8 = cc % 8
            p.op("ten", lambda e, cc=cc: e.matmul(pE[0:64, 0:128], B1[:, cc * 64:(cc + 1) * 64],
                                           c.sb["identity"][:, :], start=True, stop=True),
                 reads=("xrot1", "c_identity"), writes=("pE",))
            p.op("dve", lambda e: e.tensor_copy(vBDp[0:32, :], pE[0:32, 0:128]),
                 reads=("pE",), writes=("vBDp",))
            p.op("dve", lambda e: e.tensor_copy(bBDp[0:32, :], pE[32:64, 0:128]),
                 reads=("pE",), writes=("bBDp",))
            p.op("ten", lambda e, sl=sl, cc=cc: e.matmul(
                pD[0:32, 0:16], B0[:, cc * 32:(cc + 1) * 32],
                abf[:, sl], start=True, stop=False),
                reads=("xrot0", "abf"), writes=("pD",))
            p.op("ten", lambda e, sl=sl, cc=cc: e.matmul(
                pD[0:32, 0:16], B0[:, 4096 + cc * 32:4096 + (cc + 1) * 32],
                a2bf[:, sl], start=False, stop=True),
                reads=("xrot0", "ksb"), writes=("pD",))
            p.op("dve", lambda e: e.select(smp[0:32, 0:16], c.sb["selstrict32"][:, :],
                                           pD[0:32, 0:16], c.sb["zeros3216"][:, :]),
                 reads=("pD", "c_selstrict32", "c_zeros3216"), writes=("smp",))
            if cc > 0:
                p.op("ten", lambda e, sl=sl, g8=g8: e.matmul(
                    pA[:, g8 * 16:(g8 + 1) * 16], KVbds[:, :], abf[:, sl],
                    start=True, stop=False),
                    reads=("KVbds", "abf"), writes=("pA",))
            p.op("ten", lambda e, cc=cc, g8=g8: e.matmul(
                pA[:, g8 * 16:(g8 + 1) * 16], vBDp[:, :], smp[:, 0:16],
                start=(cc == 0), stop=True),
                reads=("vBDp", "smp"), writes=("pA",))
            p.op("ten", lambda e: e.matmul(pC[:, 0:128], bBDp[:, :], vBDp[:, :],
                                           start=True, stop=True),
                 reads=("bBDp", "vBDp"), writes=("pC",))
            p.op("dve", lambda e, cc=cc: e.scalar_tensor_tensor(
                KVbd[:, :], KVbd[:, :], dec[:, cc:cc + 1], pC[:, 0:128],
                OP.mult, OP.add),
                reads=("KVbd", "dec", "pC"), writes=("KVbd",))
            p.op("dve", lambda e: e.tensor_copy(KVbds[:, :], KVbd[:, :]),
                 reads=("KVbd",), writes=("KVbds",))
            if g8 == 7 or cc == min(T // 16, L1_NCHUNK) - 1:
                gi = cc // 8
                gsl = slice(gi * 128, gi * 128 + 128)
                p.op("dve", lambda e, gsl=gsl, hp=hp: e.tensor_mul(
                    hpre[hp][:, gsl], vfm[:, gsl], duqb[:, gsl]),
                    reads=("vfm", "Lsb"), writes=(f"hpre{hp}",))
                p.op("dve", lambda e, gsl=gsl, hp=hp: e.scalar_tensor_tensor(
                    hpre[hp][:, gsl], pA[:, 0:128], 1.0, hpre[hp][:, gsl],
                    OP.mult, OP.add),
                    reads=("pA", f"hpre{hp}"), writes=(f"hpre{hp}",))

    for odp in range(4):
        xr = xrot[odp % 2]
        xk = f"xrot{odp % 2}"
        p.dma("sp", xr.rearrange("p (k t) -> p k t", k=2),
              xin_r[:, 2 * odp:2 * odp + 2, 0:T], writes=(xk,), tag="xres")
        for j in range(2):
            od = 2 * odp + j
            wdst = wocol[:, 0:1024]
            wk = "wocol"
            p.dma("pool", wdst.rearrange("p (k o) -> p k o", k=8),
                  c.d["out_w_p"][od], writes=(wk,), tag="wout")
            for tch in range(4):
                sl = slice(tch * 512, (tch + 1) * 512)
                pst = pA if tch % 2 == 0 else pB
                kt = "pA" if tch % 2 == 0 else "pB"
                for k in range(NKT):
                    p.op("ten", lambda e, k=k, sl=sl, wdst=wdst, pst=pst: e.matmul(
                        pst[:, 0:512], wdst[:, k * 128:(k + 1) * 128], hpre[k][:, sl],
                        start=(k == 0), stop=False),
                        reads=(wk, f"hpre{k}"), writes=(kt,))
                p.op("ten", lambda e, od=od, pst=pst: e.matmul(
                    pst[:, 0:512], rows["B"][0:1, D + od * 128:D + (od + 1) * 128],
                    c.sb["ones_row_f32"][0:1, 0:512], start=False, stop=True),
                    reads=("borow", "c_ones_row_f32"), writes=(kt,))
                p.op("dve", lambda e, j=j, sl=sl, xr=xr, pst=pst, kt=kt, xk=xk: e.scalar_tensor_tensor(
                    xr[:, j * T + sl.start:j * T + sl.stop], pst[:, 0:512], 1.0,
                    xr[:, j * T + sl.start:j * T + sl.stop], OP.mult, OP.add),
                    reads=(kt, xk), writes=(xk,))
        p.dma("sp", xout_r[:, 2 * odp:2 * odp + 2, 0:T],
              xr.rearrange("p (k t) -> p k t", k=2), reads=(xk,), tag="xstore")

EMITTERS = {0: emit_l0_ffn, 1: emit_l0_attn, 2: emit_l0_ffn,
            3: emit_l1_ffn, 4: emit_l1_attn, 5: emit_l1_ffn}

_NC_CACHE = {}


def kernel(**inputs):
    g = prep_inputs(inputs)
    x = np.asarray(inputs["L0_attn_x"])
    xT = [np.ascontiguousarray(x[b].T) for b in range(8)]

    for group in SUBLAYER_GROUPS:
        key = tuple(group)
        if key not in _NC_CACHE:
            _NC_CACHE[key] = build_program(group)
        nc = _NC_CACHE[key]
        per_core = []
        for b in range(8):
            m = dict(g)
            m["xT"] = xT[b]
            per_core.append(m)
        res = run_bass_kernel_spmd(nc, per_core, core_ids=list(range(8)))
        xT = [np.ascontiguousarray(res.results[b]["out"]) for b in range(8)]

    return np.stack([o.T for o in xT], axis=0).astype(np.float32)


# version 07
# speedup vs baseline: 1.1230x; 1.0056x over previous
import numpy as np
import ml_dtypes
import concourse.bass as bass
import concourse.mybir as mybir
from concourse.bass_utils import run_bass_kernel_spmd

F32 = mybir.dt.float32
BF16 = mybir.dt.bfloat16
AF = mybir.ActivationFunctionType
OP = mybir.AluOpType
BF = ml_dtypes.bfloat16

D = 1024
T = 2048
NKT = D // 128

SUBLAYER_GROUPS = [[0, 1, 2, 3, 4, 5]]
DEBUG_TAPS = False
L1_PHASES = 99
L1_NCHUNK = 128
L1_NHP = 8
L1_COPS = 9


ENGS = ("ten", "act", "dve", "pool", "sp")
DMA_Q = ("pool", "sp")


class Op:
    __slots__ = ("eng", "fn", "reads", "writes", "idx", "needs_inc", "seq",
                 "waits", "tag", "_deps")

    def __init__(self, eng, fn, reads, writes, tag=""):
        self.eng = eng
        self.fn = fn
        self.reads = tuple(reads)
        self.writes = tuple(writes)
        self.needs_inc = False
        self.seq = 0
        self.waits = []
        self.tag = tag
        self._deps = []


class Prog:
    def __init__(self):
        self.items = []
        self.per_eng = {e: [] for e in ENGS}

    def op(self, eng, fn, reads=(), writes=(), tag=""):
        o = Op(eng, fn, reads, writes, tag or "op")
        o.tag = f"{o.tag}|{len(self.per_eng[eng])}"
        self.per_eng[eng].append(o)
        self.items.append(o)
        return o

    def dma(self, q, out, in_, reads=(), writes=(), tag=""):
        assert q in DMA_Q
        return self.op(q, lambda e, out=out, in_=in_: e.dma_start(out=out, in_=in_),
                       reads=reads, writes=writes, tag=tag)

    def barrier(self):
        self.items.append("BARRIER")

    def schedule(self):
        last_w = {}
        readers = {}
        eng_last = {e: None for e in ENGS}
        pending_barrier = {}
        for it in self.items:
            if it == "BARRIER":
                prods = [op for op in eng_last.values() if op is not None]
                pending_barrier = {e: [pp for pp in prods if pp.eng != e] for e in ENGS}
                last_w = {}
                readers = {}
                continue
            o = it
            deps = list(pending_barrier.pop(o.eng, ()))
            for b in o.reads:
                w = last_w.get(b)
                if w is not None and w.eng != o.eng:
                    deps.append(w)
            for b in o.writes:
                w = last_w.get(b)
                if w is not None and w.eng != o.eng:
                    deps.append(w)
                for reng, rop in readers.get(b, {}).items():
                    if reng != o.eng:
                        deps.append(rop)
            for p in deps:
                p.needs_inc = True
            o._deps = deps
            for b in o.reads:
                readers.setdefault(b, {})[o.eng] = o
            for b in o.writes:
                last_w[b] = o
                readers[b] = {}
            eng_last[o.eng] = o
        cnt = {e: 0 for e in ENGS}
        for it in self.items:
            if it == "BARRIER":
                continue
            if it.eng in DMA_Q:
                it.needs_inc = True
            if it.needs_inc:
                cnt[it.eng] += 1
            it.seq = cnt[it.eng]
        waited = {}
        for it in self.items:
            if it == "BARRIER":
                continue
            need = {}
            for p in it._deps:
                need[p.eng] = max(need.get(p.eng, 0), p.seq)
            for peng in sorted(need):
                seq = need[peng]
                key = (it.eng, peng)
                if waited.get(key, 0) < seq:
                    it.waits.append((peng, seq))
                    waited[key] = seq

    def emit(self, nc):
        self.schedule()
        sems = {}
        ctxs = []
        for e in ENGS:
            cm = nc.semaphore("sem_" + e)
            ctxs.append(cm)
            sems[e] = cm.__enter__()
        for q in DMA_Q:
            cm = nc.semaphore("raw_" + q)
            ctxs.append(cm)
            sems["raw_" + q] = cm.__enter__()
        blk_cm = nc.Block()
        block = blk_cm.__enter__()

        def run_engine(eng_obj, eng_name):
            is_q = eng_name in DMA_Q
            for o in self.per_eng[eng_name]:
                for peng, seq in o.waits:
                    eng_obj.wait_ge(sems[peng], seq)
                instr = o.fn(eng_obj)
                instr.annotate(o.tag)
                if o.needs_inc:
                    if is_q:
                        instr.then_inc(sems["raw_" + eng_name], 16)
                        eng_obj.wait_ge(sems["raw_" + eng_name], o.seq * 16)
                        eng_obj.sem_inc(sems[eng_name], 1)
                    else:
                        instr.then_inc(sems[o.eng], 1)

        block.tensor(lambda e: run_engine(e, "ten"))
        block.scalar(lambda e: run_engine(e, "act"))
        block.vector(lambda e: run_engine(e, "dve"))
        block.gpsimd(lambda e: run_engine(e, "pool"))

        def run_sync(e):
            run_engine(e, "sp")
            for en in ("pool", "ten", "act", "dve"):
                tot = sum(1 for o in self.per_eng[en] if o.needs_inc)
                if tot:
                    e.wait_ge(sems[en], tot)

        block.sync(run_sync)
        blk_cm.__exit__(None, None, None)
        for cm in reversed(ctxs):
            cm.__exit__(None, None, None)


def _pack_cols(wT, I, O):
    a = np.ascontiguousarray(np.asarray(wT, dtype=np.float32))
    assert a.shape == (I, O)
    return np.ascontiguousarray(
        a.reshape(I // 128, 128, O // 128, 128).transpose(2, 1, 0, 3))


def _cols(a):
    a = np.asarray(a, dtype=np.float32)
    return np.ascontiguousarray(a.reshape(-1, 128).T)


def prep_inputs(kw):
    g = {}
    g["w_qkv_p"] = _pack_cols(np.asarray(kw["L0_attn_w_qkv"]).T, D, 3 * D)
    g["b_qkv"] = np.asarray(kw["L0_attn_b_qkv"], np.float32).reshape(1, -1)
    g["w_out_p"] = _pack_cols(np.asarray(kw["L0_attn_w_out"]).T, D, D)
    g["b_out"] = np.asarray(kw["L0_attn_b_out"], np.float32).reshape(1, -1)
    om = np.asarray(kw["L0_attn_omega"], np.float32)
    g["omegaT"] = np.ascontiguousarray(np.transpose(om, (1, 2, 0, 3)).reshape(16, 64, 64))
    g["qnodes2"] = np.asarray(kw["L0_attn_quad_nodes"], np.float32).reshape(2, 1)
    g["qweights2"] = np.asarray(kw["L0_attn_quad_weights"], np.float32).reshape(2, 1)
    g["w1_p"] = _pack_cols(np.asarray(kw["L0_ffn_w1_w"]).T, D, 4096)
    g["w2_p"] = _pack_cols(np.asarray(kw["L0_ffn_w2_w"]).T, D, 4096)
    g["h1_p"] = _pack_cols(np.asarray(kw["L0_ffn_h1_w"]).T, 4096, 4096)
    g["h2_p"] = _pack_cols(np.asarray(kw["L0_ffn_h2_w"]).T, 4096, 4096)
    g["w3_p"] = _pack_cols(np.asarray(kw["L0_ffn_w3_w"]).T, 4096, D)
    for s in ("w1_s", "w2_s", "h1_s", "h2_s", "w3_s"):
        g[s] = _cols(kw["L0_ffn_" + s])
    g["key_w_p"] = _pack_cols(np.asarray(kw["L1_attn_key_w"]).T, D, D)
    g["key_b"] = np.asarray(kw["L1_attn_key_b"], np.float32).reshape(1, -1)
    g["vq_w_p"] = _pack_cols(np.asarray(kw["L1_attn_vq_w"]).T, D, 2 * D)
    g["out_w_p"] = _pack_cols(np.asarray(kw["L1_attn_out_w"]).T, D, D)
    g["out_b"] = np.asarray(kw["L1_attn_out_b"], np.float32).reshape(1, -1)
    G1R = np.asarray(kw["L1_ffn_G1_R"], np.float32)[0]
    G2R = np.asarray(kw["L1_ffn_G2_R"], np.float32)[:, :, 0, :]
    G1I = np.asarray(kw["L1_ffn_G1_I"], np.float32)[0]
    G2I = np.asarray(kw["L1_ffn_G2_I"], np.float32)[:, :, 0, :]

    def expand(G1, G2):
        A = np.broadcast_to(G1.transpose(1, 2, 0)[:, :, None, :, None], (4, 32, 16, 32, 16))
        A = np.ascontiguousarray(A.reshape(4, 512, 512))
        B = np.broadcast_to(G2.transpose(0, 2, 1)[:, None, :, None, :], (4, 32, 16, 32, 16))
        B = np.ascontiguousarray(B.reshape(4, 512, 512))
        return A, B
    g["G1R_x"], g["G2R_x"] = expand(G1R, G2R)
    g["G1I_x"], g["G2I_x"] = expand(G1I, G2I)
    g["bias_ri"] = np.concatenate([np.asarray(kw["L1_ffn_bias_real"]),
                                   np.asarray(kw["L1_ffn_bias_imag"])]).astype(np.float32).reshape(1, -1)

    g["trimask"] = np.triu(np.ones((128, 128), np.float32)).astype(BF)
    g["identity"] = np.eye(128, dtype=np.float32).astype(BF)
    g["ones_col_f32"] = np.ones((128, 1), np.float32)
    g["ones_col_bf"] = np.ones((128, 1), np.float32).astype(BF)
    g["ones_row_bf"] = np.ones((1, 512), np.float32).astype(BF)
    g["ones_row_f32"] = np.ones((1, 1024), np.float32)
    g["eps_col"] = np.full((1, 1), 1e-5, np.float32)
    tri16 = np.triu(np.ones((16, 16), np.float32)).astype(np.uint8)
    g["seltri16"] = np.concatenate([tri16, tri16], axis=1)
    g["zeros1632"] = np.zeros((16, 32), np.float32)
    g["seltri32"] = np.concatenate([tri16, tri16], axis=0)
    g["zeros3216"] = np.zeros((32, 16), np.float32)
    tri16s = np.triu(np.ones((16, 16), np.float32), 1).astype(np.uint8)
    g["selstrict32"] = np.concatenate([tri16s, tri16s], axis=0)
    bM = np.zeros((128, 128), np.float32)
    bM[:64, :64] = 1.0
    bM[64:, 64:] = 1.0
    g["blockM"] = bM.astype(BF)
    bo2 = np.zeros((128, 2), np.float32)
    bo2[:64, 0] = 1.0
    bo2[64:, 1] = 1.0
    g["blockones2"] = bo2.astype(BF)
    g["eps30"] = np.full((128, 1), 1e-30, np.float32)
    ind2 = np.zeros((2, 128), np.float32)
    ind2[0, :64] = 1.0
    ind2[1, 64:] = 1.0
    g["ind2"] = ind2
    sel2 = np.zeros((2, 128), np.float32)
    rr = (np.arange(128) // 32) % 2
    sel2[0, rr == 0] = 1.0
    sel2[1, rr == 1] = 1.0
    g["sel2"] = sel2
    return g


DEV_INPUTS = [
    ("xT", [D, T], F32),
    ("w_qkv_p", [24, 128, 8, 128], F32), ("b_qkv", [1, 3 * D], F32),
    ("w_out_p", [8, 128, 8, 128], F32), ("b_out", [1, D], F32),
    ("omegaT", [16, 64, 64], F32), ("qnodes2", [2, 1], F32), ("qweights2", [2, 1], F32),
    ("w1_p", [32, 128, 8, 128], F32), ("w2_p", [32, 128, 8, 128], F32),
    ("h1_p", [32, 128, 32, 128], F32), ("h2_p", [32, 128, 32, 128], F32),
    ("w3_p", [8, 128, 32, 128], F32),
    ("w1_s", [128, 32], F32), ("w2_s", [128, 32], F32),
    ("h1_s", [128, 32], F32), ("h2_s", [128, 32], F32), ("w3_s", [128, 8], F32),
    ("key_w_p", [8, 128, 8, 128], F32), ("key_b", [1, D], F32),
    ("vq_w_p", [16, 128, 8, 128], F32),
    ("out_w_p", [8, 128, 8, 128], F32), ("out_b", [1, D], F32),
    ("G1R_x", [4, 512, 512], F32), ("G2R_x", [4, 512, 512], F32),
    ("G1I_x", [4, 512, 512], F32), ("G2I_x", [4, 512, 512], F32),
    ("bias_ri", [1, D], F32),
    ("trimask", [128, 128], BF16), ("identity", [128, 128], BF16),
    ("ones_col_f32", [128, 1], F32), ("ones_col_bf", [128, 1], BF16),
    ("ones_row_bf", [1, 512], BF16), ("ones_row_f32", [1, 1024], F32),
    ("eps_col", [1, 1], F32),
    ("ind2", [2, 128], F32), ("sel2", [2, 128], F32),
    ("blockones2", [128, 2], BF16), ("eps30", [128, 1], F32),
    ("seltri16", [16, 32], mybir.dt.uint8), ("zeros1632", [16, 32], F32),
    ("seltri32", [32, 16], mybir.dt.uint8), ("zeros3216", [32, 16], F32),
    ("selstrict32", [32, 16], mybir.dt.uint8), ("blockM", [128, 128], BF16),
]
CONST_NAMES = ("trimask", "identity", "ones_col_f32", "ones_col_bf", "ones_row_bf",
               "ones_row_f32", "eps_col", "ind2", "sel2", "blockones2", "eps30",
               "seltri16", "zeros1632", "seltri32", "zeros3216",
               "selstrict32", "blockM")


class Ctx:
    pass


def build_program(sublayers):
    nc = bass.Bass("TRN2", target_bir_lowering=False, detect_race_conditions=False)
    c = Ctx()
    c.nc = nc
    c.d = {}
    for name, shape, dt in DEV_INPUTS:
        c.d[name] = nc.dram_tensor(name, shape, dt, kind="ExternalInput").ap()
    c.d["out"] = nc.dram_tensor("out", [D, T], F32, kind="ExternalOutput").ap()
    c.d["xs0"] = nc.dram_tensor("xs0", [D, T], F32, kind="Internal").ap()
    c.d["xs1"] = nc.dram_tensor("xs1", [D, T], F32, kind="Internal").ap()

    p = Prog()
    c.p = p
    sb_cms = []
    c.sb = {}
    for nm in CONST_NAMES:
        shape = dict((n, s) for n, s, _ in DEV_INPUTS)[nm]
        dt = dict((n, dd) for n, _, dd in DEV_INPUTS)[nm]
        cm = nc.sbuf_tensor("c_" + nm, shape, dt)
        sb_cms.append(cm)
        c.sb[nm] = cm.__enter__()
        p.dma("sp", c.sb[nm][:, :], c.d[nm], writes=("c_" + nm,), tag="const")

    cur = "xT"
    scratch = ["xs0", "xs1"]
    for i, sl in enumerate(sublayers):
        nxt = "out" if i == len(sublayers) - 1 else scratch[i % 2]
        c.pfx = f"u{i}_"
        EMITTERS[sl](c, c.d[cur], c.d[nxt])
        p.barrier()
        cur = nxt

    p.emit(nc)
    return nc


def emit_l0_ffn(c, x_in, x_out):
    nc, p = c.nc, c.p
    TC = 1024
    NCH = T // TC
    sb_cms = []

    def alloc(name, shape, dt):
        cm = nc.sbuf_tensor(c.pfx + name, shape, dt)
        sb_cms.append(cm)
        return cm.__enter__()

    xrot = [alloc(f"xrot{i}", [128, 2 * TC], F32) for i in range(2)]
    ln = [alloc(f"lnt{k}", [128, TC], BF16) for k in range(NKT)]
    hbuf = alloc("hbuf", [128, 32 * TC], BF16)
    gbuf = alloc("gbuf", [128, 32 * TC], BF16)
    wcol = alloc("wcol", [128, 2 * 32 * 128], BF16)
    wsml = alloc("wsml", [128, 2 * 2048], BF16)
    scol = alloc("scol", [128, 128], F32)
    s3col = alloc("s3col", [128, 8], F32)
    wsrow = alloc("wsrow", [1, 2 * 2 * 128], BF16)
    rowA = alloc("rowA", [1, TC], F32)
    rowB = alloc("rowB", [1, TC], F32)
    rowC = alloc("rowC", [1, TC], F32)
    rowD = alloc("rowD", [1, TC], BF16)
    ps = []
    for i in range(4):
        cm = nc.psum_tensor(f"{c.pfx}ps{i}", [128, TC], F32)
        sb_cms.append(cm)
        ps.append(cm.__enter__())

    xbf = [hbuf[:, k * TC:(k + 1) * TC] for k in range(NKT)]
    x2v = [hbuf[:, (8 + k) * TC:(9 + k) * TC] for k in range(NKT)]
    silu_v = [gbuf[:, (i * 2) * TC:(i * 2 + 2) * TC].bitcast(F32) for i in range(2)]

    p.dma("sp", scol[:, 0:32], c.d["w1_s"], writes=("scol",))
    p.dma("sp", scol[:, 32:64], c.d["w2_s"], writes=("scol",))
    p.dma("sp", scol[:, 64:96], c.d["h1_s"], writes=("scol",))
    p.dma("sp", scol[:, 96:128], c.d["h2_s"], writes=("scol",))
    p.dma("sp", s3col[:, :], c.d["w3_s"], writes=("s3col",))
    p.op("dve", lambda e: e.tensor_scalar_mul(s3col[:, :], s3col[:, :], 0.5),
         reads=("s3col",), writes=("s3col",))

    xin_r = x_in.rearrange("(k p) t -> p k t", p=128)
    xout_r = x_out.rearrange("(k p) t -> p k t", p=128)

    for ch in range(NCH):
        tsl = slice(ch * TC, (ch + 1) * TC)

        for kk in range(4):
            xr = xrot[kk % 2]
            xk = f"xrot{kk % 2}"
            p.dma("sp", xr.rearrange("p (k t) -> p k t", k=2),
                  xin_r[:, 2 * kk:2 * kk + 2, tsl], writes=(xk,), tag="xload")
            for j in range(2):
                k = 2 * kk + j
                p.op("dve", lambda e, xr=xr, j=j, k=k: e.tensor_copy(
                    xbf[k], xr[:, j * TC:(j + 1) * TC]),
                    reads=(xk,), writes=(f"h{k}",))
                p.op("dve", lambda e, k=k: e.tensor_mul(x2v[k], xbf[k], xbf[k]),
                    reads=(f"h{k}",), writes=(f"h{8 + k}",))
        for sc in range(TC // 512):
            sl = slice(sc * 512, (sc + 1) * 512)
            for k in range(NKT):
                p.op("ten", lambda e, k=k, sl=sl: e.matmul(
                    ps[0][0:1, sl], c.sb["ones_col_bf"][:, :], xbf[k][:, sl],
                    start=(k == 0), stop=(k == NKT - 1)),
                    reads=(f"h{k}", "c_ones_col_bf"), writes=("ps0",))
                p.op("ten", lambda e, k=k, sl=sl: e.matmul(
                    ps[1][0:1, sl], c.sb["ones_col_bf"][:, :], x2v[k][:, sl],
                    start=(k == 0), stop=(k == NKT - 1)),
                    reads=(f"h{8 + k}", "c_ones_col_bf"), writes=("ps1",))
            p.op("dve", lambda e, sl=sl: e.tensor_copy(rowA[0:1, sl], ps[0][0:1, sl]),
                 reads=("ps0",), writes=("rowA",))
            p.op("dve", lambda e, sl=sl: e.tensor_copy(rowB[0:1, sl], ps[1][0:1, sl]),
                 reads=("ps1",), writes=("rowB",))
        p.op("dve", lambda e: e.tensor_scalar_mul(rowA[0:1, :], rowA[0:1, :], 1.0 / D),
             reads=("rowA",), writes=("rowA",))
        p.op("dve", lambda e: e.scalar_tensor_tensor(rowC[0:1, :], rowA[0:1, :], -1.0,
                                                     rowA[0:1, :], OP.mult, OP.mult),
             reads=("rowA",), writes=("rowC",))
        p.op("dve", lambda e: e.scalar_tensor_tensor(rowB[0:1, :], rowB[0:1, :], 1.0 / D,
                                                     rowC[0:1, :], OP.mult, OP.add),
             reads=("rowB", "rowC"), writes=("rowB",))
        p.op("act", lambda e: e.activation(rowC[0:1, :], rowB[0:1, :], AF.Ln,
                                           bias=c.sb["eps_col"][0:1, 0:1], scale=1.0),
             reads=("rowB", "c_eps_col"), writes=("rowC",))
        p.op("act", lambda e: e.activation(rowC[0:1, :], rowC[0:1, :], AF.Exp, scale=-0.5),
             reads=("rowC",), writes=("rowC",))
        p.op("dve", lambda e: e.scalar_tensor_tensor(rowA[0:1, :], rowA[0:1, :], -1.0,
                                                     rowC[0:1, :], OP.mult, OP.mult),
             reads=("rowA", "rowC"), writes=("rowA",))
        p.op("dve", lambda e: e.tensor_copy(rowD[0:1, :], rowA[0:1, :]),
             reads=("rowA",), writes=("rowD",))

        for sc in range(TC // 512):
            sl = slice(sc * 512, (sc + 1) * 512)
            p.op("ten", lambda e, sl=sl: e.matmul(
                ps[0][:, sl], c.sb["ones_row_f32"][0:1, 0:128], rowC[0:1, sl],
                start=True, stop=True),
                reads=("rowC", "c_ones_row_f32"), writes=("ps0",))
        for kk in range(4):
            xr = xrot[kk % 2]
            xk = f"xrot{kk % 2}"
            p.dma("sp", xr.rearrange("p (k t) -> p k t", k=2),
                  xin_r[:, 2 * kk:2 * kk + 2, tsl], writes=(xk,), tag="xload2")
            for j in range(2):
                k = 2 * kk + j
                p.op("dve", lambda e, xr=xr, j=j, k=k: e.tensor_mul(
                    ln[k][:, :], xr[:, j * TC:(j + 1) * TC], ps[0][:, :]),
                    reads=(xk, "ps0"), writes=(f"ln{k}",))

        for o in range(32):
            ob = o % 2
            w1dst = wsml[:, ob * 2048:ob * 2048 + 1024]
            w2dst = wsml[:, ob * 2048 + 1024:ob * 2048 + 2048]
            k1, k2 = f"wsml{ob}a", f"wsml{ob}b"
            p.dma("pool", w1dst.rearrange("p (k o) -> p k o", k=8),
                  c.d["w1_p"][o], writes=(k1,), tag="w1col")
            p.dma("pool", w2dst.rearrange("p (k o) -> p k o", k=8),
                  c.d["w2_p"][o], writes=(k2,), tag="w2col")
            pst1, kt1 = ps[2 + ob], f"ps{2 + ob}"
            pst2, kt2 = ps[0 + ob], f"ps{0 + ob}"
            for wd, wk, roff in ((w1dst, k1, ob * 256), (w2dst, k2, ob * 256 + 128)):
                for k in range(NKT):
                    p.op("ten", lambda e, k=k, wd=wd: e.matmul(
                        ps[1][0:1, 0:128], c.sb["ones_col_bf"][:, :], wd[:, k * 128:(k + 1) * 128],
                        start=(k == 0), stop=(k == NKT - 1)),
                        reads=(wk, "c_ones_col_bf"), writes=("ps1",))
                p.op("dve", lambda e, roff=roff: e.tensor_copy(
                    wsrow[0:1, roff:roff + 128], ps[1][0:1, 0:128]),
                    reads=("ps1",), writes=(f"wsrow{roff}",))
            for half in range(2):
                hsl = slice(half * 512, (half + 1) * 512)
                for k in range(NKT):
                    p.op("ten", lambda e, k=k, hsl=hsl, w1dst=w1dst, pst1=pst1: e.matmul(
                        pst1[:, hsl], w1dst[:, k * 128:(k + 1) * 128], ln[k][:, hsl],
                        start=(k == 0), stop=False),
                        reads=(k1, f"ln{k}"), writes=(kt1,))
                p.op("ten", lambda e, hsl=hsl, o=o, pst1=pst1, ob=ob: e.matmul(
                    pst1[:, hsl], wsrow[0:1, ob * 256:ob * 256 + 128], rowD[0:1, hsl],
                    start=False, stop=True),
                    reads=(f"wsrow{ob * 256}", "rowD"), writes=(kt1,))
                for k in range(NKT):
                    p.op("ten", lambda e, k=k, hsl=hsl, w2dst=w2dst, pst2=pst2: e.matmul(
                        pst2[:, hsl], w2dst[:, k * 128:(k + 1) * 128], ln[k][:, hsl],
                        start=(k == 0), stop=False),
                        reads=(k2, f"ln{k}"), writes=(kt2,))
                p.op("ten", lambda e, hsl=hsl, o=o, pst2=pst2, ob=ob: e.matmul(
                    pst2[:, hsl], wsrow[0:1, ob * 256 + 128:ob * 256 + 256], rowD[0:1, hsl],
                    start=False, stop=True),
                    reads=(f"wsrow{ob * 256 + 128}", "rowD"), writes=(kt2,))
            p.op("act", lambda e, o=o, pst1=pst1: e.activation(
                silu_v[o % 2], pst1[:, :], AF.Silu, scale=scol[:, o:o + 1]),
                reads=(kt1, "scol"), writes=(f"g{(o % 2) * 2}", f"g{(o % 2) * 2 + 1}"))
            p.op("dve", lambda e, o=o, pst2=pst2: e.scalar_tensor_tensor(
                hbuf[:, o * TC:(o + 1) * TC], pst2[:, :], scol[:, 32 + o:32 + o + 1],
                silu_v[o % 2], OP.mult, OP.mult),
                reads=(kt2, f"g{(o % 2) * 2}", f"g{(o % 2) * 2 + 1}", "scol"), writes=(f"h{o}",))

        for o in range(32):
            ob = o % 2
            wdst = wcol[:, ob * 4096:(ob + 1) * 4096]
            wk = f"wcol{ob}"
            p.dma("pool", wdst.rearrange("p (k o) -> p k o", k=32),
                  c.d["h1_p"][o], writes=(wk,), tag="h1col")
            pst, kt = ps[2 + ob], f"ps{2 + ob}"
            for half in range(2):
                hsl = slice(half * 512, (half + 1) * 512)
                for k in range(32):
                    p.op("ten", lambda e, k=k, hsl=hsl, wdst=wdst, pst=pst: e.matmul(
                        pst[:, hsl], wdst[:, k * 128:(k + 1) * 128],
                        hbuf[:, k * TC + hsl.start:k * TC + hsl.stop],
                        start=(k == 0), stop=(k == 31)),
                        reads=(wk, f"h{k}"), writes=(kt,))
            p.op("act", lambda e, o=o, pst=pst: e.activation(
                gbuf[:, o * TC:(o + 1) * TC], pst[:, :], AF.Gelu,
                scale=scol[:, 64 + o:64 + o + 1]),
                reads=(kt, "scol"), writes=(f"g{o}",))

        for o in range(32):
            ob = o % 2
            wdst = wcol[:, ob * 4096:(ob + 1) * 4096]
            wk = f"wcol{ob}"
            p.dma("pool", wdst.rearrange("p (k o) -> p k o", k=32),
                  c.d["h2_p"][o], writes=(wk,), tag="h2col")
            pst, kt = ps[2 + ob], f"ps{2 + ob}"
            for half in range(2):
                hsl = slice(half * 512, (half + 1) * 512)
                for k in range(32):
                    p.op("ten", lambda e, k=k, hsl=hsl, wdst=wdst, pst=pst: e.matmul(
                        pst[:, hsl], wdst[:, k * 128:(k + 1) * 128],
                        gbuf[:, k * TC + hsl.start:k * TC + hsl.stop],
                        start=(k == 0), stop=(k == 31)),
                        reads=(wk, f"g{k}"), writes=(kt,))
            p.op("act", lambda e, o=o, pst=pst: e.activation(
                hbuf[:, o * TC:(o + 1) * TC], pst[:, :], AF.Gelu,
                scale=scol[:, 96 + o:96 + o + 1]),
                reads=(kt, "scol"), writes=(f"h{o}",))

        for op2 in range(4):
            xr = xrot[op2 % 2]
            xk = f"xrot{op2 % 2}"
            p.dma("sp", xr.rearrange("p (k t) -> p k t", k=2),
                  xin_r[:, 2 * op2:2 * op2 + 2, tsl], writes=(xk,), tag="xres")
            for j in range(2):
                od = 2 * op2 + j
                ob = od % 2
                wdst = wcol[:, ob * 4096:(ob + 1) * 4096]
                wk = f"wcol{ob}"
                p.dma("pool", wdst.rearrange("p (k o) -> p k o", k=32),
                      c.d["w3_p"][od], writes=(wk,), tag="w3col")
                pst, kt = ps[2 + ob], f"ps{2 + ob}"
                for half in range(2):
                    hsl = slice(half * 512, (half + 1) * 512)
                    for k in range(32):
                        p.op("ten", lambda e, k=k, hsl=hsl, wdst=wdst, pst=pst: e.matmul(
                            pst[:, hsl], wdst[:, k * 128:(k + 1) * 128],
                            hbuf[:, k * TC + hsl.start:k * TC + hsl.stop],
                            start=(k == 0), stop=(k == 31)),
                            reads=(wk, f"h{k}"), writes=(kt,))
                p.op("dve", lambda e, od=od, j=j, xr=xr, pst=pst: e.scalar_tensor_tensor(
                    xr[:, j * TC:(j + 1) * TC], pst[:, :], s3col[:, od:od + 1],
                    xr[:, j * TC:(j + 1) * TC], OP.mult, OP.add),
                    reads=(kt, "s3col", xk), writes=(xk,))
            p.dma("sp", xout_r[:, 2 * op2:2 * op2 + 2, tsl],
                  xr.rearrange("p (k t) -> p k t", k=2), reads=(xk,), tag="xstore")

    for cm in reversed(sb_cms):
        cm.__exit__(None, None, None)



def emit_l0_attn(c, x_in, x_out):
    nc, p = c.nc, c.p
    sb_cms = []

    def alloc(name, shape, dt):
        cm = nc.sbuf_tensor(c.pfx + name, shape, dt)
        sb_cms.append(cm)
        return cm.__enter__()

    xrot = [alloc(f"xrot{i}", [128, 2 * T], F32) for i in range(2)]
    xbf2 = alloc("xbf2", [128, 2 * T], BF16)
    ln = [alloc(f"lnt{k}", [128, T], BF16) for k in range(NKT)]
    rows = {"A": alloc("rowA", [1, T], F32), "B": alloc("rowB", [1, T], F32),
            "C": alloc("rowC", [1, T], F32), "D": alloc("rowD", [1, T], BF16)}
    rbc = alloc("rbc", [128, T], F32)
    apre = [alloc(f"apre{i}", [128, T], BF16) for i in range(8)]
    qzb = alloc("qzb", [128, T], BF16)
    kzb = alloc("kzb", [128, T], BF16)
    qf = alloc("qf", [128, T], BF16)
    kf = alloc("kf", [128, T], BF16)
    vfm = alloc("vfm", [128, T], BF16)
    z2b = vfm
    om_sb = alloc("om_sb", [128, 64], BF16)
    wqcol = alloc("wqcol", [128, 2 * 1024], BF16)
    s2row = alloc("s2row", [2, 512], F32)
    rnbc = alloc("rnbc", [128, 1024], F32)
    fw1 = None
    fw2 = None
    tbf = alloc("tbf", [128, 1024], BF16)
    brow = alloc("brow", [1, 3 * D], F32)
    borow = alloc("borow", [1, D], F32)
    qcol2 = alloc("qcol2", [2, 4], F32)
    qcols = alloc("qcols", [128, 4], F32)
    kftok = alloc("kftok", [128, 128], BF16)
    wsq = alloc("wsq", [1, 256], BF16)
    vtok = alloc("vtok", [128, 16 * 128], BF16)
    smb = [alloc(f"smb{h}", [128, 128], BF16) for h in range(2)]
    KVsb = alloc("KVsb", [128, 64], BF16)
    kssb = alloc("kssb", [128, 1], BF16)
    KVf = alloc("KVf", [128, 64], F32)
    ksf = alloc("ksf", [128, 1], F32)
    ctxsb = alloc("ctxsb", [128, 512], F32)
    denr = alloc("denr", [1, 512], F32)
    wocol = alloc("wocol", [128, 1024], BF16)
    ps_cms = []

    def palloc(name, shape, dt=F32):
        cm = nc.psum_tensor(c.pfx + name, shape, dt)
        ps_cms.append(cm)
        sb_cms.append(cm)
        return cm.__enter__()

    fw1 = rbc[:, 0:1024]
    fw2 = rbc[:, 1024:2048]
    xin_r = x_in.rearrange("(k p) t -> p k t", p=128)
    xout_r = x_out.rearrange("(k p) t -> p k t", p=128)

    p.dma("sp", brow[0:1, :], c.d["b_qkv"], writes=("brow",))
    p.dma("sp", borow[0:1, :], c.d["b_out"], writes=("borow",))
    p.dma("sp", qcol2[:, 0:1], c.d["qnodes2"], writes=("qcol2",))
    p.dma("sp", qcol2[:, 1:2], c.d["qweights2"], writes=("qcol2",))

    ln_stream(c, p, xin_r, slice(0, T), T, xrot, ln, rows, rbc, xbf2)

    psq = palloc("psq", [128, 1024])
    psk = palloc("psk", [128, 1024])
    psp = palloc("psp", [128, 1024])
    psv = palloc("psv", [128, 512])
    psw = palloc("psw", [128, 512])

    p.op("dve", lambda e: e.tensor_scalar(qcol2[0:2, 2:3], qcol2[0:2, 0:1], 0.0, 2.0,
                                          OP.max, OP.mult), reads=("qcol2",), writes=("qcol2",))
    p.op("act", lambda e: e.activation(qcol2[0:2, 2:3], qcol2[0:2, 2:3], AF.Ln,
                                       bias=c.sb["eps30"][0:2, 0:1]),
         reads=("qcol2", "c_eps30"), writes=("qcol2",))
    p.op("act", lambda e: e.activation(qcol2[0:2, 2:3], qcol2[0:2, 2:3], AF.Exp, scale=0.5),
         reads=("qcol2",), writes=("qcol2",))
    p.op("dve", lambda e: e.tensor_scalar_mul(qcol2[0:2, 3:4], qcol2[0:2, 0:1], -1.0),
         reads=("qcol2",), writes=("qcol2",))
    p.op("dve", lambda e: e.tensor_scalar_max(qcol2[0:2, 1:2], qcol2[0:2, 1:2], 0.0),
         reads=("qcol2",), writes=("qcol2",))
    p.op("act", lambda e: e.activation(qcol2[0:2, 1:2], qcol2[0:2, 1:2], AF.Ln,
                                       bias=c.sb["eps30"][0:2, 0:1]),
         reads=("qcol2", "c_eps30"), writes=("qcol2",))
    p.op("dve", lambda e: e.tensor_scalar(qcol2[0:2, 1:2], qcol2[0:2, 1:2], 0.25,
                                          -1.7328679513998633, OP.mult, OP.add),
         reads=("qcol2",), writes=("qcol2",))
    for i, src in ((0, 2), (1, 1), (2, 3)):
        p.op("ten", lambda e, i=i, src=src: e.matmul(
            psw[:, i:i + 1], c.sb["sel2"][:, :], qcol2[0:2, src:src + 1],
            start=True, stop=True),
            reads=("qcol2", "c_sel2"), writes=("psw",))
        p.op("dve", lambda e, i=i: e.tensor_copy(qcols[:, i:i + 1], psw[:, i:i + 1]),
             reads=("psw",), writes=("qcols",))

    for hp in range(8):
        hb = hp % 2
        p.dma("pool", om_sb[0:64, :], c.d["omegaT"][2 * hp], writes=("om_sb",))
        p.dma("pool", om_sb[64:128, :], c.d["omegaT"][2 * hp + 1], writes=("om_sb",))
        for w_i, (wname, pst, kt, zb, zk, boff) in enumerate(
                ((hp, psq, "psq", qzb, "qzb", 0),
                 (8 + hp, psk, "psk", kzb, "kzb", 1024),
                 (16 + hp, psv, "psv", vfm, "vfm", 2048))):
            wdst = wqcol[:, w_i % 2 * 1024:(w_i % 2 + 1) * 1024]
            wk = f"wqcol{w_i % 2}"
            p.dma("pool", wdst.rearrange("p (k o) -> p k o", k=8),
                  c.d["w_qkv_p"][wname], writes=(wk,), tag="wqkv")
            wb = w_i % 2
            for k in range(NKT):
                p.op("ten", lambda e, k=k, wdst=wdst: e.matmul(
                    psw[0:1, 0:128], c.sb["ones_col_bf"][:, :], wdst[:, k * 128:(k + 1) * 128],
                    start=(k == 0), stop=(k == NKT - 1)),
                    reads=(wk, "c_ones_col_bf"), writes=("psw",))
            p.op("dve", lambda e, wb=wb: e.tensor_copy(wsq[0:1, wb * 128:(wb + 1) * 128],
                                                       psw[0:1, 0:128]),
                 reads=("psw",), writes=(f"wsq{wb}",))
            if w_i < 2:
                for sc in range(4):
                    sl = slice(sc * 512, (sc + 1) * 512)
                    psl = slice((sc % 2) * 512, (sc % 2 + 1) * 512)
                    for k in range(NKT):
                        p.op("ten", lambda e, k=k, sl=sl, psl=psl, wdst=wdst, pst=pst: e.matmul(
                            pst[:, psl], wdst[:, k * 128:(k + 1) * 128], ln[k][:, sl],
                            start=(k == 0), stop=False),
                            reads=(wk, f"ln{k}"), writes=(kt,))
                    p.op("ten", lambda e, sl=sl, psl=psl, pst=pst, wb=wb: e.matmul(
                        pst[:, psl], wsq[0:1, wb * 128:(wb + 1) * 128], rows["D"][0:1, sl],
                        start=False, stop=False),
                        reads=(f"wsq{wb}", "rowD"), writes=(kt,))
                    p.op("ten", lambda e, sl=sl, psl=psl, pst=pst, boff=boff, hp=hp: e.matmul(
                        pst[:, psl], brow[0:1, boff + hp * 128:boff + (hp + 1) * 128],
                        c.sb["ones_row_f32"][0:1, 0:512], start=False, stop=True),
                        reads=("brow", "c_ones_row_f32"), writes=(kt,))
                    if sc % 2 == 1:
                        ch2 = sc // 2
                        p.op("act", lambda e, ch2=ch2, pst=pst, zb=zb: e.activation(
                            zb[:, ch2 * 1024:(ch2 + 1) * 1024], pst[:, :], AF.Identity),
                            reads=(kt,), writes=(zk,))
            else:
                for sc in range(4):
                    sl = slice(sc * 512, (sc + 1) * 512)
                    for k in range(NKT):
                        p.op("ten", lambda e, k=k, sl=sl, wdst=wdst, pst=pst: e.matmul(
                            pst[:, 0:512], wdst[:, k * 128:(k + 1) * 128], ln[k][:, sl],
                            start=(k == 0), stop=False),
                            reads=(wk, f"ln{k}"), writes=(kt,))
                    p.op("ten", lambda e, sl=sl, pst=pst, wb=wb: e.matmul(
                        pst[:, 0:512], wsq[0:1, wb * 128:(wb + 1) * 128], rows["D"][0:1, sl],
                        start=False, stop=False),
                        reads=(f"wsq{wb}", "rowD"), writes=(kt,))
                    p.op("ten", lambda e, pst=pst, hp=hp: e.matmul(
                        pst[:, 0:512], brow[0:1, 2048 + hp * 128:2048 + (hp + 1) * 128],
                        c.sb["ones_row_f32"][0:1, 0:512], start=False, stop=True),
                        reads=("brow", "c_ones_row_f32"), writes=(kt,))
                    p.op("act", lambda e, sc=sc, pst=pst: e.activation(
                        vfm[:, sc * 512:(sc + 1) * 512], pst[:, :], AF.Identity),
                        reads=(kt,), writes=("vfm",))
        for tt in range(16):
            p.op("ten", lambda e, tt=tt: e.transpose(
                psp[:, 0:64].bitcast(BF16), vfm[:, tt * 128:(tt + 1) * 128],
                c.sb["identity"][:, :]),
                reads=("vfm", "c_identity"), writes=("psp",))
            p.op("dve", lambda e, tt=tt: e.tensor_copy(
                vtok[:, tt * 128:(tt + 1) * 128], psp[:, 0:64].bitcast(BF16)),
                reads=("psp",), writes=(f"vtok{tt}",))

        if DEBUG_TAPS and hp == 0:
            for ki in range(2):
                dd = nc.dram_tensor(f"dbg_ln{ki}", [128, T], BF16, kind="ExternalOutput").ap()
                p.dma("sp", dd, ln[ki][:, :], reads=(f"ln{ki}",), tag="dbg")
            ddr = nc.dram_tensor("dbg_rows", [1, T], F32, kind="ExternalOutput").ap()
            p.dma("sp", ddr, rows["C"][0:1, :], reads=("rowC",), tag="dbg")
            ddr2 = nc.dram_tensor("dbg_nmr", [1, T], F32, kind="ExternalOutput").ap()
            p.dma("sp", ddr2, rows["A"][0:1, :], reads=("rowA",), tag="dbg")
            for nm, buf in (("qzb", qzb), ("kzb", kzb), ("vfm", vfm)):
                dd = nc.dram_tensor("dbg_" + nm, [128, T], BF16, kind="ExternalOutput").ap()
                p.dma("sp", dd, buf[:, :], reads=(nm if nm != "vfm" else "vfm",), tag="dbg")
        if DEBUG_TAPS and hp == 0:
            ddo = nc.dram_tensor("dbg_om", [128, 64], BF16, kind="ExternalOutput").ap()
            p.dma("sp", ddo, om_sb[:, :], reads=("om_sb",), tag="dbg")
        for qk, (zb, zk, fdst, fk) in enumerate(((qzb, "qzb", qf, "qf"),
                                                 (kzb, "kzb", kf, "kf"))):
            p.op("dve", lambda e, zb=zb: e.tensor_mul(z2b[:, :], zb[:, :], zb[:, :]),
                 reads=(zk,), writes=("vfm",))
            for ch2 in range(2):
                csl = slice(ch2 * 1024, (ch2 + 1) * 1024)
                for sc in range(2):
                    ssl = slice(ch2 * 1024 + sc * 512, ch2 * 1024 + (sc + 1) * 512)
                    p.op("ten", lambda e, ssl=ssl: e.matmul(
                        psw[0:2, 0:512], c.sb["blockones2"][:, :], z2b[:, ssl],
                        start=True, stop=True),
                        reads=("vfm", "c_blockones2"), writes=("psw",))
                    p.op("dve", lambda e: e.tensor_copy(
                        s2row[0:2, 0:512], psw[0:2, 0:512]),
                        reads=("psw",), writes=("s2row",))
                    p.op("ten", lambda e: e.matmul(
                        psw[:, 0:512], c.sb["ind2"][:, :], s2row[0:2, 0:512],
                        start=True, stop=True),
                        reads=("s2row", "c_ind2"), writes=("psw",))
                    p.op("dve", lambda e, sc=sc: e.tensor_scalar_max(
                        rnbc[:, sc * 512:(sc + 1) * 512], psw[:, 0:512], 1e-24),
                        reads=("psw",), writes=("rnbc",))
                    p.op("act", lambda e, sc=sc: e.activation(
                        rnbc[:, sc * 512:(sc + 1) * 512], rnbc[:, sc * 512:(sc + 1) * 512],
                        AF.Ln), reads=("rnbc",), writes=("rnbc",))
                    p.op("act", lambda e, sc=sc: e.activation(
                        rnbc[:, sc * 512:(sc + 1) * 512], rnbc[:, sc * 512:(sc + 1) * 512],
                        AF.Exp, scale=-0.5), reads=("rnbc",), writes=("rnbc",))
                    for h in range(2):
                        p.op("ten", lambda e, h=h, ssl=ssl, sc=sc, zb=zb: e.matmul(
                            psp[h * 64:(h + 1) * 64, sc * 512:(sc + 1) * 512],
                            om_sb[h * 64:(h + 1) * 64, :], zb[h * 64:(h + 1) * 64, ssl],
                            start=True, stop=True),
                            reads=("om_sb", zk), writes=("psp",))
                if DEBUG_TAPS and hp == 0 and qk == 0 and ch2 == 0:
                    dd0 = nc.dram_tensor("dbg_praw", [128, 1024], F32, kind="ExternalOutput").ap()
                    p.op("dve", lambda e: e.tensor_copy(tbf[:, :].bitcast(F32)[:, 0:512], psp[:, 0:512]),
                         reads=("psp",), writes=("tbf",))
                    p.dma("sp", dd0[:, 0:512], tbf[:, :].bitcast(F32)[:, 0:512], reads=("tbf",), tag="dbg")
                    p.op("dve", lambda e: e.tensor_copy(tbf[:, :].bitcast(F32)[:, 0:512], psp[:, 512:1024]),
                         reads=("psp",), writes=("tbf",))
                    p.dma("sp", dd0[:, 512:1024], tbf[:, :].bitcast(F32)[:, 0:512], reads=("tbf",), tag="dbg")
                p.op("dve", lambda e: e.tensor_mul(fw1[:, :], psp[:, :], rnbc[:, :]),
                     reads=("psp", "rnbc"), writes=("rbc",))
                if DEBUG_TAPS and hp == 0 and qk == 0 and ch2 == 0:
                    dd1 = nc.dram_tensor("dbg_p", [128, 1024], F32, kind="ExternalOutput").ap()
                    p.dma("sp", dd1, fw1[:, :], reads=("rbc",), tag="dbg")
                    dd2 = nc.dram_tensor("dbg_rn", [128, 1024], F32, kind="ExternalOutput").ap()
                    p.dma("sp", dd2, rnbc[:, :], reads=("rnbc",), tag="dbg")
                p.op("dve", lambda e: e.tensor_scalar(fw2[:, :], fw1[:, :], qcols[:, 0:1],
                                                      qcols[:, 2:3], OP.mult, OP.add),
                     reads=("rbc", "qcols"), writes=("rbc",))
                p.op("dve", lambda e: e.tensor_scalar(fw2[:, :], fw2[:, :], 20.0, -20.0,
                                                      OP.min, OP.max),
                     reads=("rbc",), writes=("rbc",))
                p.op("act", lambda e: e.activation(fw2[:, :], fw2[:, :], AF.Exp,
                                                   bias=qcols[:, 1:2], scale=0.5),
                     reads=("rbc", "qcols"), writes=("rbc",))
                p.op("dve", lambda e: e.tensor_mul(tbf[:, :], fw1[:, :], fw2[:, :]),
                     reads=("rbc",), writes=("tbf",))
                p.op("act", lambda e, fdst=fdst, csl=csl: e.activation(
                    fdst[:, csl], tbf[:, :], AF.Square),
                    reads=("tbf",), writes=(fk,))

        if DEBUG_TAPS and hp == 0:
            for nm, buf in (("qf", qf), ("kf", kf)):
                dd = nc.dram_tensor("dbg_" + nm, [128, T], BF16, kind="ExternalOutput").ap()
                p.dma("sp", dd, buf[:, :], reads=(nm,), tag="dbg")
            dd = nc.dram_tensor("dbg_vtok", [128, 16 * 128], BF16, kind="ExternalOutput").ap()
            p.dma("sp", dd, vtok[:, :], reads=tuple(f"vtok{i}" for i in range(16)), tag="dbg")
        p.op("dve", lambda e: e.memset(KVf[:, :], 0.0), writes=("KVf",))
        p.op("dve", lambda e: e.memset(ksf[:, :], 0.0), writes=("ksf",))
        for g in range(4):
            for ccl in range(4):
                cc = g * 4 + ccl
                sl = slice(cc * 128, (cc + 1) * 128)
                p.op("ten", lambda e, sl=sl: e.transpose(
                    psk[:, 0:64].bitcast(BF16), kf[:, sl], c.sb["identity"][:, :]),
                    reads=("kf", "c_identity"), writes=("psk",))
                p.op("dve", lambda e: e.tensor_copy(kftok[:, :], psk[:, 0:64].bitcast(BF16)),
                     reads=("psk",), writes=("kftok",))
                for h in range(2):
                    hsl = slice(h * 64, (h + 1) * 64)
                    p.op("ten", lambda e, h=h, sl=sl, hsl=hsl: e.matmul(
                        psp[:, h * 512:h * 512 + 128], kf[hsl, sl], qf[hsl, sl],
                        start=True, stop=True),
                        reads=("kf", "qf"), writes=("psp",))
                    p.op("dve", lambda e, h=h: e.tensor_mul(
                        smb[h][:, :], psp[:, h * 512:h * 512 + 128], c.sb["trimask"][:, :]),
                        reads=("psp", "c_trimask"), writes=(f"smb{h}",))
                    dpart = slice(0, 1) if h == 0 else slice(32, 33)
                    p.op("ten", lambda e, h=h, ccl=ccl, dpart=dpart: e.matmul(
                        psw[dpart, ccl * 128:(ccl + 1) * 128], c.sb["ones_col_bf"][:, :],
                        smb[h][:, :], start=True, stop=(cc % 16 == 0)),
                        reads=(f"smb{h}", "c_ones_col_bf"), writes=("psw",))
                    if cc % 16 != 0:
                        p.op("ten", lambda e, h=h, sl=sl, ccl=ccl, hsl=hsl, dpart=dpart: e.matmul(
                            psw[dpart, ccl * 128:(ccl + 1) * 128], kssb[hsl, 0:1],
                            qf[hsl, sl], start=False, stop=True),
                            reads=("kssb", "qf"), writes=("psw",))
                    if cc % 16 != 0:
                        p.op("ten", lambda e, h=h, sl=sl, hsl=hsl: e.matmul(
                            psq[hsl, 0:128], KVsb[hsl, :], qf[hsl, sl],
                            start=True, stop=False),
                            reads=("KVsb", "qf"), writes=("psq",))
                    p.op("ten", lambda e, h=h, cc=cc, hsl=hsl: e.matmul(
                        psq[hsl, 0:128], vtok[:, cc % 16 * 128 + h * 64:cc % 16 * 128 + (h + 1) * 64],
                        smb[h][:, :], start=(cc % 16 == 0), stop=True),
                        reads=(f"vtok{cc % 16}", f"smb{h}"), writes=("psq",))
                    p.op("ten", lambda e, h=h, cc=cc, hsl=hsl: e.matmul(
                        psv[hsl, 0:64], kftok[:, h * 64:(h + 1) * 64],
                        vtok[:, cc % 16 * 128 + h * 64:cc % 16 * 128 + (h + 1) * 64],
                        start=True, stop=False),
                        reads=("kftok", f"vtok{cc % 16}"), writes=("psv",))
                    p.op("ten", lambda e, h=h, cc=cc, hsl=hsl: e.matmul(
                        psv[hsl, 64:65], kftok[:, h * 64:(h + 1) * 64],
                        c.sb["ones_col_bf"][:, :], start=False,
                        stop=True),
                        reads=("kftok", "c_ones_col_bf"), writes=("psv",))
                p.op("dve", lambda e: e.tensor_add(KVf[:, :], KVf[:, :], psv[:, 0:64]),
                     reads=("psv", "KVf"), writes=("KVf",))
                p.op("dve", lambda e: e.tensor_add(ksf[:, :], ksf[:, :], psv[:, 64:65]),
                     reads=("psv", "ksf"), writes=("ksf",))
                p.op("dve", lambda e: e.tensor_copy(KVsb[:, :], KVf[:, :]),
                     reads=("KVf",), writes=("KVsb",))
                p.op("dve", lambda e: e.tensor_copy(kssb[:, :], ksf[:, :]),
                     reads=("ksf",), writes=("kssb",))
                if DEBUG_TAPS and hp == 0 and cc == 0:
                    ddk = nc.dram_tensor("dbg_KVf", [128, 64], F32, kind="ExternalOutput").ap()
                    p.dma("sp", ddk, KVf[:, :], reads=("KVf",), tag="dbg")
                    ddk2 = nc.dram_tensor("dbg_ksf", [128, 1], F32, kind="ExternalOutput").ap()
                    p.dma("sp", ddk2, ksf[:, :], reads=("ksf",), tag="dbg")
                    ddk3 = nc.dram_tensor("dbg_kftok", [128, 128], BF16, kind="ExternalOutput").ap()
                    p.dma("sp", ddk3, kftok[:, :], reads=("kftok",), tag="dbg")
                p.op("dve", lambda e, ccl=ccl: e.tensor_copy(
                    ctxsb[:, ccl * 128:(ccl + 1) * 128], psq[:, 0:128]),
                    reads=("psq",), writes=("ctxsb",))
            for h in range(2):
                dpart = slice(0, 1) if h == 0 else slice(32, 33)
                p.op("dve", lambda e, h=h, dpart=dpart: e.tensor_scalar_max(
                    denr[0:1, 0:512], psw[dpart, 0:512], 1e-6),
                    reads=("psw",), writes=("denr",))
                p.op("dve", lambda e: e.reciprocal(
                    denr[0:1, 0:512], denr[0:1, 0:512]),
                    reads=("denr",), writes=("denr",))
                p.op("ten", lambda e, h=h: e.matmul(
                    psk[h * 64:(h + 1) * 64, 0:512], c.sb["ones_row_f32"][0:1, 0:64],
                    denr[0:1, 0:512], start=True, stop=True),
                    reads=("denr", "c_ones_row_f32"), writes=("psk",))
            p.op("dve", lambda e, g=g, hp=hp: e.tensor_mul(
                apre[hp][:, g * 512:(g + 1) * 512], ctxsb[:, :], psk[:, 0:512]),
                reads=("ctxsb", "psk"), writes=(f"apre{hp}",))

    if DEBUG_TAPS:
        for i in range(8):
            dd = nc.dram_tensor(f"dbg_apre{i}", [128, T], BF16, kind="ExternalOutput").ap()
            p.dma("sp", dd, apre[i][:, :], reads=(f"apre{i}",), tag="dbg")
    for odp in range(4):
        xr = xrot[odp % 2]
        xk = f"xrot{odp % 2}"
        p.dma("sp", xr.rearrange("p (k t) -> p k t", k=2),
              xin_r[:, 2 * odp:2 * odp + 2, 0:T], writes=(xk,), tag="xres")
        for j in range(2):
            od = 2 * odp + j
            wdst = wocol[:, 0:1024]
            wk = "wocol"
            p.dma("pool", wdst.rearrange("p (k o) -> p k o", k=8),
                  c.d["w_out_p"][od], writes=(wk,), tag="wout")
            for tch in range(4):
                sl = slice(tch * 512, (tch + 1) * 512)
                pst = psq if tch % 2 == 0 else psk
                kt = "psq" if tch % 2 == 0 else "psk"
                for k in range(NKT):
                    p.op("ten", lambda e, k=k, sl=sl, wdst=wdst, pst=pst: e.matmul(
                        pst[:, 0:512], wdst[:, k * 128:(k + 1) * 128], apre[k][:, sl],
                        start=(k == 0), stop=False),
                        reads=(wk, f"apre{k}"), writes=(kt,))
                p.op("ten", lambda e, od=od, pst=pst: e.matmul(
                    pst[:, 0:512], borow[0:1, od * 128:(od + 1) * 128],
                    c.sb["ones_row_f32"][0:1, 0:512], start=False, stop=True),
                    reads=("borow", "c_ones_row_f32"), writes=(kt,))
                p.op("dve", lambda e, j=j, sl=sl, xr=xr, pst=pst: e.scalar_tensor_tensor(
                    xr[:, j * T + sl.start:j * T + sl.stop], pst[:, 0:512], 1.0,
                    xr[:, j * T + sl.start:j * T + sl.stop], OP.mult, OP.add),
                    reads=(kt, xk), writes=(xk,))
        p.dma("sp", xout_r[:, 2 * odp:2 * odp + 2, 0:T],
              xr.rearrange("p (k t) -> p k t", k=2), reads=(xk,), tag="xstore")
    for cm in reversed(sb_cms):
        cm.__exit__(None, None, None)


def ln_stream(c, p, x_in_r, tsl, TC, xrot, ln, rows, rbc, xbf2):
    nc = c.nc
    psA_cm = nc.psum_tensor(c.pfx + "lnpsA", [128, 2048], F32)
    psA = psA_cm.__enter__()
    psB_cm = nc.psum_tensor(c.pfx + "lnpsB", [128, 2048], F32)
    psB = psB_cm.__enter__()
    pA, pB = "psA", "psB"
    for kk in range(4):
        xr = xrot[kk % 2]
        xk = f"xrot{kk % 2}"
        p.dma("sp", xr.rearrange("p (k t) -> p k t", k=2),
              x_in_r[:, 2 * kk:2 * kk + 2, tsl], writes=(xk,), tag="lnx")
        for j in range(2):
            k = 2 * kk + j
            xb = xbf2[:, j * TC:(j + 1) * TC]
            xbk = f"xbf{j}"
            p.op("dve", lambda e, xr=xr, j=j, xb=xb: e.tensor_copy(
                xb, xr[:, j * TC:(j + 1) * TC]), reads=(xk,), writes=(xbk,))
            nch = TC // 512
            for sc in range(nch):
                sl = slice(sc * 512, (sc + 1) * 512)
                bsl = sl
                p.op("ten", lambda e, xb=xb, sl=sl, bsl=bsl, k=k: e.matmul(
                    psA[0:1, bsl], c.sb["ones_col_bf"][:, :], xb[:, sl],
                    start=(k == 0), stop=(k == NKT - 1)),
                    reads=(xbk, "c_ones_col_bf"), writes=(pA,))
            x2 = xbf2[:, (1 - j) * TC:(2 - j) * TC]
            x2k = f"xbf{1 - j}"
            p.op("dve", lambda e, xb=xb, x2=x2: e.tensor_mul(x2, xb, xb),
                 reads=(xbk,), writes=(x2k,))
            for sc in range(nch):
                sl = slice(sc * 512, (sc + 1) * 512)
                bsl = sl
                p.op("ten", lambda e, x2=x2, sl=sl, bsl=bsl, k=k: e.matmul(
                    psB[0:1, bsl], c.sb["ones_col_bf"][:, :], x2[:, sl],
                    start=(k == 0), stop=(k == NKT - 1)),
                    reads=(x2k, "c_ones_col_bf"), writes=(pB,))
    rowA, rowB, rowC, rowD = rows["A"], rows["B"], rows["C"], rows["D"]
    for sc in range(TC // 512):
        sl = slice(sc * 512, (sc + 1) * 512)
        bsl = sl
        p.op("dve", lambda e, sl=sl, bsl=bsl: e.tensor_copy(rowA[0:1, sl], psA[0:1, bsl]),
             reads=(pA,), writes=("rowA",))
        p.op("dve", lambda e, sl=sl, bsl=bsl: e.tensor_copy(rowB[0:1, sl], psB[0:1, bsl]),
             reads=(pB,), writes=("rowB",))
    p.op("dve", lambda e: e.tensor_scalar_mul(rowA[0:1, :TC], rowA[0:1, :TC], 1.0 / D),
         reads=("rowA",), writes=("rowA",))
    p.op("dve", lambda e: e.scalar_tensor_tensor(rowC[0:1, :TC], rowA[0:1, :TC], -1.0,
                                                 rowA[0:1, :TC], OP.mult, OP.mult),
         reads=("rowA",), writes=("rowC",))
    p.op("dve", lambda e: e.scalar_tensor_tensor(rowB[0:1, :TC], rowB[0:1, :TC], 1.0 / D,
                                                 rowC[0:1, :TC], OP.mult, OP.add),
         reads=("rowB", "rowC"), writes=("rowB",))
    p.op("act", lambda e: e.activation(rowC[0:1, :TC], rowB[0:1, :TC], AF.Ln,
                                       bias=c.sb["eps_col"][0:1, 0:1], scale=1.0),
         reads=("rowB", "c_eps_col"), writes=("rowC",))
    p.op("act", lambda e: e.activation(rowC[0:1, :TC], rowC[0:1, :TC], AF.Exp, scale=-0.5),
         reads=("rowC",), writes=("rowC",))
    p.op("dve", lambda e: e.scalar_tensor_tensor(rowA[0:1, :TC], rowA[0:1, :TC], -1.0,
                                                 rowC[0:1, :TC], OP.mult, OP.mult),
         reads=("rowA", "rowC"), writes=("rowA",))
    p.op("dve", lambda e: e.tensor_copy(rowD[0:1, :TC], rowA[0:1, :TC]),
         reads=("rowA",), writes=("rowD",))
    for sc in range(TC // 512):
        sl = slice(sc * 512, (sc + 1) * 512)
        bsl = sl
        p.op("ten", lambda e, sl=sl, bsl=bsl: e.matmul(
            psA[:, sc * 512:(sc + 1) * 512][:, 0:512] if False else psA[:, bsl],
            c.sb["ones_row_f32"][0:1, 0:128], rowC[0:1, sl],
            start=True, stop=True),
            reads=("rowC", "c_ones_row_f32"), writes=(pA,))
        p.op("dve", lambda e, sl=sl, bsl=bsl: e.tensor_copy(rbc[:, sl], psA[:, bsl]),
             reads=(pA,), writes=("rbc",))
    for kk in range(4):
        xr = xrot[kk % 2]
        xk = f"xrot{kk % 2}"
        p.dma("sp", xr.rearrange("p (k t) -> p k t", k=2),
              x_in_r[:, 2 * kk:2 * kk + 2, tsl], writes=(xk,), tag="lnx2")
        for j in range(2):
            k = 2 * kk + j
            p.op("dve", lambda e, xr=xr, j=j, k=k: e.tensor_mul(
                ln[k][:, :TC], xr[:, j * TC:(j + 1) * TC], rbc[:, :TC]),
                reads=(xk, "rbc"), writes=(f"ln{k}",))
    psB_cm.__exit__(None, None, None)
    psA_cm.__exit__(None, None, None)


def emit_l1_ffn(c, x_in, x_out):
    nc, p = c.nc, c.p
    TC = T
    sb_cms = []

    def alloc(name, shape, dt):
        cm = nc.sbuf_tensor(c.pfx + name, shape, dt)
        sb_cms.append(cm)
        return cm.__enter__()

    xrot = [alloc(f"xrot{i}", [128, 2 * TC], F32) for i in range(2)]
    xbf2 = alloc("xbf2", [128, 2 * TC], BF16)
    ln = [alloc(f"lnt{k}", [128, TC], BF16) for k in range(NKT)]
    rows = {"A": alloc("rowA", [1, TC], F32), "B": alloc("rowB", [1, TC], F32),
            "C": alloc("rowC", [1, TC], F32), "D": alloc("rowD", [1, TC], BF16)}
    rbc = alloc("rbc", [128, TC], F32)
    gA = alloc("gA", [128, 512], F32)
    gB = alloc("gB", [128, 512], F32)
    gT = alloc("gT", [128, 512], F32)
    Wf = [alloc(f"Wf{rt}", [128, 4 * 512], F32) for rt in range(2)]
    WB = [alloc(f"WB{i}", [128, 4 * 512], BF16) for i in range(3)]
    wsx = alloc("wsx", [1, 1024], BF16)
    bias_sb = alloc("bias_sb", [1, 1024], F32)
    p.dma("sp", bias_sb[0:1, :], c.d["bias_ri"], writes=("bias_sb",))
    xin_r = x_in.rearrange("(k p) t -> p k t", p=128)
    xout_r = x_out.rearrange("(k p) t -> p k t", p=128)

    ln_stream(c, p, xin_r, slice(0, T), TC, xrot, ln, rows, rbc, xbf2)
    ps_cms = []
    pso = []
    for i in range(2):
        cm = nc.psum_tensor(f"{c.pfx}pso{i}", [128, 512], F32)
        ps_cms.append(cm)
        pso.append(cm.__enter__())
    psB_cm2 = nc.psum_tensor(c.pfx + "psB2", [128, 512], F32)
    ps_cms.append(psB_cm2)
    psB = psB_cm2.__enter__()
    sb_cms.extend(ps_cms)

    for rt in range(2):
        g1 = c.d["G1R_x"] if rt == 0 else c.d["G1I_x"]
        g2 = c.d["G2R_x"] if rt == 0 else c.d["G2I_x"]
        for I in range(4):
            acc = Wf[rt][:, I * 512:(I + 1) * 512]
            ak = f"Wf{rt}_{I}"
            for ci in range(4):
                p.dma("sp", gA[:, :], g1[ci, I * 128:(I + 1) * 128, :], writes=("gA",))
                p.dma("sp", gB[:, :], g2[ci, I * 128:(I + 1) * 128, :], writes=("gB",))
                if ci == 0:
                    p.op("dve", lambda e, acc=acc: e.tensor_mul(acc, gA[:, :], gB[:, :]),
                         reads=("gA", "gB"), writes=(ak,))
                else:
                    p.op("dve", lambda e: e.tensor_mul(gT[:, :], gA[:, :], gB[:, :]),
                         reads=("gA", "gB"), writes=("gT",))
                    p.op("dve", lambda e, acc=acc: e.tensor_add(acc, acc, gT[:, :]),
                         reads=("gT", ak), writes=(ak,))
            p.op("dve", lambda e, acc=acc, rt=rt, I=I: e.tensor_copy(
                WB[rt][:, I * 512:(I + 1) * 512], acc),
                reads=(ak,), writes=(f"WB{rt}_{I}",))
            if rt == 1:
                p.op("dve", lambda e, acc=acc, I=I: e.tensor_scalar_mul(
                    WB[2][:, I * 512:(I + 1) * 512], acc, -1.0),
                    reads=(ak,), writes=(f"WB2_{I}",))

    for half in range(2):
        Wa = WB[0] if half == 0 else WB[1]
        Wak = "WB0" if half == 0 else "WB1"
        Wb = WB[2] if half == 0 else WB[0]
        Wbk = "WB2" if half == 0 else "WB0"
        for O in range(4):
            for I in range(4):
                p.op("ten", lambda e, Wa=Wa, I=I, O=O: e.matmul(
                    psB[0:1, 0:128], c.sb["ones_col_bf"][:, :],
                    Wa[:, I * 512 + O * 128:I * 512 + (O + 1) * 128],
                    start=(I == 0), stop=False),
                    reads=(f"{Wak}_{I}", "c_ones_col_bf"), writes=("psB",))
            for I in range(4):
                p.op("ten", lambda e, Wb=Wb, I=I, O=O: e.matmul(
                    psB[0:1, 0:128], c.sb["ones_col_bf"][:, :],
                    Wb[:, I * 512 + O * 128:I * 512 + (O + 1) * 128],
                    start=False, stop=(I == 3)),
                    reads=(f"{Wbk}_{I}", "c_ones_col_bf"), writes=("psB",))
            p.op("dve", lambda e, half=half, O=O: e.tensor_copy(
                wsx[0:1, half * 512 + O * 128:half * 512 + (O + 1) * 128], psB[0:1, 0:128]),
                reads=("psB",), writes=("wsx",))

    for odp in range(4):
        xr = xrot[odp % 2]
        xk = f"xrot{odp % 2}"
        p.dma("sp", xr.rearrange("p (k t) -> p k t", k=2),
              xin_r[:, 2 * odp:2 * odp + 2, 0:T], writes=(xk,), tag="xres")
        for j in range(2):
            od = 2 * odp + j
            half = od // 4
            O = od % 4
            Wa = WB[0] if half == 0 else WB[1]
            Wak = "WB0" if half == 0 else "WB1"
            Wb = WB[2] if half == 0 else WB[0]
            Wbk = "WB2" if half == 0 else "WB0"
            for tch in range(4):
                sl = slice(tch * 512, (tch + 1) * 512)
                pst = pso[tch % 2]
                kt = f"pso{tch % 2}"
                for I in range(4):
                    p.op("ten", lambda e, Wa=Wa, I=I, O=O, sl=sl, pst=pst: e.matmul(
                        pst[:, 0:512], Wa[:, I * 512 + O * 128:I * 512 + (O + 1) * 128],
                        ln[I][:, sl], start=(I == 0), stop=False),
                        reads=(f"{Wak}_{I}", f"ln{I}"), writes=(kt,))
                for I in range(4):
                    p.op("ten", lambda e, Wb=Wb, I=I, O=O, sl=sl, pst=pst: e.matmul(
                        pst[:, 0:512], Wb[:, I * 512 + O * 128:I * 512 + (O + 1) * 128],
                        ln[4 + I][:, sl], start=False, stop=False),
                        reads=(f"{Wbk}_{I}", f"ln{4 + I}"), writes=(kt,))
                p.op("ten", lambda e, od=od, sl=sl, pst=pst: e.matmul(
                    pst[:, 0:512], wsx[0:1, od * 128:(od + 1) * 128], rows["D"][0:1, sl],
                    start=False, stop=False),
                    reads=("wsx", "rowD"), writes=(kt,))
                p.op("ten", lambda e, od=od, pst=pst: e.matmul(
                    pst[:, 0:512], bias_sb[0:1, od * 128:(od + 1) * 128],
                    c.sb["ones_row_f32"][0:1, 0:512], start=False, stop=True),
                    reads=("bias_sb", "c_ones_row_f32"), writes=(kt,))
                p.op("dve", lambda e, j=j, sl=sl, xr=xr, pst=pst: e.scalar_tensor_tensor(
                    xr[:, j * TC + sl.start:j * TC + sl.stop], pst[:, 0:512], 0.5,
                    xr[:, j * TC + sl.start:j * TC + sl.stop], OP.mult, OP.add),
                    reads=(kt, xk), writes=(xk,))
        p.dma("sp", xout_r[:, 2 * odp:2 * odp + 2, 0:T],
              xr.rearrange("p (k t) -> p k t", k=2), reads=(xk,), tag="xstore")
    for cm in reversed(sb_cms):
        cm.__exit__(None, None, None)


def emit_l1_attn(c, x_in, x_out):
    nc, p = c.nc, c.p
    sb_cms = []

    def alloc(name, shape, dt):
        cm = nc.sbuf_tensor(c.pfx + name, shape, dt)
        sb_cms.append(cm)
        return cm.__enter__()

    abf = alloc("abf", [128, T], BF16)
    ktl = alloc("ktl", [128, T], BF16)
    xrot = [alloc(f"xrot{i}", [128, 2 * T], F32) for i in range(2)]
    xbf2 = alloc("xbf2", [128, 2 * T], BF16)
    ln = [alloc(f"lnt{k}", [128, T], BF16) for k in range(NKT)]
    rows = {"A": alloc("rowA", [1, T], F32), "B": alloc("rowB", [1, T], F32),
            "C": alloc("rowC", [1, T], F32), "D": alloc("rowD", [1, T], BF16)}
    rbc = alloc("rbc", [128, T], F32)
    hpre = [alloc(f"hpre{i}", [128, T], BF16) for i in range(8)]
    ksb = alloc("ksb", [128, T], F32)
    usb = alloc("usb", [128, T], F32)
    qsb = alloc("qsb", [128, T], F32)
    Lsb = alloc("Lsb", [128, T + 16], F32)
    ktl2 = ksb[:, :].bitcast(BF16)[:, 0:T]
    a2bf = ksb[:, :].bitcast(BF16)[:, T:2 * T]
    duqb = Lsb[:, :].bitcast(BF16)[:, 0:T]
    bbf = alloc("bbf", [128, T], BF16)
    vfm = alloc("vfm", [128, T], BF16)
    dec = alloc("dec", [128, 128], F32)
    vBDp = alloc("vBDp", [128, 128], BF16)
    bBDp = alloc("bBDp", [128, 128], BF16)
    smp = alloc("smp", [128, 16], BF16)
    KVbd = alloc("KVbd", [128, 128], F32)
    KVbds = alloc("KVbds", [128, 128], BF16)
    wqcol = alloc("wqcol", [128, 2 * 1024], BF16)
    wsq = alloc("wsq", [1, 256], BF16)
    wocol = alloc("wocol", [128, 1024], BF16)

    def palloc(name, shape, dt=F32):
        cm = nc.psum_tensor(c.pfx + name, shape, dt)
        sb_cms.append(cm)
        return cm.__enter__()

    xin_r = x_in.rearrange("(k p) t -> p k t", p=128)
    xout_r = x_out.rearrange("(k p) t -> p k t", p=128)

    fw1 = rbc
    af32 = xbf2[:, :].bitcast(F32)
    brow = rows["B"][0:1, 0:D]
    borow = rows["B"][0:1, D:2 * D]

    for i in range(8):
        p.op("dve", lambda e, i=i: e.memset(hpre[i][:, :], 0.0), writes=(f"hpre{i}",))
    p.op("dve", lambda e: e.memset(vfm[:, :], 0.0), writes=("vfm",))
    p.op("dve", lambda e: e.memset(abf[:, :], 0.0), writes=("abf",))
    p.op("dve", lambda e: e.memset(ktl[:, :], 0.0), writes=("ktl",))
    p.op("dve", lambda e: e.memset(bbf[:, :], 0.0), writes=("bbf",))

    ln_stream(c, p, xin_r, slice(0, T), T, xrot, ln, rows, rbc, xbf2)
    p.dma("sp", brow, c.d["key_b"], reads=("rowB",), writes=("brow",))
    p.dma("sp", borow, c.d["out_b"], reads=("rowB",), writes=("borow",))

    pA = palloc("pA", [128, 1024])
    pB = palloc("pB", [128, 1024])
    pC = palloc("pC", [128, 512])
    pD = palloc("pD", [128, 512])
    pE = palloc("pE", [128, 512])

    for hp in range(8):
        _phA = [] if L1_COPS in (2.95,) else list(enumerate(
                (("key_w_p", hp, pA, "pA", 0),
                 ("vq_w_p", 8 + hp, pB, "pB", None),
                 ("vq_w_p", hp, pC, "pC", None))))
        for w_i, (wtens, wname, pst, kt, boff) in _phA:
            wdst = wqcol[:, (w_i % 2) * 1024:(w_i % 2 + 1) * 1024]
            wk = f"wqcol{w_i % 2}"
            p.dma("pool", wdst.rearrange("p (k o) -> p k o", k=8),
                  c.d[wtens][wname], writes=(wk,), tag="wl1")
            wb = w_i % 2
            for k in range(NKT):
                p.op("ten", lambda e, k=k, wdst=wdst: e.matmul(
                    pD[0:1, 0:128], c.sb["ones_col_bf"][:, :], wdst[:, k * 128:(k + 1) * 128],
                    start=(k == 0), stop=(k == NKT - 1)),
                    reads=(wk, "c_ones_col_bf"), writes=("pD",))
            p.op("dve", lambda e, wb=wb: e.tensor_copy(wsq[0:1, wb * 128:(wb + 1) * 128],
                                                       pD[0:1, 0:128]),
                 reads=("pD",), writes=(f"wsq{wb}",))
            for sc in range(4):
                sl = slice(sc * 512, (sc + 1) * 512)
                psl = slice((sc % 2) * 512, (sc % 2 + 1) * 512) if w_i < 2 else slice(0, 512)
                for k in range(NKT):
                    p.op("ten", lambda e, k=k, sl=sl, psl=psl, wdst=wdst, pst=pst: e.matmul(
                        pst[:, psl], wdst[:, k * 128:(k + 1) * 128], ln[k][:, sl],
                        start=(k == 0), stop=False),
                        reads=(wk, f"ln{k}"), writes=(kt,))
                p.op("ten", lambda e, sl=sl, psl=psl, pst=pst, wb=wb, boff=boff: e.matmul(
                    pst[:, psl], wsq[0:1, wb * 128:(wb + 1) * 128], rows["D"][0:1, sl],
                    start=False, stop=(boff is None)),
                    reads=(f"wsq{wb}", "rowD"), writes=(kt,))
                if boff is not None:
                    p.op("ten", lambda e, sl=sl, psl=psl, pst=pst, boff=boff, hp=hp: e.matmul(
                        pst[:, psl], brow[0:1, hp * 128:(hp + 1) * 128],
                        c.sb["ones_row_f32"][0:1, 0:512], start=False, stop=True),
                        reads=("brow", "c_ones_row_f32"), writes=(kt,))
                if w_i == 0 and sc % 2 == 1:
                    ch2 = sc // 2
                    p.op("act", lambda e, ch2=ch2, pst=pst: e.activation(
                        ksb[:, ch2 * 1024:(ch2 + 1) * 1024], pst[:, :], AF.Sigmoid),
                        reads=(kt,), writes=("ksb",))
                elif w_i == 1 and sc % 2 == 1:
                    ch2 = sc // 2
                    p.op("act", lambda e, ch2=ch2, pst=pst: e.activation(
                        qsb[:, ch2 * 1024:(ch2 + 1) * 1024], pst[:, :], AF.Identity),
                        reads=(kt,), writes=("qsb",))
                elif w_i == 2:
                    p.op("act", lambda e, sc=sc, pst=pst: e.activation(
                        vfm[:, sc * 512:(sc + 1) * 512], pst[:, :], AF.Identity),
                        reads=(kt,), writes=("vfm",))

        if L1_PHASES < 2:
            continue
        if L1_COPS in (2.9, 2.95):
            p.op("dve", lambda e: e.tensor_copy(ktl[:, :], ln[0][:, :]),
                 reads=("ln0",), writes=("ktl",))
            p.op("dve", lambda e: e.tensor_copy(abf[:, :], ln[1][:, :]),
                 reads=("ln1",), writes=("abf",))
            p.op("dve", lambda e: e.tensor_copy(bbf[:, :], ln[2][:, :]),
                 reads=("ln2",), writes=("bbf",))
            p.op("dve", lambda e: e.memset(dec[:, :], 1.0), writes=("dec",))
            p.op("dve", lambda e: e.memset(KVf[:, :], 0.0), writes=("KVf",))
            p.op("dve", lambda e: e.memset(KVsb[:, :], 0.0), writes=("KVsb",))
            p.op("dve", lambda e: e.memset(vtok16[:, :], 0.0), writes=("vtok16",))
            p.op("dve", lambda e: e.memset(btok16[:, :], 0.0), writes=("btok16",))
            p.op("dve", lambda e: e.memset(smbf[:, :], 0.0), writes=("smbf",))
            p.op("dve", lambda e: e.memset(qsb[:, :], 0.0), writes=("qsb",))
        else:
            if True:
                pass
        if L1_COPS not in (2.9, 2.95):
            p.op("dve", lambda e: e.tensor_scalar(usb[:, :], ksb[:, :], -1.0, 1.0,
                                                  OP.mult, OP.add),
                 reads=("ksb",), writes=("usb",))
            p.op("dve", lambda e: e.tensor_mul(abf[:, :], usb[:, :], qsb[:, :]),
                 reads=("usb", "qsb"), writes=("abf",))
            _duq_tiles = ((pB, "pB", 0), (pB, "pB", 512), (pC, "pC", 0), (pD, "pD", 0))
            for i, (bank, bkey, off) in enumerate(_duq_tiles):
                p.op("ten", lambda e, bank=bank, off=off, i=i: e.matmul(
                    bank[:, off:off + 512], c.sb["blockM"][:, :],
                    abf[:, i * 512:(i + 1) * 512], start=True, stop=True),
                    reads=("c_blockM", "abf"), writes=(bkey,))
            p.op("dve", lambda e: e.tensor_scalar_max(ksb[:, :], ksb[:, :], 1e-38),
                 reads=("ksb",), writes=("ksb",))
            p.op("act", lambda e: e.activation(ksb[:, :], ksb[:, :], AF.Ln),
                 reads=("ksb",), writes=("ksb",))
            p.op("dve", lambda e: e.memset(fw1[:, :], 0.0), writes=("rbc",))
            p.op("dve", lambda e: e.memset(Lsb[:, 0:16], 0.0), writes=("Lsb",))
            p.op("dve", lambda e: e.tensor_tensor_scan(Lsb[:, 1:T + 1], ksb[:, :], fw1[:, :],
                                                       0.0, OP.add, OP.add),
                 reads=("ksb", "rbc"), writes=("Lsb",))
            pivA = Lsb[:, 0:T:16].unsqueeze(2).broadcast_to([128, T // 16, 16])
            pivB = Lsb[:, 16:T + 16:16].unsqueeze(2).broadcast_to([128, T // 16, 16])
            Lv = Lsb[:, 1:T + 1].rearrange("p (a b) -> p a b", b=16)
            p.op("dve", lambda e: e.tensor_tensor(fw1[:, :].rearrange("p (a b) -> p a b", b=16),
                                                  Lv, pivA, OP.subtract),
                 reads=("Lsb",), writes=("rbc",))
            p.op("dve", lambda e: e.tensor_scalar(af32[:, :], fw1[:, :], 43.0, 0.0,
                                                  OP.add, OP.min),
                 reads=("rbc",), writes=("afx", "xbf0", "xbf1"))
            p.op("act", lambda e: e.activation(af32[:, :], af32[:, :], AF.Exp),
                 reads=("afx",), writes=("afx",))
            p.op("dve", lambda e: e.tensor_mul(a2bf[:, :], qsb[:, :], af32[:, :]),
                 reads=("qsb", "afx"), writes=("ksb",))
            p.op("act", lambda e: e.activation(af32[:, :], fw1[:, :], AF.Exp),
                 reads=("rbc",), writes=("afx",))
            p.op("dve", lambda e: e.tensor_mul(qsb[:, :], qsb[:, :], af32[:, :]),
                 reads=("qsb", "afx"), writes=("qsb",))
            p.op("dve", lambda e: e.tensor_copy(abf[:, :], qsb[:, :]),
                 reads=("qsb",), writes=("abf",))
            p.op("dve", lambda e: e.tensor_scalar(af32[:, :], fw1[:, :], -1.0, -43.0,
                                                  OP.mult, OP.add),
                 reads=("rbc",), writes=("afx",))
            p.op("dve", lambda e: e.tensor_scalar_min(fw1[:, :], af32[:, :], 43.0),
                 reads=("afx",), writes=("rbc",))
            p.op("act", lambda e: e.activation(fw1[:, :], fw1[:, :], AF.Exp),
                 reads=("rbc",), writes=("rbc",))
            p.op("dve", lambda e: e.tensor_scalar_min(af32[:, :], af32[:, :], 0.0),
                 reads=("afx",), writes=("afx",))
            p.op("act", lambda e: e.activation(af32[:, :], af32[:, :], AF.Exp),
                 reads=("afx",), writes=("afx",))
            p.op("dve", lambda e: e.scalar_tensor_tensor(
                ktl[:, :], af32[:, :], 4.727392e18, usb[:, :], OP.mult, OP.mult),
                reads=("afx", "usb"), writes=("ktl",))
            p.op("dve", lambda e: e.tensor_sub(fw1[:, :], fw1[:, :], af32[:, :]),
                 reads=("rbc", "afx"), writes=("rbc",))
            p.op("dve", lambda e: e.tensor_mul(ktl2[:, :], usb[:, :], fw1[:, :]),
                 reads=("usb", "rbc"), writes=("ksb",))
            p.op("dve", lambda e: e.tensor_tensor(fw1[:, :].rearrange("p (a b) -> p a b", b=16),
                                                  pivB, Lv, OP.subtract),
                 reads=("Lsb",), writes=("rbc",))
            p.op("act", lambda e: e.activation(fw1[:, :], fw1[:, :], AF.Exp),
                 reads=("rbc",), writes=("rbc",))
            p.op("dve", lambda e: e.tensor_mul(bbf[:, :], usb[:, :], fw1[:, :]),
                 reads=("usb", "rbc"), writes=("bbf",))
            p.op("dve", lambda e: e.tensor_tensor(dec[:, :], Lsb[:, 16:T + 16:16],
                                                  Lsb[:, 0:T:16], OP.subtract),
                 reads=("Lsb",), writes=("dec",))
            p.op("act", lambda e: e.activation(dec[:, :], dec[:, :], AF.Exp),
                 reads=("dec",), writes=("dec",))
            for i, (bank, bkey, off) in enumerate(_duq_tiles):
                p.op("dve", lambda e, bank=bank, off=off, i=i: e.tensor_copy(
                    duqb[:, i * 512:(i + 1) * 512], bank[:, off:off + 512]),
                    reads=(bkey,), writes=("Lsb",))

            if DEBUG_TAPS and hp == 0:
                for nm, buf in (("ktl", ktl), ("abf", abf)):
                    dd = nc.dram_tensor("dbg_" + nm, [128, T], BF16, kind="ExternalOutput").ap()
                    p.dma("sp", dd, buf[:, :], reads=(nm,), tag="dbg")

        if L1_PHASES < 3 or hp >= L1_NHP:
            continue
        p.op("dve", lambda e: e.memset(vBDp[:, :], 0.0), writes=("vBDp",))
        p.op("dve", lambda e: e.memset(bBDp[:, :], 0.0), writes=("bBDp",))
        p.op("dve", lambda e: e.memset(smp[:, :], 0.0), writes=("smp",))
        p.op("dve", lambda e: e.memset(KVbd[:, :], 0.0), writes=("KVbd",))
        p.op("dve", lambda e: e.memset(KVbds[:, :], 0.0), writes=("KVbds",))
        B0 = xrot[0][:, :].bitcast(BF16)
        B1 = xrot[1][:, :].bitcast(BF16)
        p.op("dve", lambda e: e.memset(B0[:, :], 0.0), writes=("xrot0",))
        p.op("dve", lambda e: e.memset(B1[:, :], 0.0), writes=("xrot1",))
        for (srcb, srck, pr, dst, lo) in (
                (ktl, "ktl", slice(0, 64), B0[0:64, 0:4096], 0),
                (ktl, "ktl", slice(64, 128), B0[64:128, 0:4096], 16),
                (ktl2, "ksb", slice(0, 64), B0[0:64, 4096:8192], 0),
                (ktl2, "ksb", slice(64, 128), B0[64:128, 4096:8192], 16)):
            p.op("dve", lambda e, srcb=srcb, pr=pr, dst=dst, lo=lo: e.tensor_copy(
                dst.rearrange("p (c u) -> p c u", u=32)[:, :, lo:lo + 16],
                srcb[pr, :].rearrange("p (c s) -> p c s", s=16)),
                reads=(srck,), writes=("xrot0",))
        for (srcb, srck, pr, lo) in ((vfm, "vfm", slice(0, 64), 0),
                                     (vfm, "vfm", slice(64, 128), 16),
                                     (bbf, "bbf", slice(0, 64), 32),
                                     (bbf, "bbf", slice(64, 128), 48)):
            p.op("dve", lambda e, srcb=srcb, pr=pr, lo=lo: e.tensor_copy(
                B1[pr, :].rearrange("p (c u) -> p c u", u=64)[:, :, lo:lo + 16],
                srcb[pr, :].rearrange("p (c s) -> p c s", s=16)),
                reads=(srck,), writes=("xrot1",))
        for cc in range(min(T // 16, L1_NCHUNK)):
            sl = slice(cc * 16, (cc + 1) * 16)
            g8 = cc % 8
            p.op("ten", lambda e, cc=cc: e.matmul(pE[0:64, 0:128], B1[:, cc * 64:(cc + 1) * 64],
                                           c.sb["identity"][:, :], start=True, stop=True),
                 reads=("xrot1", "c_identity"), writes=("pE",))
            p.op("dve", lambda e: e.tensor_copy(vBDp[0:32, :], pE[0:32, 0:128]),
                 reads=("pE",), writes=("vBDp",))
            p.op("dve", lambda e: e.tensor_copy(bBDp[0:32, :], pE[32:64, 0:128]),
                 reads=("pE",), writes=("bBDp",))
            p.op("ten", lambda e, sl=sl, cc=cc: e.matmul(
                pD[0:32, 0:16], B0[:, cc * 32:(cc + 1) * 32],
                abf[:, sl], start=True, stop=False),
                reads=("xrot0", "abf"), writes=("pD",))
            p.op("ten", lambda e, sl=sl, cc=cc: e.matmul(
                pD[0:32, 0:16], B0[:, 4096 + cc * 32:4096 + (cc + 1) * 32],
                a2bf[:, sl], start=False, stop=True),
                reads=("xrot0", "ksb"), writes=("pD",))
            p.op("dve", lambda e: e.select(smp[0:32, 0:16], c.sb["selstrict32"][:, :],
                                           pD[0:32, 0:16], c.sb["zeros3216"][:, :]),
                 reads=("pD", "c_selstrict32", "c_zeros3216"), writes=("smp",))
            if cc > 0:
                p.op("ten", lambda e, sl=sl, g8=g8: e.matmul(
                    pA[:, g8 * 16:(g8 + 1) * 16], KVbds[:, :], abf[:, sl],
                    start=True, stop=False),
                    reads=("KVbds", "abf"), writes=("pA",))
            p.op("ten", lambda e, cc=cc, g8=g8: e.matmul(
                pA[:, g8 * 16:(g8 + 1) * 16], vBDp[:, :], smp[:, 0:16],
                start=(cc == 0), stop=True),
                reads=("vBDp", "smp"), writes=("pA",))
            p.op("ten", lambda e: e.matmul(pC[:, 0:128], bBDp[:, :], vBDp[:, :],
                                           start=True, stop=True),
                 reads=("bBDp", "vBDp"), writes=("pC",))
            p.op("dve", lambda e, cc=cc: e.scalar_tensor_tensor(
                KVbd[:, :], KVbd[:, :], dec[:, cc:cc + 1], pC[:, 0:128],
                OP.mult, OP.add),
                reads=("KVbd", "dec", "pC"), writes=("KVbd",))
            p.op("dve", lambda e: e.tensor_copy(KVbds[:, :], KVbd[:, :]),
                 reads=("KVbd",), writes=("KVbds",))
            if g8 == 7 or cc == min(T // 16, L1_NCHUNK) - 1:
                gi = cc // 8
                gsl = slice(gi * 128, gi * 128 + 128)
                p.op("dve", lambda e, gsl=gsl, hp=hp: e.tensor_mul(
                    hpre[hp][:, gsl], vfm[:, gsl], duqb[:, gsl]),
                    reads=("vfm", "Lsb"), writes=(f"hpre{hp}",))
                p.op("dve", lambda e, gsl=gsl, hp=hp: e.scalar_tensor_tensor(
                    hpre[hp][:, gsl], pA[:, 0:128], 1.0, hpre[hp][:, gsl],
                    OP.mult, OP.add),
                    reads=("pA", f"hpre{hp}"), writes=(f"hpre{hp}",))

    for odp in range(4):
        xr = xrot[odp % 2]
        xk = f"xrot{odp % 2}"
        p.dma("sp", xr.rearrange("p (k t) -> p k t", k=2),
              xin_r[:, 2 * odp:2 * odp + 2, 0:T], writes=(xk,), tag="xres")
        for j in range(2):
            od = 2 * odp + j
            wdst = wocol[:, 0:1024]
            wk = "wocol"
            p.dma("pool", wdst.rearrange("p (k o) -> p k o", k=8),
                  c.d["out_w_p"][od], writes=(wk,), tag="wout")
            for tch in range(4):
                sl = slice(tch * 512, (tch + 1) * 512)
                pst = pA if tch % 2 == 0 else pB
                kt = "pA" if tch % 2 == 0 else "pB"
                for k in range(NKT):
                    p.op("ten", lambda e, k=k, sl=sl, wdst=wdst, pst=pst: e.matmul(
                        pst[:, 0:512], wdst[:, k * 128:(k + 1) * 128], hpre[k][:, sl],
                        start=(k == 0), stop=False),
                        reads=(wk, f"hpre{k}"), writes=(kt,))
                p.op("ten", lambda e, od=od, pst=pst: e.matmul(
                    pst[:, 0:512], rows["B"][0:1, D + od * 128:D + (od + 1) * 128],
                    c.sb["ones_row_f32"][0:1, 0:512], start=False, stop=True),
                    reads=("borow", "c_ones_row_f32"), writes=(kt,))
                p.op("dve", lambda e, j=j, sl=sl, xr=xr, pst=pst, kt=kt, xk=xk: e.scalar_tensor_tensor(
                    xr[:, j * T + sl.start:j * T + sl.stop], pst[:, 0:512], 1.0,
                    xr[:, j * T + sl.start:j * T + sl.stop], OP.mult, OP.add),
                    reads=(kt, xk), writes=(xk,))
        p.dma("sp", xout_r[:, 2 * odp:2 * odp + 2, 0:T],
              xr.rearrange("p (k t) -> p k t", k=2), reads=(xk,), tag="xstore")
    for cm in reversed(sb_cms):
        cm.__exit__(None, None, None)

EMITTERS = {0: emit_l0_ffn, 1: emit_l0_attn, 2: emit_l0_ffn,
            3: emit_l1_ffn, 4: emit_l1_attn, 5: emit_l1_ffn}

_NC_CACHE = {}


def kernel(**inputs):
    g = prep_inputs(inputs)
    x = np.asarray(inputs["L0_attn_x"])
    xT = [np.ascontiguousarray(x[b].T) for b in range(8)]

    for group in SUBLAYER_GROUPS:
        key = tuple(group)
        if key not in _NC_CACHE:
            _NC_CACHE[key] = build_program(group)
        nc = _NC_CACHE[key]
        per_core = []
        for b in range(8):
            m = dict(g)
            m["xT"] = xT[b]
            per_core.append(m)
        res = run_bass_kernel_spmd(nc, per_core, core_ids=list(range(8)))
        xT = [np.ascontiguousarray(res.results[b]["out"]) for b in range(8)]

    return np.stack([o.T for o in xT], axis=0).astype(np.float32)


# version 08 (final)
# speedup vs baseline: 1.1408x; 1.0159x over previous
import numpy as np
import ml_dtypes
import concourse.bass as bass
import concourse.mybir as mybir
from concourse.bass_utils import run_bass_kernel_spmd

F32 = mybir.dt.float32
BF16 = mybir.dt.bfloat16
AF = mybir.ActivationFunctionType
OP = mybir.AluOpType
BF = ml_dtypes.bfloat16

D = 1024
T = 2048
NKT = D // 128

SUBLAYER_GROUPS = [[0, 1, 2, 3, 4, 5]]
DEBUG_TAPS = False
L1_PHASES = 99
L1_NCHUNK = 128
L1_NHP = 8
L1_COPS = 9


ENGS = ("ten", "act", "dve", "pool", "sp")
DMA_Q = ("pool", "sp")


class Op:
    __slots__ = ("eng", "fn", "reads", "writes", "idx", "needs_inc", "seq",
                 "waits", "tag", "_deps")

    def __init__(self, eng, fn, reads, writes, tag=""):
        self.eng = eng
        self.fn = fn
        self.reads = tuple(reads)
        self.writes = tuple(writes)
        self.needs_inc = False
        self.seq = 0
        self.waits = []
        self.tag = tag
        self._deps = []


class Prog:
    def __init__(self):
        self.items = []
        self.per_eng = {e: [] for e in ENGS}

    def op(self, eng, fn, reads=(), writes=(), tag=""):
        o = Op(eng, fn, reads, writes, tag or "op")
        o.tag = f"{o.tag}|{len(self.per_eng[eng])}"
        self.per_eng[eng].append(o)
        self.items.append(o)
        return o

    def dma(self, q, out, in_, reads=(), writes=(), tag=""):
        assert q in DMA_Q
        return self.op(q, lambda e, out=out, in_=in_: e.dma_start(out=out, in_=in_),
                       reads=reads, writes=writes, tag=tag)

    def barrier(self):
        self.items.append("BARRIER")

    def schedule(self):
        last_w = {}
        readers = {}
        eng_last = {e: None for e in ENGS}
        pending_barrier = {}
        for it in self.items:
            if it == "BARRIER":
                prods = [op for op in eng_last.values() if op is not None]
                pending_barrier = {e: [pp for pp in prods if pp.eng != e] for e in ENGS}
                last_w = {}
                readers = {}
                continue
            o = it
            deps = list(pending_barrier.pop(o.eng, ()))
            for b in o.reads:
                w = last_w.get(b)
                if w is not None and w.eng != o.eng:
                    deps.append(w)
            for b in o.writes:
                w = last_w.get(b)
                if w is not None and w.eng != o.eng:
                    deps.append(w)
                for reng, rop in readers.get(b, {}).items():
                    if reng != o.eng:
                        deps.append(rop)
            for p in deps:
                p.needs_inc = True
            o._deps = deps
            for b in o.reads:
                readers.setdefault(b, {})[o.eng] = o
            for b in o.writes:
                last_w[b] = o
                readers[b] = {}
            eng_last[o.eng] = o
        cnt = {e: 0 for e in ENGS}
        for it in self.items:
            if it == "BARRIER":
                continue
            if it.eng in DMA_Q:
                it.needs_inc = True
            if it.needs_inc:
                cnt[it.eng] += 1
            it.seq = cnt[it.eng]
        waited = {}
        for it in self.items:
            if it == "BARRIER":
                continue
            need = {}
            for p in it._deps:
                need[p.eng] = max(need.get(p.eng, 0), p.seq)
            for peng in sorted(need):
                seq = need[peng]
                key = (it.eng, peng)
                if waited.get(key, 0) < seq:
                    it.waits.append((peng, seq))
                    waited[key] = seq

    def emit(self, nc):
        self.schedule()
        sems = {}
        ctxs = []
        for e in ENGS:
            cm = nc.semaphore("sem_" + e)
            ctxs.append(cm)
            sems[e] = cm.__enter__()
        for q in DMA_Q:
            cm = nc.semaphore("raw_" + q)
            ctxs.append(cm)
            sems["raw_" + q] = cm.__enter__()
        blk_cm = nc.Block()
        block = blk_cm.__enter__()

        def run_engine(eng_obj, eng_name):
            is_q = eng_name in DMA_Q
            for o in self.per_eng[eng_name]:
                for peng, seq in o.waits:
                    eng_obj.wait_ge(sems[peng], seq)
                instr = o.fn(eng_obj)
                instr.annotate(o.tag)
                if o.needs_inc:
                    if is_q:
                        instr.then_inc(sems["raw_" + eng_name], 16)
                        eng_obj.wait_ge(sems["raw_" + eng_name], o.seq * 16)
                        eng_obj.sem_inc(sems[eng_name], 1)
                    else:
                        instr.then_inc(sems[o.eng], 1)

        block.tensor(lambda e: run_engine(e, "ten"))
        block.scalar(lambda e: run_engine(e, "act"))
        block.vector(lambda e: run_engine(e, "dve"))
        block.gpsimd(lambda e: run_engine(e, "pool"))

        def run_sync(e):
            run_engine(e, "sp")
            for en in ("pool", "ten", "act", "dve"):
                tot = sum(1 for o in self.per_eng[en] if o.needs_inc)
                if tot:
                    e.wait_ge(sems[en], tot)

        block.sync(run_sync)
        blk_cm.__exit__(None, None, None)
        for cm in reversed(ctxs):
            cm.__exit__(None, None, None)


def _pack_cols(wT, I, O):
    a = np.ascontiguousarray(np.asarray(wT, dtype=np.float32))
    assert a.shape == (I, O)
    return np.ascontiguousarray(
        a.reshape(I // 128, 128, O // 128, 128).transpose(2, 1, 0, 3).astype(BF))


def _cols(a):
    a = np.asarray(a, dtype=np.float32)
    return np.ascontiguousarray(a.reshape(-1, 128).T)


def prep_inputs(kw):
    g = {}
    g["w_qkv_p"] = _pack_cols(np.asarray(kw["L0_attn_w_qkv"]).T, D, 3 * D)
    g["b_qkv"] = np.asarray(kw["L0_attn_b_qkv"], np.float32).reshape(1, -1)
    g["w_out_p"] = _pack_cols(np.asarray(kw["L0_attn_w_out"]).T, D, D)
    g["b_out"] = np.asarray(kw["L0_attn_b_out"], np.float32).reshape(1, -1)
    om = np.asarray(kw["L0_attn_omega"], np.float32)
    g["omegaT"] = np.ascontiguousarray(np.transpose(om, (1, 2, 0, 3)).reshape(16, 64, 64))
    g["qnodes2"] = np.asarray(kw["L0_attn_quad_nodes"], np.float32).reshape(2, 1)
    g["qweights2"] = np.asarray(kw["L0_attn_quad_weights"], np.float32).reshape(2, 1)
    g["w1_p"] = _pack_cols(np.asarray(kw["L0_ffn_w1_w"]).T, D, 4096)
    g["w2_p"] = _pack_cols(np.asarray(kw["L0_ffn_w2_w"]).T, D, 4096)
    g["h1_p"] = _pack_cols(np.asarray(kw["L0_ffn_h1_w"]).T, 4096, 4096)
    g["h2_p"] = _pack_cols(np.asarray(kw["L0_ffn_h2_w"]).T, 4096, 4096)
    g["w3_p"] = _pack_cols(np.asarray(kw["L0_ffn_w3_w"]).T, 4096, D)
    for s in ("w1_s", "w2_s", "h1_s", "h2_s", "w3_s"):
        g[s] = _cols(kw["L0_ffn_" + s])
    g["key_w_p"] = _pack_cols(np.asarray(kw["L1_attn_key_w"]).T, D, D)
    g["key_b"] = np.asarray(kw["L1_attn_key_b"], np.float32).reshape(1, -1)
    g["vq_w_p"] = _pack_cols(np.asarray(kw["L1_attn_vq_w"]).T, D, 2 * D)
    g["out_w_p"] = _pack_cols(np.asarray(kw["L1_attn_out_w"]).T, D, D)
    g["out_b"] = np.asarray(kw["L1_attn_out_b"], np.float32).reshape(1, -1)
    G1R = np.asarray(kw["L1_ffn_G1_R"], np.float32)[0]
    G2R = np.asarray(kw["L1_ffn_G2_R"], np.float32)[:, :, 0, :]
    G1I = np.asarray(kw["L1_ffn_G1_I"], np.float32)[0]
    G2I = np.asarray(kw["L1_ffn_G2_I"], np.float32)[:, :, 0, :]

    def expand(G1, G2):
        A = np.broadcast_to(G1.transpose(1, 2, 0)[:, :, None, :, None], (4, 32, 16, 32, 16))
        A = np.ascontiguousarray(A.reshape(4, 512, 512))
        B = np.broadcast_to(G2.transpose(0, 2, 1)[:, None, :, None, :], (4, 32, 16, 32, 16))
        B = np.ascontiguousarray(B.reshape(4, 512, 512))
        return A, B
    g["G1R_x"], g["G2R_x"] = expand(G1R, G2R)
    g["G1I_x"], g["G2I_x"] = expand(G1I, G2I)
    g["bias_ri"] = np.concatenate([np.asarray(kw["L1_ffn_bias_real"]),
                                   np.asarray(kw["L1_ffn_bias_imag"])]).astype(np.float32).reshape(1, -1)

    g["trimask"] = np.triu(np.ones((128, 128), np.float32)).astype(BF)
    g["identity"] = np.eye(128, dtype=np.float32).astype(BF)
    g["ones_col_f32"] = np.ones((128, 1), np.float32)
    g["ones_col_bf"] = np.ones((128, 1), np.float32).astype(BF)
    g["ones_row_bf"] = np.ones((1, 512), np.float32).astype(BF)
    g["ones_row_f32"] = np.ones((1, 1024), np.float32)
    g["eps_col"] = np.full((1, 1), 1e-5, np.float32)
    tri16 = np.triu(np.ones((16, 16), np.float32)).astype(np.uint8)
    g["seltri16"] = np.concatenate([tri16, tri16], axis=1)
    g["zeros1632"] = np.zeros((16, 32), np.float32)
    g["seltri32"] = np.concatenate([tri16, tri16], axis=0)
    g["zeros3216"] = np.zeros((32, 16), np.float32)
    tri16s = np.triu(np.ones((16, 16), np.float32), 1).astype(np.uint8)
    g["selstrict32"] = np.concatenate([tri16s, tri16s], axis=0)
    bM = np.zeros((128, 128), np.float32)
    bM[:64, :64] = 1.0
    bM[64:, 64:] = 1.0
    g["blockM"] = bM.astype(BF)
    bo2 = np.zeros((128, 2), np.float32)
    bo2[:64, 0] = 1.0
    bo2[64:, 1] = 1.0
    g["blockones2"] = bo2.astype(BF)
    g["eps30"] = np.full((128, 1), 1e-30, np.float32)
    ind2 = np.zeros((2, 128), np.float32)
    ind2[0, :64] = 1.0
    ind2[1, 64:] = 1.0
    g["ind2"] = ind2
    sel2 = np.zeros((2, 128), np.float32)
    rr = (np.arange(128) // 32) % 2
    sel2[0, rr == 0] = 1.0
    sel2[1, rr == 1] = 1.0
    g["sel2"] = sel2
    return g


DEV_INPUTS = [
    ("xT", [D, T], F32),
    ("w_qkv_p", [24, 128, 8, 128], BF16), ("b_qkv", [1, 3 * D], F32),
    ("w_out_p", [8, 128, 8, 128], BF16), ("b_out", [1, D], F32),
    ("omegaT", [16, 64, 64], F32), ("qnodes2", [2, 1], F32), ("qweights2", [2, 1], F32),
    ("w1_p", [32, 128, 8, 128], BF16), ("w2_p", [32, 128, 8, 128], BF16),
    ("h1_p", [32, 128, 32, 128], BF16), ("h2_p", [32, 128, 32, 128], BF16),
    ("w3_p", [8, 128, 32, 128], BF16),
    ("w1_s", [128, 32], F32), ("w2_s", [128, 32], F32),
    ("h1_s", [128, 32], F32), ("h2_s", [128, 32], F32), ("w3_s", [128, 8], F32),
    ("key_w_p", [8, 128, 8, 128], BF16), ("key_b", [1, D], F32),
    ("vq_w_p", [16, 128, 8, 128], BF16),
    ("out_w_p", [8, 128, 8, 128], BF16), ("out_b", [1, D], F32),
    ("G1R_x", [4, 512, 512], F32), ("G2R_x", [4, 512, 512], F32),
    ("G1I_x", [4, 512, 512], F32), ("G2I_x", [4, 512, 512], F32),
    ("bias_ri", [1, D], F32),
    ("trimask", [128, 128], BF16), ("identity", [128, 128], BF16),
    ("ones_col_f32", [128, 1], F32), ("ones_col_bf", [128, 1], BF16),
    ("ones_row_bf", [1, 512], BF16), ("ones_row_f32", [1, 1024], F32),
    ("eps_col", [1, 1], F32),
    ("ind2", [2, 128], F32), ("sel2", [2, 128], F32),
    ("blockones2", [128, 2], BF16), ("eps30", [128, 1], F32),
    ("seltri16", [16, 32], mybir.dt.uint8), ("zeros1632", [16, 32], F32),
    ("seltri32", [32, 16], mybir.dt.uint8), ("zeros3216", [32, 16], F32),
    ("selstrict32", [32, 16], mybir.dt.uint8), ("blockM", [128, 128], BF16),
]
CONST_NAMES = ("trimask", "identity", "ones_col_f32", "ones_col_bf", "ones_row_bf",
               "ones_row_f32", "eps_col", "ind2", "sel2", "blockones2", "eps30",
               "seltri16", "zeros1632", "seltri32", "zeros3216",
               "selstrict32", "blockM")


class Ctx:
    pass


def build_program(sublayers):
    nc = bass.Bass("TRN2", target_bir_lowering=False, detect_race_conditions=False)
    c = Ctx()
    c.nc = nc
    c.d = {}
    for name, shape, dt in DEV_INPUTS:
        c.d[name] = nc.dram_tensor(name, shape, dt, kind="ExternalInput").ap()
    c.d["out"] = nc.dram_tensor("out", [D, T], F32, kind="ExternalOutput").ap()
    c.d["xs0"] = nc.dram_tensor("xs0", [D, T], F32, kind="Internal").ap()
    c.d["xs1"] = nc.dram_tensor("xs1", [D, T], F32, kind="Internal").ap()

    p = Prog()
    c.p = p
    sb_cms = []
    c.sb = {}
    for nm in CONST_NAMES:
        shape = dict((n, s) for n, s, _ in DEV_INPUTS)[nm]
        dt = dict((n, dd) for n, _, dd in DEV_INPUTS)[nm]
        cm = nc.sbuf_tensor("c_" + nm, shape, dt)
        sb_cms.append(cm)
        c.sb[nm] = cm.__enter__()
        p.dma("sp", c.sb[nm][:, :], c.d[nm], writes=("c_" + nm,), tag="const")

    cur = "xT"
    scratch = ["xs0", "xs1"]
    for i, sl in enumerate(sublayers):
        nxt = "out" if i == len(sublayers) - 1 else scratch[i % 2]
        c.pfx = f"u{i}_"
        EMITTERS[sl](c, c.d[cur], c.d[nxt])
        p.barrier()
        cur = nxt

    p.emit(nc)
    return nc


def emit_l0_ffn(c, x_in, x_out):
    nc, p = c.nc, c.p
    TC = 1024
    NCH = T // TC
    sb_cms = []

    def alloc(name, shape, dt):
        cm = nc.sbuf_tensor(c.pfx + name, shape, dt)
        sb_cms.append(cm)
        return cm.__enter__()

    xrot = [alloc(f"xrot{i}", [128, 2 * TC], F32) for i in range(2)]
    ln = [alloc(f"lnt{k}", [128, TC], BF16) for k in range(NKT)]
    hbuf = alloc("hbuf", [128, 32 * TC], BF16)
    gbuf = alloc("gbuf", [128, 32 * TC], BF16)
    wcol = alloc("wcol", [128, 2 * 32 * 128], BF16)
    wsml = alloc("wsml", [128, 2 * 2048], BF16)
    scol = alloc("scol", [128, 128], F32)
    s3col = alloc("s3col", [128, 8], F32)
    wsrow = alloc("wsrow", [1, 2 * 2 * 128], BF16)
    rowA = alloc("rowA", [1, TC], F32)
    rowB = alloc("rowB", [1, TC], F32)
    rowC = alloc("rowC", [1, TC], F32)
    rowD = alloc("rowD", [1, TC], BF16)
    ps = []
    for i in range(4):
        cm = nc.psum_tensor(f"{c.pfx}ps{i}", [128, TC], F32)
        sb_cms.append(cm)
        ps.append(cm.__enter__())

    xbf = [hbuf[:, k * TC:(k + 1) * TC] for k in range(NKT)]
    x2v = [hbuf[:, (8 + k) * TC:(9 + k) * TC] for k in range(NKT)]
    silu_v = [gbuf[:, (i * 2) * TC:(i * 2 + 2) * TC].bitcast(F32) for i in range(2)]

    p.dma("sp", scol[:, 0:32], c.d["w1_s"], writes=("scol",))
    p.dma("sp", scol[:, 32:64], c.d["w2_s"], writes=("scol",))
    p.dma("sp", scol[:, 64:96], c.d["h1_s"], writes=("scol",))
    p.dma("sp", scol[:, 96:128], c.d["h2_s"], writes=("scol",))
    p.dma("sp", s3col[:, :], c.d["w3_s"], writes=("s3col",))
    p.op("dve", lambda e: e.tensor_scalar_mul(s3col[:, :], s3col[:, :], 0.5),
         reads=("s3col",), writes=("s3col",))

    xin_r = x_in.rearrange("(k p) t -> p k t", p=128)
    xout_r = x_out.rearrange("(k p) t -> p k t", p=128)

    for ch in range(NCH):
        tsl = slice(ch * TC, (ch + 1) * TC)

        for kk in range(4):
            xr = xrot[kk % 2]
            xk = f"xrot{kk % 2}"
            p.dma("sp", xr.rearrange("p (k t) -> p k t", k=2),
                  xin_r[:, 2 * kk:2 * kk + 2, tsl], writes=(xk,), tag="xload")
            for j in range(2):
                k = 2 * kk + j
                p.op("dve", lambda e, xr=xr, j=j, k=k: e.tensor_copy(
                    xbf[k], xr[:, j * TC:(j + 1) * TC]),
                    reads=(xk,), writes=(f"h{k}",))
                p.op("dve", lambda e, k=k: e.tensor_mul(x2v[k], xbf[k], xbf[k]),
                    reads=(f"h{k}",), writes=(f"h{8 + k}",))
        for sc in range(TC // 512):
            sl = slice(sc * 512, (sc + 1) * 512)
            for k in range(NKT):
                p.op("ten", lambda e, k=k, sl=sl: e.matmul(
                    ps[0][0:1, sl], c.sb["ones_col_bf"][:, :], xbf[k][:, sl],
                    start=(k == 0), stop=(k == NKT - 1)),
                    reads=(f"h{k}", "c_ones_col_bf"), writes=("ps0",))
                p.op("ten", lambda e, k=k, sl=sl: e.matmul(
                    ps[1][0:1, sl], c.sb["ones_col_bf"][:, :], x2v[k][:, sl],
                    start=(k == 0), stop=(k == NKT - 1)),
                    reads=(f"h{8 + k}", "c_ones_col_bf"), writes=("ps1",))
            p.op("dve", lambda e, sl=sl: e.tensor_copy(rowA[0:1, sl], ps[0][0:1, sl]),
                 reads=("ps0",), writes=("rowA",))
            p.op("dve", lambda e, sl=sl: e.tensor_copy(rowB[0:1, sl], ps[1][0:1, sl]),
                 reads=("ps1",), writes=("rowB",))
        p.op("dve", lambda e: e.tensor_scalar_mul(rowA[0:1, :], rowA[0:1, :], 1.0 / D),
             reads=("rowA",), writes=("rowA",))
        p.op("dve", lambda e: e.scalar_tensor_tensor(rowC[0:1, :], rowA[0:1, :], -1.0,
                                                     rowA[0:1, :], OP.mult, OP.mult),
             reads=("rowA",), writes=("rowC",))
        p.op("dve", lambda e: e.scalar_tensor_tensor(rowB[0:1, :], rowB[0:1, :], 1.0 / D,
                                                     rowC[0:1, :], OP.mult, OP.add),
             reads=("rowB", "rowC"), writes=("rowB",))
        p.op("act", lambda e: e.activation(rowC[0:1, :], rowB[0:1, :], AF.Ln,
                                           bias=c.sb["eps_col"][0:1, 0:1], scale=1.0),
             reads=("rowB", "c_eps_col"), writes=("rowC",))
        p.op("act", lambda e: e.activation(rowC[0:1, :], rowC[0:1, :], AF.Exp, scale=-0.5),
             reads=("rowC",), writes=("rowC",))
        p.op("dve", lambda e: e.scalar_tensor_tensor(rowA[0:1, :], rowA[0:1, :], -1.0,
                                                     rowC[0:1, :], OP.mult, OP.mult),
             reads=("rowA", "rowC"), writes=("rowA",))
        p.op("dve", lambda e: e.tensor_copy(rowD[0:1, :], rowA[0:1, :]),
             reads=("rowA",), writes=("rowD",))

        for sc in range(TC // 512):
            sl = slice(sc * 512, (sc + 1) * 512)
            p.op("ten", lambda e, sl=sl: e.matmul(
                ps[0][:, sl], c.sb["ones_row_f32"][0:1, 0:128], rowC[0:1, sl],
                start=True, stop=True),
                reads=("rowC", "c_ones_row_f32"), writes=("ps0",))
        for kk in range(4):
            xr = xrot[kk % 2]
            xk = f"xrot{kk % 2}"
            p.dma("sp", xr.rearrange("p (k t) -> p k t", k=2),
                  xin_r[:, 2 * kk:2 * kk + 2, tsl], writes=(xk,), tag="xload2")
            for j in range(2):
                k = 2 * kk + j
                p.op("dve", lambda e, xr=xr, j=j, k=k: e.tensor_mul(
                    ln[k][:, :], xr[:, j * TC:(j + 1) * TC], ps[0][:, :]),
                    reads=(xk, "ps0"), writes=(f"ln{k}",))

        for o in range(32):
            ob = o % 2
            w1dst = wsml[:, ob * 2048:ob * 2048 + 1024]
            w2dst = wsml[:, ob * 2048 + 1024:ob * 2048 + 2048]
            k1, k2 = f"wsml{ob}a", f"wsml{ob}b"
            p.dma("pool", w1dst.rearrange("p (k o) -> p k o", k=8),
                  c.d["w1_p"][o], writes=(k1,), tag="w1col")
            p.dma("pool", w2dst.rearrange("p (k o) -> p k o", k=8),
                  c.d["w2_p"][o], writes=(k2,), tag="w2col")
            pst1, kt1 = ps[2 + ob], f"ps{2 + ob}"
            pst2, kt2 = ps[0 + ob], f"ps{0 + ob}"
            for wd, wk, roff in ((w1dst, k1, ob * 256), (w2dst, k2, ob * 256 + 128)):
                for k in range(NKT):
                    p.op("ten", lambda e, k=k, wd=wd: e.matmul(
                        ps[1][0:1, 0:128], c.sb["ones_col_bf"][:, :], wd[:, k * 128:(k + 1) * 128],
                        start=(k == 0), stop=(k == NKT - 1)),
                        reads=(wk, "c_ones_col_bf"), writes=("ps1",))
                p.op("dve", lambda e, roff=roff: e.tensor_copy(
                    wsrow[0:1, roff:roff + 128], ps[1][0:1, 0:128]),
                    reads=("ps1",), writes=(f"wsrow{roff}",))
            for half in range(2):
                hsl = slice(half * 512, (half + 1) * 512)
                for k in range(NKT):
                    p.op("ten", lambda e, k=k, hsl=hsl, w1dst=w1dst, pst1=pst1: e.matmul(
                        pst1[:, hsl], w1dst[:, k * 128:(k + 1) * 128], ln[k][:, hsl],
                        start=(k == 0), stop=False),
                        reads=(k1, f"ln{k}"), writes=(kt1,))
                p.op("ten", lambda e, hsl=hsl, o=o, pst1=pst1, ob=ob: e.matmul(
                    pst1[:, hsl], wsrow[0:1, ob * 256:ob * 256 + 128], rowD[0:1, hsl],
                    start=False, stop=True),
                    reads=(f"wsrow{ob * 256}", "rowD"), writes=(kt1,))
                for k in range(NKT):
                    p.op("ten", lambda e, k=k, hsl=hsl, w2dst=w2dst, pst2=pst2: e.matmul(
                        pst2[:, hsl], w2dst[:, k * 128:(k + 1) * 128], ln[k][:, hsl],
                        start=(k == 0), stop=False),
                        reads=(k2, f"ln{k}"), writes=(kt2,))
                p.op("ten", lambda e, hsl=hsl, o=o, pst2=pst2, ob=ob: e.matmul(
                    pst2[:, hsl], wsrow[0:1, ob * 256 + 128:ob * 256 + 256], rowD[0:1, hsl],
                    start=False, stop=True),
                    reads=(f"wsrow{ob * 256 + 128}", "rowD"), writes=(kt2,))
            p.op("act", lambda e, o=o, pst1=pst1: e.activation(
                silu_v[o % 2], pst1[:, :], AF.Silu, scale=scol[:, o:o + 1]),
                reads=(kt1, "scol"), writes=(f"g{(o % 2) * 2}", f"g{(o % 2) * 2 + 1}"))
            p.op("dve", lambda e, o=o, pst2=pst2: e.scalar_tensor_tensor(
                hbuf[:, o * TC:(o + 1) * TC], pst2[:, :], scol[:, 32 + o:32 + o + 1],
                silu_v[o % 2], OP.mult, OP.mult),
                reads=(kt2, f"g{(o % 2) * 2}", f"g{(o % 2) * 2 + 1}", "scol"), writes=(f"h{o}",))

        for o in range(32):
            ob = o % 2
            wdst = wcol[:, ob * 4096:(ob + 1) * 4096]
            wk = f"wcol{ob}"
            p.dma("pool", wdst.rearrange("p (k o) -> p k o", k=32),
                  c.d["h1_p"][o], writes=(wk,), tag="h1col")
            pst, kt = ps[2 + ob], f"ps{2 + ob}"
            for half in range(2):
                hsl = slice(half * 512, (half + 1) * 512)
                for k in range(32):
                    p.op("ten", lambda e, k=k, hsl=hsl, wdst=wdst, pst=pst: e.matmul(
                        pst[:, hsl], wdst[:, k * 128:(k + 1) * 128],
                        hbuf[:, k * TC + hsl.start:k * TC + hsl.stop],
                        start=(k == 0), stop=(k == 31)),
                        reads=(wk, f"h{k}"), writes=(kt,))
            p.op("act", lambda e, o=o, pst=pst: e.activation(
                gbuf[:, o * TC:(o + 1) * TC], pst[:, :], AF.Gelu,
                scale=scol[:, 64 + o:64 + o + 1]),
                reads=(kt, "scol"), writes=(f"g{o}",))

        for o in range(32):
            ob = o % 2
            wdst = wcol[:, ob * 4096:(ob + 1) * 4096]
            wk = f"wcol{ob}"
            p.dma("pool", wdst.rearrange("p (k o) -> p k o", k=32),
                  c.d["h2_p"][o], writes=(wk,), tag="h2col")
            pst, kt = ps[2 + ob], f"ps{2 + ob}"
            for half in range(2):
                hsl = slice(half * 512, (half + 1) * 512)
                for k in range(32):
                    p.op("ten", lambda e, k=k, hsl=hsl, wdst=wdst, pst=pst: e.matmul(
                        pst[:, hsl], wdst[:, k * 128:(k + 1) * 128],
                        gbuf[:, k * TC + hsl.start:k * TC + hsl.stop],
                        start=(k == 0), stop=(k == 31)),
                        reads=(wk, f"g{k}"), writes=(kt,))
            p.op("act", lambda e, o=o, pst=pst: e.activation(
                hbuf[:, o * TC:(o + 1) * TC], pst[:, :], AF.Gelu,
                scale=scol[:, 96 + o:96 + o + 1]),
                reads=(kt, "scol"), writes=(f"h{o}",))

        for op2 in range(4):
            xr = xrot[op2 % 2]
            xk = f"xrot{op2 % 2}"
            p.dma("sp", xr.rearrange("p (k t) -> p k t", k=2),
                  xin_r[:, 2 * op2:2 * op2 + 2, tsl], writes=(xk,), tag="xres")
            for j in range(2):
                od = 2 * op2 + j
                ob = od % 2
                wdst = wcol[:, ob * 4096:(ob + 1) * 4096]
                wk = f"wcol{ob}"
                p.dma("pool", wdst.rearrange("p (k o) -> p k o", k=32),
                      c.d["w3_p"][od], writes=(wk,), tag="w3col")
                pst, kt = ps[2 + ob], f"ps{2 + ob}"
                for half in range(2):
                    hsl = slice(half * 512, (half + 1) * 512)
                    for k in range(32):
                        p.op("ten", lambda e, k=k, hsl=hsl, wdst=wdst, pst=pst: e.matmul(
                            pst[:, hsl], wdst[:, k * 128:(k + 1) * 128],
                            hbuf[:, k * TC + hsl.start:k * TC + hsl.stop],
                            start=(k == 0), stop=(k == 31)),
                            reads=(wk, f"h{k}"), writes=(kt,))
                p.op("dve", lambda e, od=od, j=j, xr=xr, pst=pst: e.scalar_tensor_tensor(
                    xr[:, j * TC:(j + 1) * TC], pst[:, :], s3col[:, od:od + 1],
                    xr[:, j * TC:(j + 1) * TC], OP.mult, OP.add),
                    reads=(kt, "s3col", xk), writes=(xk,))
            p.dma("sp", xout_r[:, 2 * op2:2 * op2 + 2, tsl],
                  xr.rearrange("p (k t) -> p k t", k=2), reads=(xk,), tag="xstore")

    for cm in reversed(sb_cms):
        cm.__exit__(None, None, None)



def emit_l0_attn(c, x_in, x_out):
    nc, p = c.nc, c.p
    sb_cms = []

    def alloc(name, shape, dt):
        cm = nc.sbuf_tensor(c.pfx + name, shape, dt)
        sb_cms.append(cm)
        return cm.__enter__()

    xrot = [alloc(f"xrot{i}", [128, 2 * T], F32) for i in range(2)]
    xbf2 = alloc("xbf2", [128, 2 * T], BF16)
    ln = [alloc(f"lnt{k}", [128, T], BF16) for k in range(NKT)]
    rows = {"A": alloc("rowA", [1, T], F32), "B": alloc("rowB", [1, T], F32),
            "C": alloc("rowC", [1, T], F32), "D": alloc("rowD", [1, T], BF16)}
    rbc = alloc("rbc", [128, T], F32)
    apre = [alloc(f"apre{i}", [128, T], BF16) for i in range(8)]
    qzb = alloc("qzb", [128, T], BF16)
    kzb = alloc("kzb", [128, T], BF16)
    qf = alloc("qf", [128, T], BF16)
    kf = alloc("kf", [128, T], BF16)
    vfm = alloc("vfm", [128, T], BF16)
    z2b = vfm
    om_sb = alloc("om_sb", [128, 64], BF16)
    wqcol = alloc("wqcol", [128, 2 * 1024], BF16)
    s2row = alloc("s2row", [2, 512], F32)
    rnbc = alloc("rnbc", [128, 1024], F32)
    fw1 = None
    fw2 = None
    tbf = alloc("tbf", [128, 1024], BF16)
    brow = alloc("brow", [1, 3 * D], F32)
    borow = alloc("borow", [1, D], F32)
    qcol2 = alloc("qcol2", [2, 4], F32)
    qcols = alloc("qcols", [128, 4], F32)
    kftok = alloc("kftok", [128, 128], BF16)
    wsq = alloc("wsq", [1, 256], BF16)
    vtok = alloc("vtok", [128, 16 * 128], BF16)
    smb = [alloc(f"smb{h}", [128, 128], BF16) for h in range(2)]
    KVsb = alloc("KVsb", [128, 64], BF16)
    kssb = alloc("kssb", [128, 1], BF16)
    KVf = alloc("KVf", [128, 64], F32)
    ksf = alloc("ksf", [128, 1], F32)
    ctxsb = alloc("ctxsb", [128, 512], F32)
    denr = alloc("denr", [1, 512], F32)
    wocol = alloc("wocol", [128, 1024], BF16)
    ps_cms = []

    def palloc(name, shape, dt=F32):
        cm = nc.psum_tensor(c.pfx + name, shape, dt)
        ps_cms.append(cm)
        sb_cms.append(cm)
        return cm.__enter__()

    fw1 = rbc[:, 0:1024]
    fw2 = rbc[:, 1024:2048]
    xin_r = x_in.rearrange("(k p) t -> p k t", p=128)
    xout_r = x_out.rearrange("(k p) t -> p k t", p=128)

    p.dma("sp", brow[0:1, :], c.d["b_qkv"], writes=("brow",))
    p.dma("sp", borow[0:1, :], c.d["b_out"], writes=("borow",))
    p.dma("sp", qcol2[:, 0:1], c.d["qnodes2"], writes=("qcol2",))
    p.dma("sp", qcol2[:, 1:2], c.d["qweights2"], writes=("qcol2",))

    ln_stream(c, p, xin_r, slice(0, T), T, xrot, ln, rows, rbc, xbf2)

    psq = palloc("psq", [128, 1024])
    psk = palloc("psk", [128, 1024])
    psp = palloc("psp", [128, 1024])
    psv = palloc("psv", [128, 512])
    psw = palloc("psw", [128, 512])

    p.op("dve", lambda e: e.tensor_scalar(qcol2[0:2, 2:3], qcol2[0:2, 0:1], 0.0, 2.0,
                                          OP.max, OP.mult), reads=("qcol2",), writes=("qcol2",))
    p.op("act", lambda e: e.activation(qcol2[0:2, 2:3], qcol2[0:2, 2:3], AF.Ln,
                                       bias=c.sb["eps30"][0:2, 0:1]),
         reads=("qcol2", "c_eps30"), writes=("qcol2",))
    p.op("act", lambda e: e.activation(qcol2[0:2, 2:3], qcol2[0:2, 2:3], AF.Exp, scale=0.5),
         reads=("qcol2",), writes=("qcol2",))
    p.op("dve", lambda e: e.tensor_scalar_mul(qcol2[0:2, 3:4], qcol2[0:2, 0:1], -1.0),
         reads=("qcol2",), writes=("qcol2",))
    p.op("dve", lambda e: e.tensor_scalar_max(qcol2[0:2, 1:2], qcol2[0:2, 1:2], 0.0),
         reads=("qcol2",), writes=("qcol2",))
    p.op("act", lambda e: e.activation(qcol2[0:2, 1:2], qcol2[0:2, 1:2], AF.Ln,
                                       bias=c.sb["eps30"][0:2, 0:1]),
         reads=("qcol2", "c_eps30"), writes=("qcol2",))
    p.op("dve", lambda e: e.tensor_scalar(qcol2[0:2, 1:2], qcol2[0:2, 1:2], 0.25,
                                          -1.7328679513998633, OP.mult, OP.add),
         reads=("qcol2",), writes=("qcol2",))
    for i, src in ((0, 2), (1, 1), (2, 3)):
        p.op("ten", lambda e, i=i, src=src: e.matmul(
            psw[:, i:i + 1], c.sb["sel2"][:, :], qcol2[0:2, src:src + 1],
            start=True, stop=True),
            reads=("qcol2", "c_sel2"), writes=("psw",))
        p.op("dve", lambda e, i=i: e.tensor_copy(qcols[:, i:i + 1], psw[:, i:i + 1]),
             reads=("psw",), writes=("qcols",))

    for hp in range(8):
        hb = hp % 2
        p.dma("pool", om_sb[0:64, :], c.d["omegaT"][2 * hp], writes=("om_sb",))
        p.dma("pool", om_sb[64:128, :], c.d["omegaT"][2 * hp + 1], writes=("om_sb",))
        for w_i, (wname, pst, kt, zb, zk, boff) in enumerate(
                ((hp, psq, "psq", qzb, "qzb", 0),
                 (8 + hp, psk, "psk", kzb, "kzb", 1024),
                 (16 + hp, psv, "psv", vfm, "vfm", 2048))):
            wdst = wqcol[:, w_i % 2 * 1024:(w_i % 2 + 1) * 1024]
            wk = f"wqcol{w_i % 2}"
            p.dma("pool", wdst.rearrange("p (k o) -> p k o", k=8),
                  c.d["w_qkv_p"][wname], writes=(wk,), tag="wqkv")
            wb = w_i % 2
            for k in range(NKT):
                p.op("ten", lambda e, k=k, wdst=wdst: e.matmul(
                    psw[0:1, 0:128], c.sb["ones_col_bf"][:, :], wdst[:, k * 128:(k + 1) * 128],
                    start=(k == 0), stop=(k == NKT - 1)),
                    reads=(wk, "c_ones_col_bf"), writes=("psw",))
            p.op("dve", lambda e, wb=wb: e.tensor_copy(wsq[0:1, wb * 128:(wb + 1) * 128],
                                                       psw[0:1, 0:128]),
                 reads=("psw",), writes=(f"wsq{wb}",))
            if w_i < 2:
                for sc in range(4):
                    sl = slice(sc * 512, (sc + 1) * 512)
                    psl = slice((sc % 2) * 512, (sc % 2 + 1) * 512)
                    for k in range(NKT):
                        p.op("ten", lambda e, k=k, sl=sl, psl=psl, wdst=wdst, pst=pst: e.matmul(
                            pst[:, psl], wdst[:, k * 128:(k + 1) * 128], ln[k][:, sl],
                            start=(k == 0), stop=False),
                            reads=(wk, f"ln{k}"), writes=(kt,))
                    p.op("ten", lambda e, sl=sl, psl=psl, pst=pst, wb=wb: e.matmul(
                        pst[:, psl], wsq[0:1, wb * 128:(wb + 1) * 128], rows["D"][0:1, sl],
                        start=False, stop=False),
                        reads=(f"wsq{wb}", "rowD"), writes=(kt,))
                    p.op("ten", lambda e, sl=sl, psl=psl, pst=pst, boff=boff, hp=hp: e.matmul(
                        pst[:, psl], brow[0:1, boff + hp * 128:boff + (hp + 1) * 128],
                        c.sb["ones_row_f32"][0:1, 0:512], start=False, stop=True),
                        reads=("brow", "c_ones_row_f32"), writes=(kt,))
                    if sc % 2 == 1:
                        ch2 = sc // 2
                        p.op("act", lambda e, ch2=ch2, pst=pst, zb=zb: e.activation(
                            zb[:, ch2 * 1024:(ch2 + 1) * 1024], pst[:, :], AF.Identity),
                            reads=(kt,), writes=(zk,))
            else:
                for sc in range(4):
                    sl = slice(sc * 512, (sc + 1) * 512)
                    for k in range(NKT):
                        p.op("ten", lambda e, k=k, sl=sl, wdst=wdst, pst=pst: e.matmul(
                            pst[:, 0:512], wdst[:, k * 128:(k + 1) * 128], ln[k][:, sl],
                            start=(k == 0), stop=False),
                            reads=(wk, f"ln{k}"), writes=(kt,))
                    p.op("ten", lambda e, sl=sl, pst=pst, wb=wb: e.matmul(
                        pst[:, 0:512], wsq[0:1, wb * 128:(wb + 1) * 128], rows["D"][0:1, sl],
                        start=False, stop=False),
                        reads=(f"wsq{wb}", "rowD"), writes=(kt,))
                    p.op("ten", lambda e, pst=pst, hp=hp: e.matmul(
                        pst[:, 0:512], brow[0:1, 2048 + hp * 128:2048 + (hp + 1) * 128],
                        c.sb["ones_row_f32"][0:1, 0:512], start=False, stop=True),
                        reads=("brow", "c_ones_row_f32"), writes=(kt,))
                    p.op("act", lambda e, sc=sc, pst=pst: e.activation(
                        vfm[:, sc * 512:(sc + 1) * 512], pst[:, :], AF.Identity),
                        reads=(kt,), writes=("vfm",))
        for tt in range(16):
            p.op("ten", lambda e, tt=tt: e.transpose(
                psp[:, 0:64].bitcast(BF16), vfm[:, tt * 128:(tt + 1) * 128],
                c.sb["identity"][:, :]),
                reads=("vfm", "c_identity"), writes=("psp",))
            p.op("dve", lambda e, tt=tt: e.tensor_copy(
                vtok[:, tt * 128:(tt + 1) * 128], psp[:, 0:64].bitcast(BF16)),
                reads=("psp",), writes=(f"vtok{tt}",))

        if DEBUG_TAPS and hp == 0:
            for ki in range(2):
                dd = nc.dram_tensor(f"dbg_ln{ki}", [128, T], BF16, kind="ExternalOutput").ap()
                p.dma("sp", dd, ln[ki][:, :], reads=(f"ln{ki}",), tag="dbg")
            ddr = nc.dram_tensor("dbg_rows", [1, T], F32, kind="ExternalOutput").ap()
            p.dma("sp", ddr, rows["C"][0:1, :], reads=("rowC",), tag="dbg")
            ddr2 = nc.dram_tensor("dbg_nmr", [1, T], F32, kind="ExternalOutput").ap()
            p.dma("sp", ddr2, rows["A"][0:1, :], reads=("rowA",), tag="dbg")
            for nm, buf in (("qzb", qzb), ("kzb", kzb), ("vfm", vfm)):
                dd = nc.dram_tensor("dbg_" + nm, [128, T], BF16, kind="ExternalOutput").ap()
                p.dma("sp", dd, buf[:, :], reads=(nm if nm != "vfm" else "vfm",), tag="dbg")
        if DEBUG_TAPS and hp == 0:
            ddo = nc.dram_tensor("dbg_om", [128, 64], BF16, kind="ExternalOutput").ap()
            p.dma("sp", ddo, om_sb[:, :], reads=("om_sb",), tag="dbg")
        for qk, (zb, zk, fdst, fk) in enumerate(((qzb, "qzb", qf, "qf"),
                                                 (kzb, "kzb", kf, "kf"))):
            p.op("dve", lambda e, zb=zb: e.tensor_mul(z2b[:, :], zb[:, :], zb[:, :]),
                 reads=(zk,), writes=("vfm",))
            for ch2 in range(2):
                csl = slice(ch2 * 1024, (ch2 + 1) * 1024)
                for sc in range(2):
                    ssl = slice(ch2 * 1024 + sc * 512, ch2 * 1024 + (sc + 1) * 512)
                    p.op("ten", lambda e, ssl=ssl: e.matmul(
                        psw[0:2, 0:512], c.sb["blockones2"][:, :], z2b[:, ssl],
                        start=True, stop=True),
                        reads=("vfm", "c_blockones2"), writes=("psw",))
                    p.op("dve", lambda e: e.tensor_copy(
                        s2row[0:2, 0:512], psw[0:2, 0:512]),
                        reads=("psw",), writes=("s2row",))
                    p.op("ten", lambda e: e.matmul(
                        psw[:, 0:512], c.sb["ind2"][:, :], s2row[0:2, 0:512],
                        start=True, stop=True),
                        reads=("s2row", "c_ind2"), writes=("psw",))
                    p.op("dve", lambda e, sc=sc: e.tensor_scalar_max(
                        rnbc[:, sc * 512:(sc + 1) * 512], psw[:, 0:512], 1e-24),
                        reads=("psw",), writes=("rnbc",))
                    p.op("act", lambda e, sc=sc: e.activation(
                        rnbc[:, sc * 512:(sc + 1) * 512], rnbc[:, sc * 512:(sc + 1) * 512],
                        AF.Ln), reads=("rnbc",), writes=("rnbc",))
                    p.op("act", lambda e, sc=sc: e.activation(
                        rnbc[:, sc * 512:(sc + 1) * 512], rnbc[:, sc * 512:(sc + 1) * 512],
                        AF.Exp, scale=-0.5), reads=("rnbc",), writes=("rnbc",))
                    for h in range(2):
                        p.op("ten", lambda e, h=h, ssl=ssl, sc=sc, zb=zb: e.matmul(
                            psp[h * 64:(h + 1) * 64, sc * 512:(sc + 1) * 512],
                            om_sb[h * 64:(h + 1) * 64, :], zb[h * 64:(h + 1) * 64, ssl],
                            start=True, stop=True),
                            reads=("om_sb", zk), writes=("psp",))
                if DEBUG_TAPS and hp == 0 and qk == 0 and ch2 == 0:
                    dd0 = nc.dram_tensor("dbg_praw", [128, 1024], F32, kind="ExternalOutput").ap()
                    p.op("dve", lambda e: e.tensor_copy(tbf[:, :].bitcast(F32)[:, 0:512], psp[:, 0:512]),
                         reads=("psp",), writes=("tbf",))
                    p.dma("sp", dd0[:, 0:512], tbf[:, :].bitcast(F32)[:, 0:512], reads=("tbf",), tag="dbg")
                    p.op("dve", lambda e: e.tensor_copy(tbf[:, :].bitcast(F32)[:, 0:512], psp[:, 512:1024]),
                         reads=("psp",), writes=("tbf",))
                    p.dma("sp", dd0[:, 512:1024], tbf[:, :].bitcast(F32)[:, 0:512], reads=("tbf",), tag="dbg")
                p.op("dve", lambda e: e.tensor_mul(fw1[:, :], psp[:, :], rnbc[:, :]),
                     reads=("psp", "rnbc"), writes=("rbc",))
                if DEBUG_TAPS and hp == 0 and qk == 0 and ch2 == 0:
                    dd1 = nc.dram_tensor("dbg_p", [128, 1024], F32, kind="ExternalOutput").ap()
                    p.dma("sp", dd1, fw1[:, :], reads=("rbc",), tag="dbg")
                    dd2 = nc.dram_tensor("dbg_rn", [128, 1024], F32, kind="ExternalOutput").ap()
                    p.dma("sp", dd2, rnbc[:, :], reads=("rnbc",), tag="dbg")
                p.op("dve", lambda e: e.tensor_scalar(fw2[:, :], fw1[:, :], qcols[:, 0:1],
                                                      qcols[:, 2:3], OP.mult, OP.add),
                     reads=("rbc", "qcols"), writes=("rbc",))
                p.op("dve", lambda e: e.tensor_scalar(fw2[:, :], fw2[:, :], 20.0, -20.0,
                                                      OP.min, OP.max),
                     reads=("rbc",), writes=("rbc",))
                p.op("act", lambda e: e.activation(fw2[:, :], fw2[:, :], AF.Exp,
                                                   bias=qcols[:, 1:2], scale=0.5),
                     reads=("rbc", "qcols"), writes=("rbc",))
                p.op("dve", lambda e: e.tensor_mul(tbf[:, :], fw1[:, :], fw2[:, :]),
                     reads=("rbc",), writes=("tbf",))
                p.op("act", lambda e, fdst=fdst, csl=csl: e.activation(
                    fdst[:, csl], tbf[:, :], AF.Square),
                    reads=("tbf",), writes=(fk,))

        if DEBUG_TAPS and hp == 0:
            for nm, buf in (("qf", qf), ("kf", kf)):
                dd = nc.dram_tensor("dbg_" + nm, [128, T], BF16, kind="ExternalOutput").ap()
                p.dma("sp", dd, buf[:, :], reads=(nm,), tag="dbg")
            dd = nc.dram_tensor("dbg_vtok", [128, 16 * 128], BF16, kind="ExternalOutput").ap()
            p.dma("sp", dd, vtok[:, :], reads=tuple(f"vtok{i}" for i in range(16)), tag="dbg")
        p.op("dve", lambda e: e.memset(KVf[:, :], 0.0), writes=("KVf",))
        p.op("dve", lambda e: e.memset(ksf[:, :], 0.0), writes=("ksf",))
        for g in range(4):
            for ccl in range(4):
                cc = g * 4 + ccl
                sl = slice(cc * 128, (cc + 1) * 128)
                p.op("ten", lambda e, sl=sl: e.transpose(
                    psk[:, 0:64].bitcast(BF16), kf[:, sl], c.sb["identity"][:, :]),
                    reads=("kf", "c_identity"), writes=("psk",))
                p.op("dve", lambda e: e.tensor_copy(kftok[:, :], psk[:, 0:64].bitcast(BF16)),
                     reads=("psk",), writes=("kftok",))
                for h in range(2):
                    hsl = slice(h * 64, (h + 1) * 64)
                    p.op("ten", lambda e, h=h, sl=sl, hsl=hsl: e.matmul(
                        psp[:, h * 512:h * 512 + 128], kf[hsl, sl], qf[hsl, sl],
                        start=True, stop=True),
                        reads=("kf", "qf"), writes=("psp",))
                    p.op("dve", lambda e, h=h: e.tensor_mul(
                        smb[h][:, :], psp[:, h * 512:h * 512 + 128], c.sb["trimask"][:, :]),
                        reads=("psp", "c_trimask"), writes=(f"smb{h}",))
                    dpart = slice(0, 1) if h == 0 else slice(32, 33)
                    p.op("ten", lambda e, h=h, ccl=ccl, dpart=dpart: e.matmul(
                        psw[dpart, ccl * 128:(ccl + 1) * 128], c.sb["ones_col_bf"][:, :],
                        smb[h][:, :], start=True, stop=(cc % 16 == 0)),
                        reads=(f"smb{h}", "c_ones_col_bf"), writes=("psw",))
                    if cc % 16 != 0:
                        p.op("ten", lambda e, h=h, sl=sl, ccl=ccl, hsl=hsl, dpart=dpart: e.matmul(
                            psw[dpart, ccl * 128:(ccl + 1) * 128], kssb[hsl, 0:1],
                            qf[hsl, sl], start=False, stop=True),
                            reads=("kssb", "qf"), writes=("psw",))
                    if cc % 16 != 0:
                        p.op("ten", lambda e, h=h, sl=sl, hsl=hsl: e.matmul(
                            psq[hsl, 0:128], KVsb[hsl, :], qf[hsl, sl],
                            start=True, stop=False),
                            reads=("KVsb", "qf"), writes=("psq",))
                    p.op("ten", lambda e, h=h, cc=cc, hsl=hsl: e.matmul(
                        psq[hsl, 0:128], vtok[:, cc % 16 * 128 + h * 64:cc % 16 * 128 + (h + 1) * 64],
                        smb[h][:, :], start=(cc % 16 == 0), stop=True),
                        reads=(f"vtok{cc % 16}", f"smb{h}"), writes=("psq",))
                    p.op("ten", lambda e, h=h, cc=cc, hsl=hsl: e.matmul(
                        psv[hsl, 0:64], kftok[:, h * 64:(h + 1) * 64],
                        vtok[:, cc % 16 * 128 + h * 64:cc % 16 * 128 + (h + 1) * 64],
                        start=True, stop=False),
                        reads=("kftok", f"vtok{cc % 16}"), writes=("psv",))
                    p.op("ten", lambda e, h=h, cc=cc, hsl=hsl: e.matmul(
                        psv[hsl, 64:65], kftok[:, h * 64:(h + 1) * 64],
                        c.sb["ones_col_bf"][:, :], start=False,
                        stop=True),
                        reads=("kftok", "c_ones_col_bf"), writes=("psv",))
                p.op("dve", lambda e: e.tensor_add(KVf[:, :], KVf[:, :], psv[:, 0:64]),
                     reads=("psv", "KVf"), writes=("KVf",))
                p.op("dve", lambda e: e.tensor_add(ksf[:, :], ksf[:, :], psv[:, 64:65]),
                     reads=("psv", "ksf"), writes=("ksf",))
                p.op("dve", lambda e: e.tensor_copy(KVsb[:, :], KVf[:, :]),
                     reads=("KVf",), writes=("KVsb",))
                p.op("dve", lambda e: e.tensor_copy(kssb[:, :], ksf[:, :]),
                     reads=("ksf",), writes=("kssb",))
                if DEBUG_TAPS and hp == 0 and cc == 0:
                    ddk = nc.dram_tensor("dbg_KVf", [128, 64], F32, kind="ExternalOutput").ap()
                    p.dma("sp", ddk, KVf[:, :], reads=("KVf",), tag="dbg")
                    ddk2 = nc.dram_tensor("dbg_ksf", [128, 1], F32, kind="ExternalOutput").ap()
                    p.dma("sp", ddk2, ksf[:, :], reads=("ksf",), tag="dbg")
                    ddk3 = nc.dram_tensor("dbg_kftok", [128, 128], BF16, kind="ExternalOutput").ap()
                    p.dma("sp", ddk3, kftok[:, :], reads=("kftok",), tag="dbg")
                p.op("dve", lambda e, ccl=ccl: e.tensor_copy(
                    ctxsb[:, ccl * 128:(ccl + 1) * 128], psq[:, 0:128]),
                    reads=("psq",), writes=("ctxsb",))
            for h in range(2):
                dpart = slice(0, 1) if h == 0 else slice(32, 33)
                p.op("dve", lambda e, h=h, dpart=dpart: e.tensor_scalar_max(
                    denr[0:1, 0:512], psw[dpart, 0:512], 1e-6),
                    reads=("psw",), writes=("denr",))
                p.op("dve", lambda e: e.reciprocal(
                    denr[0:1, 0:512], denr[0:1, 0:512]),
                    reads=("denr",), writes=("denr",))
                p.op("ten", lambda e, h=h: e.matmul(
                    psk[h * 64:(h + 1) * 64, 0:512], c.sb["ones_row_f32"][0:1, 0:64],
                    denr[0:1, 0:512], start=True, stop=True),
                    reads=("denr", "c_ones_row_f32"), writes=("psk",))
            p.op("dve", lambda e, g=g, hp=hp: e.tensor_mul(
                apre[hp][:, g * 512:(g + 1) * 512], ctxsb[:, :], psk[:, 0:512]),
                reads=("ctxsb", "psk"), writes=(f"apre{hp}",))

    if DEBUG_TAPS:
        for i in range(8):
            dd = nc.dram_tensor(f"dbg_apre{i}", [128, T], BF16, kind="ExternalOutput").ap()
            p.dma("sp", dd, apre[i][:, :], reads=(f"apre{i}",), tag="dbg")
    for odp in range(4):
        xr = xrot[odp % 2]
        xk = f"xrot{odp % 2}"
        p.dma("sp", xr.rearrange("p (k t) -> p k t", k=2),
              xin_r[:, 2 * odp:2 * odp + 2, 0:T], writes=(xk,), tag="xres")
        for j in range(2):
            od = 2 * odp + j
            wdst = wocol[:, 0:1024]
            wk = "wocol"
            p.dma("pool", wdst.rearrange("p (k o) -> p k o", k=8),
                  c.d["w_out_p"][od], writes=(wk,), tag="wout")
            for tch in range(4):
                sl = slice(tch * 512, (tch + 1) * 512)
                pst = psq if tch % 2 == 0 else psk
                kt = "psq" if tch % 2 == 0 else "psk"
                for k in range(NKT):
                    p.op("ten", lambda e, k=k, sl=sl, wdst=wdst, pst=pst: e.matmul(
                        pst[:, 0:512], wdst[:, k * 128:(k + 1) * 128], apre[k][:, sl],
                        start=(k == 0), stop=False),
                        reads=(wk, f"apre{k}"), writes=(kt,))
                p.op("ten", lambda e, od=od, pst=pst: e.matmul(
                    pst[:, 0:512], borow[0:1, od * 128:(od + 1) * 128],
                    c.sb["ones_row_f32"][0:1, 0:512], start=False, stop=True),
                    reads=("borow", "c_ones_row_f32"), writes=(kt,))
                p.op("dve", lambda e, j=j, sl=sl, xr=xr, pst=pst: e.scalar_tensor_tensor(
                    xr[:, j * T + sl.start:j * T + sl.stop], pst[:, 0:512], 1.0,
                    xr[:, j * T + sl.start:j * T + sl.stop], OP.mult, OP.add),
                    reads=(kt, xk), writes=(xk,))
        p.dma("sp", xout_r[:, 2 * odp:2 * odp + 2, 0:T],
              xr.rearrange("p (k t) -> p k t", k=2), reads=(xk,), tag="xstore")
    for cm in reversed(sb_cms):
        cm.__exit__(None, None, None)


def ln_stream(c, p, x_in_r, tsl, TC, xrot, ln, rows, rbc, xbf2):
    nc = c.nc
    psA_cm = nc.psum_tensor(c.pfx + "lnpsA", [128, 2048], F32)
    psA = psA_cm.__enter__()
    psB_cm = nc.psum_tensor(c.pfx + "lnpsB", [128, 2048], F32)
    psB = psB_cm.__enter__()
    pA, pB = "psA", "psB"
    for kk in range(4):
        xr = xrot[kk % 2]
        xk = f"xrot{kk % 2}"
        p.dma("sp", xr.rearrange("p (k t) -> p k t", k=2),
              x_in_r[:, 2 * kk:2 * kk + 2, tsl], writes=(xk,), tag="lnx")
        for j in range(2):
            k = 2 * kk + j
            xb = xbf2[:, j * TC:(j + 1) * TC]
            xbk = f"xbf{j}"
            p.op("dve", lambda e, xr=xr, j=j, xb=xb: e.tensor_copy(
                xb, xr[:, j * TC:(j + 1) * TC]), reads=(xk,), writes=(xbk,))
            nch = TC // 512
            for sc in range(nch):
                sl = slice(sc * 512, (sc + 1) * 512)
                bsl = sl
                p.op("ten", lambda e, xb=xb, sl=sl, bsl=bsl, k=k: e.matmul(
                    psA[0:1, bsl], c.sb["ones_col_bf"][:, :], xb[:, sl],
                    start=(k == 0), stop=(k == NKT - 1)),
                    reads=(xbk, "c_ones_col_bf"), writes=(pA,))
            x2 = xbf2[:, (1 - j) * TC:(2 - j) * TC]
            x2k = f"xbf{1 - j}"
            p.op("dve", lambda e, xb=xb, x2=x2: e.tensor_mul(x2, xb, xb),
                 reads=(xbk,), writes=(x2k,))
            for sc in range(nch):
                sl = slice(sc * 512, (sc + 1) * 512)
                bsl = sl
                p.op("ten", lambda e, x2=x2, sl=sl, bsl=bsl, k=k: e.matmul(
                    psB[0:1, bsl], c.sb["ones_col_bf"][:, :], x2[:, sl],
                    start=(k == 0), stop=(k == NKT - 1)),
                    reads=(x2k, "c_ones_col_bf"), writes=(pB,))
    rowA, rowB, rowC, rowD = rows["A"], rows["B"], rows["C"], rows["D"]
    for sc in range(TC // 512):
        sl = slice(sc * 512, (sc + 1) * 512)
        bsl = sl
        p.op("dve", lambda e, sl=sl, bsl=bsl: e.tensor_copy(rowA[0:1, sl], psA[0:1, bsl]),
             reads=(pA,), writes=("rowA",))
        p.op("dve", lambda e, sl=sl, bsl=bsl: e.tensor_copy(rowB[0:1, sl], psB[0:1, bsl]),
             reads=(pB,), writes=("rowB",))
    p.op("dve", lambda e: e.tensor_scalar_mul(rowA[0:1, :TC], rowA[0:1, :TC], 1.0 / D),
         reads=("rowA",), writes=("rowA",))
    p.op("dve", lambda e: e.scalar_tensor_tensor(rowC[0:1, :TC], rowA[0:1, :TC], -1.0,
                                                 rowA[0:1, :TC], OP.mult, OP.mult),
         reads=("rowA",), writes=("rowC",))
    p.op("dve", lambda e: e.scalar_tensor_tensor(rowB[0:1, :TC], rowB[0:1, :TC], 1.0 / D,
                                                 rowC[0:1, :TC], OP.mult, OP.add),
         reads=("rowB", "rowC"), writes=("rowB",))
    p.op("act", lambda e: e.activation(rowC[0:1, :TC], rowB[0:1, :TC], AF.Ln,
                                       bias=c.sb["eps_col"][0:1, 0:1], scale=1.0),
         reads=("rowB", "c_eps_col"), writes=("rowC",))
    p.op("act", lambda e: e.activation(rowC[0:1, :TC], rowC[0:1, :TC], AF.Exp, scale=-0.5),
         reads=("rowC",), writes=("rowC",))
    p.op("dve", lambda e: e.scalar_tensor_tensor(rowA[0:1, :TC], rowA[0:1, :TC], -1.0,
                                                 rowC[0:1, :TC], OP.mult, OP.mult),
         reads=("rowA", "rowC"), writes=("rowA",))
    p.op("dve", lambda e: e.tensor_copy(rowD[0:1, :TC], rowA[0:1, :TC]),
         reads=("rowA",), writes=("rowD",))
    for sc in range(TC // 512):
        sl = slice(sc * 512, (sc + 1) * 512)
        bsl = sl
        p.op("ten", lambda e, sl=sl, bsl=bsl: e.matmul(
            psA[:, sc * 512:(sc + 1) * 512][:, 0:512] if False else psA[:, bsl],
            c.sb["ones_row_f32"][0:1, 0:128], rowC[0:1, sl],
            start=True, stop=True),
            reads=("rowC", "c_ones_row_f32"), writes=(pA,))
        p.op("dve", lambda e, sl=sl, bsl=bsl: e.tensor_copy(rbc[:, sl], psA[:, bsl]),
             reads=(pA,), writes=("rbc",))
    for kk in range(4):
        xr = xrot[kk % 2]
        xk = f"xrot{kk % 2}"
        p.dma("sp", xr.rearrange("p (k t) -> p k t", k=2),
              x_in_r[:, 2 * kk:2 * kk + 2, tsl], writes=(xk,), tag="lnx2")
        for j in range(2):
            k = 2 * kk + j
            p.op("dve", lambda e, xr=xr, j=j, k=k: e.tensor_mul(
                ln[k][:, :TC], xr[:, j * TC:(j + 1) * TC], rbc[:, :TC]),
                reads=(xk, "rbc"), writes=(f"ln{k}",))
    psB_cm.__exit__(None, None, None)
    psA_cm.__exit__(None, None, None)


def emit_l1_ffn(c, x_in, x_out):
    nc, p = c.nc, c.p
    TC = T
    sb_cms = []

    def alloc(name, shape, dt):
        cm = nc.sbuf_tensor(c.pfx + name, shape, dt)
        sb_cms.append(cm)
        return cm.__enter__()

    xrot = [alloc(f"xrot{i}", [128, 2 * TC], F32) for i in range(2)]
    xbf2 = alloc("xbf2", [128, 2 * TC], BF16)
    ln = [alloc(f"lnt{k}", [128, TC], BF16) for k in range(NKT)]
    rows = {"A": alloc("rowA", [1, TC], F32), "B": alloc("rowB", [1, TC], F32),
            "C": alloc("rowC", [1, TC], F32), "D": alloc("rowD", [1, TC], BF16)}
    rbc = alloc("rbc", [128, TC], F32)
    gA = alloc("gA", [128, 512], F32)
    gB = alloc("gB", [128, 512], F32)
    gT = alloc("gT", [128, 512], F32)
    Wf = [alloc(f"Wf{rt}", [128, 4 * 512], F32) for rt in range(2)]
    WB = [alloc(f"WB{i}", [128, 4 * 512], BF16) for i in range(3)]
    wsx = alloc("wsx", [1, 1024], BF16)
    bias_sb = alloc("bias_sb", [1, 1024], F32)
    p.dma("sp", bias_sb[0:1, :], c.d["bias_ri"], writes=("bias_sb",))
    xin_r = x_in.rearrange("(k p) t -> p k t", p=128)
    xout_r = x_out.rearrange("(k p) t -> p k t", p=128)

    ln_stream(c, p, xin_r, slice(0, T), TC, xrot, ln, rows, rbc, xbf2)
    ps_cms = []
    pso = []
    for i in range(2):
        cm = nc.psum_tensor(f"{c.pfx}pso{i}", [128, 512], F32)
        ps_cms.append(cm)
        pso.append(cm.__enter__())
    psB_cm2 = nc.psum_tensor(c.pfx + "psB2", [128, 512], F32)
    ps_cms.append(psB_cm2)
    psB = psB_cm2.__enter__()
    sb_cms.extend(ps_cms)

    for rt in range(2):
        g1 = c.d["G1R_x"] if rt == 0 else c.d["G1I_x"]
        g2 = c.d["G2R_x"] if rt == 0 else c.d["G2I_x"]
        for I in range(4):
            acc = Wf[rt][:, I * 512:(I + 1) * 512]
            ak = f"Wf{rt}_{I}"
            for ci in range(4):
                p.dma("sp", gA[:, :], g1[ci, I * 128:(I + 1) * 128, :], writes=("gA",))
                p.dma("sp", gB[:, :], g2[ci, I * 128:(I + 1) * 128, :], writes=("gB",))
                if ci == 0:
                    p.op("dve", lambda e, acc=acc: e.tensor_mul(acc, gA[:, :], gB[:, :]),
                         reads=("gA", "gB"), writes=(ak,))
                else:
                    p.op("dve", lambda e: e.tensor_mul(gT[:, :], gA[:, :], gB[:, :]),
                         reads=("gA", "gB"), writes=("gT",))
                    p.op("dve", lambda e, acc=acc: e.tensor_add(acc, acc, gT[:, :]),
                         reads=("gT", ak), writes=(ak,))
            p.op("dve", lambda e, acc=acc, rt=rt, I=I: e.tensor_copy(
                WB[rt][:, I * 512:(I + 1) * 512], acc),
                reads=(ak,), writes=(f"WB{rt}_{I}",))
            if rt == 1:
                p.op("dve", lambda e, acc=acc, I=I: e.tensor_scalar_mul(
                    WB[2][:, I * 512:(I + 1) * 512], acc, -1.0),
                    reads=(ak,), writes=(f"WB2_{I}",))

    for half in range(2):
        Wa = WB[0] if half == 0 else WB[1]
        Wak = "WB0" if half == 0 else "WB1"
        Wb = WB[2] if half == 0 else WB[0]
        Wbk = "WB2" if half == 0 else "WB0"
        for O in range(4):
            for I in range(4):
                p.op("ten", lambda e, Wa=Wa, I=I, O=O: e.matmul(
                    psB[0:1, 0:128], c.sb["ones_col_bf"][:, :],
                    Wa[:, I * 512 + O * 128:I * 512 + (O + 1) * 128],
                    start=(I == 0), stop=False),
                    reads=(f"{Wak}_{I}", "c_ones_col_bf"), writes=("psB",))
            for I in range(4):
                p.op("ten", lambda e, Wb=Wb, I=I, O=O: e.matmul(
                    psB[0:1, 0:128], c.sb["ones_col_bf"][:, :],
                    Wb[:, I * 512 + O * 128:I * 512 + (O + 1) * 128],
                    start=False, stop=(I == 3)),
                    reads=(f"{Wbk}_{I}", "c_ones_col_bf"), writes=("psB",))
            p.op("dve", lambda e, half=half, O=O: e.tensor_copy(
                wsx[0:1, half * 512 + O * 128:half * 512 + (O + 1) * 128], psB[0:1, 0:128]),
                reads=("psB",), writes=("wsx",))

    for odp in range(4):
        xr = xrot[odp % 2]
        xk = f"xrot{odp % 2}"
        p.dma("sp", xr.rearrange("p (k t) -> p k t", k=2),
              xin_r[:, 2 * odp:2 * odp + 2, 0:T], writes=(xk,), tag="xres")
        for j in range(2):
            od = 2 * odp + j
            half = od // 4
            O = od % 4
            Wa = WB[0] if half == 0 else WB[1]
            Wak = "WB0" if half == 0 else "WB1"
            Wb = WB[2] if half == 0 else WB[0]
            Wbk = "WB2" if half == 0 else "WB0"
            for tch in range(4):
                sl = slice(tch * 512, (tch + 1) * 512)
                pst = pso[tch % 2]
                kt = f"pso{tch % 2}"
                for I in range(4):
                    p.op("ten", lambda e, Wa=Wa, I=I, O=O, sl=sl, pst=pst: e.matmul(
                        pst[:, 0:512], Wa[:, I * 512 + O * 128:I * 512 + (O + 1) * 128],
                        ln[I][:, sl], start=(I == 0), stop=False),
                        reads=(f"{Wak}_{I}", f"ln{I}"), writes=(kt,))
                for I in range(4):
                    p.op("ten", lambda e, Wb=Wb, I=I, O=O, sl=sl, pst=pst: e.matmul(
                        pst[:, 0:512], Wb[:, I * 512 + O * 128:I * 512 + (O + 1) * 128],
                        ln[4 + I][:, sl], start=False, stop=False),
                        reads=(f"{Wbk}_{I}", f"ln{4 + I}"), writes=(kt,))
                p.op("ten", lambda e, od=od, sl=sl, pst=pst: e.matmul(
                    pst[:, 0:512], wsx[0:1, od * 128:(od + 1) * 128], rows["D"][0:1, sl],
                    start=False, stop=False),
                    reads=("wsx", "rowD"), writes=(kt,))
                p.op("ten", lambda e, od=od, pst=pst: e.matmul(
                    pst[:, 0:512], bias_sb[0:1, od * 128:(od + 1) * 128],
                    c.sb["ones_row_f32"][0:1, 0:512], start=False, stop=True),
                    reads=("bias_sb", "c_ones_row_f32"), writes=(kt,))
                p.op("dve", lambda e, j=j, sl=sl, xr=xr, pst=pst: e.scalar_tensor_tensor(
                    xr[:, j * TC + sl.start:j * TC + sl.stop], pst[:, 0:512], 0.5,
                    xr[:, j * TC + sl.start:j * TC + sl.stop], OP.mult, OP.add),
                    reads=(kt, xk), writes=(xk,))
        p.dma("sp", xout_r[:, 2 * odp:2 * odp + 2, 0:T],
              xr.rearrange("p (k t) -> p k t", k=2), reads=(xk,), tag="xstore")
    for cm in reversed(sb_cms):
        cm.__exit__(None, None, None)


def emit_l1_attn(c, x_in, x_out):
    nc, p = c.nc, c.p
    sb_cms = []

    def alloc(name, shape, dt):
        cm = nc.sbuf_tensor(c.pfx + name, shape, dt)
        sb_cms.append(cm)
        return cm.__enter__()

    abf = alloc("abf", [128, T], BF16)
    ktl = alloc("ktl", [128, T], BF16)
    xrot = [alloc(f"xrot{i}", [128, 2 * T], F32) for i in range(2)]
    xbf2 = alloc("xbf2", [128, 2 * T], BF16)
    ln = [alloc(f"lnt{k}", [128, T], BF16) for k in range(NKT)]
    rows = {"A": alloc("rowA", [1, T], F32), "B": alloc("rowB", [1, T], F32),
            "C": alloc("rowC", [1, T], F32), "D": alloc("rowD", [1, T], BF16)}
    rbc = alloc("rbc", [128, T], F32)
    hpre = [alloc(f"hpre{i}", [128, T], BF16) for i in range(8)]
    ksb = alloc("ksb", [128, T], F32)
    usb = alloc("usb", [128, T], F32)
    qsb = alloc("qsb", [128, T], F32)
    Lsb = alloc("Lsb", [128, T + 16], F32)
    ktl2 = ksb[:, :].bitcast(BF16)[:, 0:T]
    a2bf = ksb[:, :].bitcast(BF16)[:, T:2 * T]
    duqb = Lsb[:, :].bitcast(BF16)[:, 0:T]
    bbf = alloc("bbf", [128, T], BF16)
    vfm = alloc("vfm", [128, T], BF16)
    dec = alloc("dec", [128, 128], F32)
    vBDp = alloc("vBDp", [128, 128], BF16)
    bBDp = alloc("bBDp", [128, 128], BF16)
    smp = alloc("smp", [128, 16], BF16)
    KVbd = alloc("KVbd", [128, 128], F32)
    KVbds = alloc("KVbds", [128, 128], BF16)
    wqcol = alloc("wqcol", [128, 2 * 1024], BF16)
    wsq = alloc("wsq", [1, 256], BF16)
    wocol = alloc("wocol", [128, 1024], BF16)

    def palloc(name, shape, dt=F32):
        cm = nc.psum_tensor(c.pfx + name, shape, dt)
        sb_cms.append(cm)
        return cm.__enter__()

    xin_r = x_in.rearrange("(k p) t -> p k t", p=128)
    xout_r = x_out.rearrange("(k p) t -> p k t", p=128)

    fw1 = rbc
    af32 = xbf2[:, :].bitcast(F32)
    brow = rows["B"][0:1, 0:D]
    borow = rows["B"][0:1, D:2 * D]

    for i in range(8):
        p.op("dve", lambda e, i=i: e.memset(hpre[i][:, :], 0.0), writes=(f"hpre{i}",))
    p.op("dve", lambda e: e.memset(vfm[:, :], 0.0), writes=("vfm",))
    p.op("dve", lambda e: e.memset(abf[:, :], 0.0), writes=("abf",))
    p.op("dve", lambda e: e.memset(ktl[:, :], 0.0), writes=("ktl",))
    p.op("dve", lambda e: e.memset(bbf[:, :], 0.0), writes=("bbf",))

    ln_stream(c, p, xin_r, slice(0, T), T, xrot, ln, rows, rbc, xbf2)
    p.dma("sp", brow, c.d["key_b"], reads=("rowB",), writes=("brow",))
    p.dma("sp", borow, c.d["out_b"], reads=("rowB",), writes=("borow",))

    pA = palloc("pA", [128, 1024])
    pB = palloc("pB", [128, 1024])
    pC = palloc("pC", [128, 512])
    pD = palloc("pD", [128, 512])
    pE = palloc("pE", [128, 512])

    for hp in range(8):
        _phA = [] if L1_COPS in (2.95,) else list(enumerate(
                (("key_w_p", hp, pA, "pA", 0),
                 ("vq_w_p", 8 + hp, pB, "pB", None),
                 ("vq_w_p", hp, pC, "pC", None))))
        for w_i, (wtens, wname, pst, kt, boff) in _phA:
            wdst = wqcol[:, (w_i % 2) * 1024:(w_i % 2 + 1) * 1024]
            wk = f"wqcol{w_i % 2}"
            p.dma("pool", wdst.rearrange("p (k o) -> p k o", k=8),
                  c.d[wtens][wname], writes=(wk,), tag="wl1")
            wb = w_i % 2
            for k in range(NKT):
                p.op("ten", lambda e, k=k, wdst=wdst: e.matmul(
                    pD[0:1, 0:128], c.sb["ones_col_bf"][:, :], wdst[:, k * 128:(k + 1) * 128],
                    start=(k == 0), stop=(k == NKT - 1)),
                    reads=(wk, "c_ones_col_bf"), writes=("pD",))
            p.op("dve", lambda e, wb=wb: e.tensor_copy(wsq[0:1, wb * 128:(wb + 1) * 128],
                                                       pD[0:1, 0:128]),
                 reads=("pD",), writes=(f"wsq{wb}",))
            for sc in range(4):
                sl = slice(sc * 512, (sc + 1) * 512)
                psl = slice((sc % 2) * 512, (sc % 2 + 1) * 512) if w_i < 2 else slice(0, 512)
                for k in range(NKT):
                    p.op("ten", lambda e, k=k, sl=sl, psl=psl, wdst=wdst, pst=pst: e.matmul(
                        pst[:, psl], wdst[:, k * 128:(k + 1) * 128], ln[k][:, sl],
                        start=(k == 0), stop=False),
                        reads=(wk, f"ln{k}"), writes=(kt,))
                p.op("ten", lambda e, sl=sl, psl=psl, pst=pst, wb=wb, boff=boff: e.matmul(
                    pst[:, psl], wsq[0:1, wb * 128:(wb + 1) * 128], rows["D"][0:1, sl],
                    start=False, stop=(boff is None)),
                    reads=(f"wsq{wb}", "rowD"), writes=(kt,))
                if boff is not None:
                    p.op("ten", lambda e, sl=sl, psl=psl, pst=pst, boff=boff, hp=hp: e.matmul(
                        pst[:, psl], brow[0:1, hp * 128:(hp + 1) * 128],
                        c.sb["ones_row_f32"][0:1, 0:512], start=False, stop=True),
                        reads=("brow", "c_ones_row_f32"), writes=(kt,))
                if w_i == 0 and sc % 2 == 1:
                    ch2 = sc // 2
                    p.op("act", lambda e, ch2=ch2, pst=pst: e.activation(
                        ksb[:, ch2 * 1024:(ch2 + 1) * 1024], pst[:, :], AF.Sigmoid),
                        reads=(kt,), writes=("ksb",))
                elif w_i == 1 and sc % 2 == 1:
                    ch2 = sc // 2
                    p.op("act", lambda e, ch2=ch2, pst=pst: e.activation(
                        qsb[:, ch2 * 1024:(ch2 + 1) * 1024], pst[:, :], AF.Identity),
                        reads=(kt,), writes=("qsb",))
                elif w_i == 2:
                    p.op("act", lambda e, sc=sc, pst=pst: e.activation(
                        vfm[:, sc * 512:(sc + 1) * 512], pst[:, :], AF.Identity),
                        reads=(kt,), writes=("vfm",))

        if L1_PHASES < 2:
            continue
        if L1_COPS in (2.9, 2.95):
            p.op("dve", lambda e: e.tensor_copy(ktl[:, :], ln[0][:, :]),
                 reads=("ln0",), writes=("ktl",))
            p.op("dve", lambda e: e.tensor_copy(abf[:, :], ln[1][:, :]),
                 reads=("ln1",), writes=("abf",))
            p.op("dve", lambda e: e.tensor_copy(bbf[:, :], ln[2][:, :]),
                 reads=("ln2",), writes=("bbf",))
            p.op("dve", lambda e: e.memset(dec[:, :], 1.0), writes=("dec",))
            p.op("dve", lambda e: e.memset(KVf[:, :], 0.0), writes=("KVf",))
            p.op("dve", lambda e: e.memset(KVsb[:, :], 0.0), writes=("KVsb",))
            p.op("dve", lambda e: e.memset(vtok16[:, :], 0.0), writes=("vtok16",))
            p.op("dve", lambda e: e.memset(btok16[:, :], 0.0), writes=("btok16",))
            p.op("dve", lambda e: e.memset(smbf[:, :], 0.0), writes=("smbf",))
            p.op("dve", lambda e: e.memset(qsb[:, :], 0.0), writes=("qsb",))
        else:
            if True:
                pass
        if L1_COPS not in (2.9, 2.95):
            p.op("dve", lambda e: e.tensor_scalar(usb[:, :], ksb[:, :], -1.0, 1.0,
                                                  OP.mult, OP.add),
                 reads=("ksb",), writes=("usb",))
            p.op("dve", lambda e: e.tensor_mul(abf[:, :], usb[:, :], qsb[:, :]),
                 reads=("usb", "qsb"), writes=("abf",))
            _duq_tiles = ((pB, "pB", 0), (pB, "pB", 512), (pC, "pC", 0), (pD, "pD", 0))
            for i, (bank, bkey, off) in enumerate(_duq_tiles):
                p.op("ten", lambda e, bank=bank, off=off, i=i: e.matmul(
                    bank[:, off:off + 512], c.sb["blockM"][:, :],
                    abf[:, i * 512:(i + 1) * 512], start=True, stop=True),
                    reads=("c_blockM", "abf"), writes=(bkey,))
            p.op("dve", lambda e: e.tensor_scalar_max(ksb[:, :], ksb[:, :], 1e-38),
                 reads=("ksb",), writes=("ksb",))
            p.op("act", lambda e: e.activation(ksb[:, :], ksb[:, :], AF.Ln),
                 reads=("ksb",), writes=("ksb",))
            p.op("dve", lambda e: e.memset(fw1[:, :], 0.0), writes=("rbc",))
            p.op("dve", lambda e: e.memset(Lsb[:, 0:16], 0.0), writes=("Lsb",))
            p.op("dve", lambda e: e.tensor_tensor_scan(Lsb[:, 1:T + 1], ksb[:, :], fw1[:, :],
                                                       0.0, OP.add, OP.add),
                 reads=("ksb", "rbc"), writes=("Lsb",))
            pivA = Lsb[:, 0:T:16].unsqueeze(2).broadcast_to([128, T // 16, 16])
            pivB = Lsb[:, 16:T + 16:16].unsqueeze(2).broadcast_to([128, T // 16, 16])
            Lv = Lsb[:, 1:T + 1].rearrange("p (a b) -> p a b", b=16)
            p.op("dve", lambda e: e.tensor_tensor(fw1[:, :].rearrange("p (a b) -> p a b", b=16),
                                                  Lv, pivA, OP.subtract),
                 reads=("Lsb",), writes=("rbc",))
            p.op("dve", lambda e: e.tensor_scalar(af32[:, :], fw1[:, :], 43.0, 0.0,
                                                  OP.add, OP.min),
                 reads=("rbc",), writes=("afx", "xbf0", "xbf1"))
            p.op("act", lambda e: e.activation(af32[:, :], af32[:, :], AF.Exp),
                 reads=("afx",), writes=("afx",))
            p.op("dve", lambda e: e.tensor_mul(a2bf[:, :], qsb[:, :], af32[:, :]),
                 reads=("qsb", "afx"), writes=("ksb",))
            p.op("act", lambda e: e.activation(af32[:, :], fw1[:, :], AF.Exp),
                 reads=("rbc",), writes=("afx",))
            p.op("dve", lambda e: e.tensor_mul(qsb[:, :], qsb[:, :], af32[:, :]),
                 reads=("qsb", "afx"), writes=("qsb",))
            p.op("dve", lambda e: e.tensor_copy(abf[:, :], qsb[:, :]),
                 reads=("qsb",), writes=("abf",))
            p.op("dve", lambda e: e.tensor_scalar(af32[:, :], fw1[:, :], -1.0, -43.0,
                                                  OP.mult, OP.add),
                 reads=("rbc",), writes=("afx",))
            p.op("dve", lambda e: e.tensor_scalar_min(fw1[:, :], af32[:, :], 43.0),
                 reads=("afx",), writes=("rbc",))
            p.op("act", lambda e: e.activation(fw1[:, :], fw1[:, :], AF.Exp),
                 reads=("rbc",), writes=("rbc",))
            p.op("dve", lambda e: e.tensor_scalar_min(af32[:, :], af32[:, :], 0.0),
                 reads=("afx",), writes=("afx",))
            p.op("act", lambda e: e.activation(af32[:, :], af32[:, :], AF.Exp),
                 reads=("afx",), writes=("afx",))
            p.op("dve", lambda e: e.scalar_tensor_tensor(
                ktl[:, :], af32[:, :], 4.727392e18, usb[:, :], OP.mult, OP.mult),
                reads=("afx", "usb"), writes=("ktl",))
            p.op("dve", lambda e: e.tensor_sub(fw1[:, :], fw1[:, :], af32[:, :]),
                 reads=("rbc", "afx"), writes=("rbc",))
            p.op("dve", lambda e: e.tensor_mul(ktl2[:, :], usb[:, :], fw1[:, :]),
                 reads=("usb", "rbc"), writes=("ksb",))
            p.op("dve", lambda e: e.tensor_tensor(fw1[:, :].rearrange("p (a b) -> p a b", b=16),
                                                  pivB, Lv, OP.subtract),
                 reads=("Lsb",), writes=("rbc",))
            p.op("act", lambda e: e.activation(fw1[:, :], fw1[:, :], AF.Exp),
                 reads=("rbc",), writes=("rbc",))
            p.op("dve", lambda e: e.tensor_mul(bbf[:, :], usb[:, :], fw1[:, :]),
                 reads=("usb", "rbc"), writes=("bbf",))
            p.op("dve", lambda e: e.tensor_tensor(dec[:, :], Lsb[:, 16:T + 16:16],
                                                  Lsb[:, 0:T:16], OP.subtract),
                 reads=("Lsb",), writes=("dec",))
            p.op("act", lambda e: e.activation(dec[:, :], dec[:, :], AF.Exp),
                 reads=("dec",), writes=("dec",))
            for i, (bank, bkey, off) in enumerate(_duq_tiles):
                p.op("dve", lambda e, bank=bank, off=off, i=i: e.tensor_copy(
                    duqb[:, i * 512:(i + 1) * 512], bank[:, off:off + 512]),
                    reads=(bkey,), writes=("Lsb",))

            if DEBUG_TAPS and hp == 0:
                for nm, buf in (("ktl", ktl), ("abf", abf)):
                    dd = nc.dram_tensor("dbg_" + nm, [128, T], BF16, kind="ExternalOutput").ap()
                    p.dma("sp", dd, buf[:, :], reads=(nm,), tag="dbg")

        if L1_PHASES < 3 or hp >= L1_NHP:
            continue
        p.op("dve", lambda e: e.memset(vBDp[:, :], 0.0), writes=("vBDp",))
        p.op("dve", lambda e: e.memset(bBDp[:, :], 0.0), writes=("bBDp",))
        p.op("dve", lambda e: e.memset(smp[:, :], 0.0), writes=("smp",))
        p.op("dve", lambda e: e.memset(KVbd[:, :], 0.0), writes=("KVbd",))
        p.op("dve", lambda e: e.memset(KVbds[:, :], 0.0), writes=("KVbds",))
        B0 = xrot[0][:, :].bitcast(BF16)
        B1 = xrot[1][:, :].bitcast(BF16)
        p.op("dve", lambda e: e.memset(B0[:, :], 0.0), writes=("xrot0",))
        p.op("dve", lambda e: e.memset(B1[:, :], 0.0), writes=("xrot1",))
        for (srcb, srck, pr, dst, lo) in (
                (ktl, "ktl", slice(0, 64), B0[0:64, 0:4096], 0),
                (ktl, "ktl", slice(64, 128), B0[64:128, 0:4096], 16),
                (ktl2, "ksb", slice(0, 64), B0[0:64, 4096:8192], 0),
                (ktl2, "ksb", slice(64, 128), B0[64:128, 4096:8192], 16)):
            p.op("dve", lambda e, srcb=srcb, pr=pr, dst=dst, lo=lo: e.tensor_copy(
                dst.rearrange("p (c u) -> p c u", u=32)[:, :, lo:lo + 16],
                srcb[pr, :].rearrange("p (c s) -> p c s", s=16)),
                reads=(srck,), writes=("xrot0",))
        for (srcb, srck, pr, lo) in ((vfm, "vfm", slice(0, 64), 0),
                                     (vfm, "vfm", slice(64, 128), 16),
                                     (bbf, "bbf", slice(0, 64), 32),
                                     (bbf, "bbf", slice(64, 128), 48)):
            p.op("dve", lambda e, srcb=srcb, pr=pr, lo=lo: e.tensor_copy(
                B1[pr, :].rearrange("p (c u) -> p c u", u=64)[:, :, lo:lo + 16],
                srcb[pr, :].rearrange("p (c s) -> p c s", s=16)),
                reads=(srck,), writes=("xrot1",))
        for cc in range(min(T // 16, L1_NCHUNK)):
            sl = slice(cc * 16, (cc + 1) * 16)
            g8 = cc % 8
            p.op("ten", lambda e, cc=cc: e.matmul(pE[0:64, 0:128], B1[:, cc * 64:(cc + 1) * 64],
                                           c.sb["identity"][:, :], start=True, stop=True),
                 reads=("xrot1", "c_identity"), writes=("pE",))
            p.op("dve", lambda e: e.tensor_copy(vBDp[0:32, :], pE[0:32, 0:128]),
                 reads=("pE",), writes=("vBDp",))
            p.op("dve", lambda e: e.tensor_copy(bBDp[0:32, :], pE[32:64, 0:128]),
                 reads=("pE",), writes=("bBDp",))
            p.op("ten", lambda e, sl=sl, cc=cc: e.matmul(
                pD[0:32, 0:16], B0[:, cc * 32:(cc + 1) * 32],
                abf[:, sl], start=True, stop=False),
                reads=("xrot0", "abf"), writes=("pD",))
            p.op("ten", lambda e, sl=sl, cc=cc: e.matmul(
                pD[0:32, 0:16], B0[:, 4096 + cc * 32:4096 + (cc + 1) * 32],
                a2bf[:, sl], start=False, stop=True),
                reads=("xrot0", "ksb"), writes=("pD",))
            p.op("dve", lambda e: e.select(smp[0:32, 0:16], c.sb["selstrict32"][:, :],
                                           pD[0:32, 0:16], c.sb["zeros3216"][:, :]),
                 reads=("pD", "c_selstrict32", "c_zeros3216"), writes=("smp",))
            if cc > 0:
                p.op("ten", lambda e, sl=sl, g8=g8: e.matmul(
                    pA[:, g8 * 16:(g8 + 1) * 16], KVbds[:, :], abf[:, sl],
                    start=True, stop=False),
                    reads=("KVbds", "abf"), writes=("pA",))
            p.op("ten", lambda e, cc=cc, g8=g8: e.matmul(
                pA[:, g8 * 16:(g8 + 1) * 16], vBDp[:, :], smp[:, 0:16],
                start=(cc == 0), stop=True),
                reads=("vBDp", "smp"), writes=("pA",))
            p.op("ten", lambda e: e.matmul(pC[:, 0:128], bBDp[:, :], vBDp[:, :],
                                           start=True, stop=True),
                 reads=("bBDp", "vBDp"), writes=("pC",))
            p.op("dve", lambda e, cc=cc: e.scalar_tensor_tensor(
                KVbd[:, :], KVbd[:, :], dec[:, cc:cc + 1], pC[:, 0:128],
                OP.mult, OP.add),
                reads=("KVbd", "dec", "pC"), writes=("KVbd",))
            p.op("dve", lambda e: e.tensor_copy(KVbds[:, :], KVbd[:, :]),
                 reads=("KVbd",), writes=("KVbds",))
            if g8 == 7 or cc == min(T // 16, L1_NCHUNK) - 1:
                gi = cc // 8
                gsl = slice(gi * 128, gi * 128 + 128)
                p.op("dve", lambda e, gsl=gsl, hp=hp: e.tensor_mul(
                    hpre[hp][:, gsl], vfm[:, gsl], duqb[:, gsl]),
                    reads=("vfm", "Lsb"), writes=(f"hpre{hp}",))
                p.op("dve", lambda e, gsl=gsl, hp=hp: e.scalar_tensor_tensor(
                    hpre[hp][:, gsl], pA[:, 0:128], 1.0, hpre[hp][:, gsl],
                    OP.mult, OP.add),
                    reads=("pA", f"hpre{hp}"), writes=(f"hpre{hp}",))

    for odp in range(4):
        xr = xrot[odp % 2]
        xk = f"xrot{odp % 2}"
        p.dma("sp", xr.rearrange("p (k t) -> p k t", k=2),
              xin_r[:, 2 * odp:2 * odp + 2, 0:T], writes=(xk,), tag="xres")
        for j in range(2):
            od = 2 * odp + j
            wdst = wocol[:, 0:1024]
            wk = "wocol"
            p.dma("pool", wdst.rearrange("p (k o) -> p k o", k=8),
                  c.d["out_w_p"][od], writes=(wk,), tag="wout")
            for tch in range(4):
                sl = slice(tch * 512, (tch + 1) * 512)
                pst = pA if tch % 2 == 0 else pB
                kt = "pA" if tch % 2 == 0 else "pB"
                for k in range(NKT):
                    p.op("ten", lambda e, k=k, sl=sl, wdst=wdst, pst=pst: e.matmul(
                        pst[:, 0:512], wdst[:, k * 128:(k + 1) * 128], hpre[k][:, sl],
                        start=(k == 0), stop=False),
                        reads=(wk, f"hpre{k}"), writes=(kt,))
                p.op("ten", lambda e, od=od, pst=pst: e.matmul(
                    pst[:, 0:512], rows["B"][0:1, D + od * 128:D + (od + 1) * 128],
                    c.sb["ones_row_f32"][0:1, 0:512], start=False, stop=True),
                    reads=("borow", "c_ones_row_f32"), writes=(kt,))
                p.op("dve", lambda e, j=j, sl=sl, xr=xr, pst=pst, kt=kt, xk=xk: e.scalar_tensor_tensor(
                    xr[:, j * T + sl.start:j * T + sl.stop], pst[:, 0:512], 1.0,
                    xr[:, j * T + sl.start:j * T + sl.stop], OP.mult, OP.add),
                    reads=(kt, xk), writes=(xk,))
        p.dma("sp", xout_r[:, 2 * odp:2 * odp + 2, 0:T],
              xr.rearrange("p (k t) -> p k t", k=2), reads=(xk,), tag="xstore")
    for cm in reversed(sb_cms):
        cm.__exit__(None, None, None)

EMITTERS = {0: emit_l0_ffn, 1: emit_l0_attn, 2: emit_l0_ffn,
            3: emit_l1_ffn, 4: emit_l1_attn, 5: emit_l1_ffn}

_NC_CACHE = {}


def kernel(**inputs):
    g = prep_inputs(inputs)
    x = np.asarray(inputs["L0_attn_x"])
    xT = [np.ascontiguousarray(x[b].T) for b in range(8)]

    for group in SUBLAYER_GROUPS:
        key = tuple(group)
        if key not in _NC_CACHE:
            _NC_CACHE[key] = build_program(group)
        nc = _NC_CACHE[key]
        per_core = []
        for b in range(8):
            m = dict(g)
            m["xT"] = xT[b]
            per_core.append(m)
        res = run_bass_kernel_spmd(nc, per_core, core_ids=list(range(8)))
        xT = [np.ascontiguousarray(res.results[b]["out"]) for b in range(8)]

    return np.stack([o.T for o in xT], axis=0).astype(np.float32)
